# version 08
# baseline (speedup 1.0000x reference)
import math
import numpy as np
from contextlib import ExitStack

import concourse.bass as bass
import concourse.mybir as mybir
from concourse.bass_utils import run_bass_kernel_spmd
import ml_dtypes

F32 = mybir.dt.float32
BF16 = mybir.dt.bfloat16
U8 = mybir.dt.uint8
U32 = mybir.dt.uint32
AF = mybir.ActivationFunctionType
ALU = mybir.AluOpType
BF = ml_dtypes.bfloat16

HID = 2048
T_ALL = 8192
CH = 64
TILE = 256
NTILE = T_ALL // TILE
NCORE = 8
TOK_PC = 1024
LNDKI = float(np.float32(math.log(128.0 ** -0.5)))
CLAMP = 80.0


class Sem:
    def __init__(self):
        self.h = None
        self.n = 0


class Emitter:
    ENGINES = ("sync", "scalar", "vector", "tensor", "gpsimd")

    def __init__(self):
        self.ops = {e: [] for e in self.ENGINES}

    def wait(self, eng, sem, val=None):
        v = sem.n if val is None else val
        if v > 0:
            self.ops[eng].append(("w", sem, v))

    def emit(self, eng, fn, inc=None):
        incs = [] if inc is None else ([inc] if isinstance(inc, tuple) else list(inc))
        for s, k in incs:
            s.n += k
        self.ops[eng].append(("e", fn, incs))

    def edge(self, eng, sem):
        for it in reversed(self.ops[eng]):
            if it[0] == "e":
                if not any(s is sem for s, _k in it[2]):
                    sem.n += 1
                    it[2].append((sem, 1))
                break
        else:
            raise RuntimeError("edge: no prior op on " + eng)
        self.ops[eng].append(("w", sem, sem.n))

    def run(self, eng, engobj):
        for it in self.ops[eng]:
            if it[0] == "w":
                engobj.wait_ge(it[1].h, it[2])
            else:
                ins = it[1](engobj)
                for s, k in it[2]:
                    ins = ins.then_inc(s.h, k)


def ap_(base_ap, off, dims):
    return bass.AP(base_ap.tensor, base_ap.offset + off, dims)


def _host_prep(inputs):
    x = np.ascontiguousarray(
        np.asarray(inputs["L0_attn_hidden_states"], np.float32).reshape(T_ALL, HID))
    x_fm = np.ascontiguousarray(x.T)
    Wq = np.asarray(inputs["L0_attn_W_qkvz"], np.float32)
    Wba = np.asarray(inputs["L0_attn_W_ba"], np.float32)
    convw = np.asarray(inputs["L0_attn_conv_w"], np.float32).reshape(8192, 4)
    dtb = np.asarray(inputs["L0_attn_dt_bias"], np.float32)
    Alog = np.asarray(inputs["L0_attn_A_log"], np.float32)
    gamma = np.asarray(inputs["L0_attn_norm_weight"], np.float32)
    WoutT = np.ascontiguousarray(np.asarray(inputs["L0_attn_W_out"], np.float32).T)
    gate_wT = np.ascontiguousarray(np.asarray(inputs["L0_ffn_gate_w"], np.int32).astype(np.uint8).T.astype(BF))
    up_wT = np.ascontiguousarray(np.asarray(inputs["L0_ffn_up_w"], np.int32).astype(np.uint8).T.astype(BF))
    down_wT = np.ascontiguousarray(np.asarray(inputs["L0_ffn_down_w"], np.int32).astype(np.uint8).T.astype(BF))

    idx = np.arange(CH)
    t_lo = np.ascontiguousarray(np.tile(-(idx[:, None] > idx[None, :]).astype(np.float32), (2, 1)))
    t_hi = np.ascontiguousarray(np.tile(-(idx[:, None] < idx[None, :]).astype(np.float32), (2, 1)))
    t_hii = np.ascontiguousarray(np.tile((idx[:, None] <= idx[None, :]).astype(np.float32), (2, 1)))
    eye2 = np.ascontiguousarray(np.tile(np.eye(CH, dtype=np.float32), (2, 1)).astype(BF))
    ident128 = np.ascontiguousarray(np.eye(128, dtype=np.float32).astype(BF))
    ident128f = np.ascontiguousarray(np.eye(128, dtype=np.float32))
    onescol = np.ascontiguousarray(np.ones((128, 1), dtype=np.float32).astype(BF))

    percore = []
    for c in range(NCORE):
        kh = [2 * c, 2 * c + 1]
        vh = [4 * c + i for i in range(4)]
        rows = []
        for h in kh:
            rows += list(range(h * 128, (h + 1) * 128))
        for h in kh:
            rows += list(range(2048 + h * 128, 2048 + (h + 1) * 128))
        for h in vh:
            rows += list(range(4096 + h * 128, 4096 + (h + 1) * 128))
        zrows = [8192 + h * 128 + i for h in vh for i in range(128)]
        barows = list(vh) + [32 + h for h in vh]
        sel = np.zeros((128, 8), np.float32)
        sel[:, c] = 1
        percore.append(dict(
            x_fm=x_fm,
            x2_fm=np.ascontiguousarray(x_fm[:, c * TOK_PC:(c + 1) * TOK_PC]),
            x_tm=np.ascontiguousarray(x[c * TOK_PC:(c + 1) * TOK_PC, :]),
            WqkvT=np.ascontiguousarray(Wq[rows, :].T),
            WzT=np.ascontiguousarray(Wq[zrows, :].T),
            WbaT=np.ascontiguousarray(Wba[barows, :].T),
            convw=np.ascontiguousarray(convw[rows, :]),
            dtb=np.ascontiguousarray(dtb[vh].reshape(4, 1)),
            alog=np.ascontiguousarray(Alog[vh].reshape(4, 1)),
            gamma=np.ascontiguousarray(gamma.reshape(128, 1)),
            WoutT=WoutT,
            gate_wT=gate_wT, up_wT=up_wT, down_wT=down_wT,
            gate_s=np.ascontiguousarray(np.asarray(inputs["L0_ffn_gate_scale"], np.float32).reshape(64, 128).T),
            up_s=np.ascontiguousarray(np.asarray(inputs["L0_ffn_up_scale"], np.float32).reshape(64, 128).T),
            down_s=np.ascontiguousarray(np.asarray(inputs["L0_ffn_down_scale"], np.float32).reshape(16, 128).T),
            t_lo=t_lo, t_hi=t_hi, t_hii=t_hii, eye2=eye2,
            ident128=ident128, ident128f=ident128f, onescol=onescol,
            sel=sel,
        ))
    return percore, x


def build_program(ntile=NTILE, phase2=True, debug=()):
    nc = bass.Bass("TRN2", target_bir_lowering=False,
                   detect_race_conditions=False)
    E = Emitter()

    din = lambda name, shape, dt: nc.dram_tensor(name, shape, dt, kind="ExternalInput").ap()
    x_fm_d = din("x_fm", [HID, T_ALL], F32)
    x2_fm_d = din("x2_fm", [HID, TOK_PC], F32)
    x_tm_d = din("x_tm", [TOK_PC, HID], F32)
    WqkvT_d = din("WqkvT", [HID, 1024], F32)
    WzT_d = din("WzT", [HID, 512], F32)
    WbaT_d = din("WbaT", [HID, 8], F32)
    convw_d = din("convw", [1024, 4], F32)
    dtb_d = din("dtb", [4, 1], F32)
    alog_d = din("alog", [4, 1], F32)
    gamma_d = din("gamma", [128, 1], F32)
    WoutT_d = din("WoutT", [4096, 2048], F32)
    gate_wT_d = din("gate_wT", [HID, 8192], BF16)
    up_wT_d = din("up_wT", [HID, 8192], BF16)
    down_wT_d = din("down_wT", [8192, HID], BF16)
    gate_s_d = din("gate_s", [128, 64], F32)
    up_s_d = din("up_s", [128, 64], F32)
    down_s_d = din("down_s", [128, 16], F32)
    tlo_d = din("t_lo", [128, 64], F32)
    thi_d = din("t_hi", [128, 64], F32)
    thii_d = din("t_hii", [128, 64], F32)
    eye2_d = din("eye2", [128, 64], BF16)
    id128_d = din("ident128", [128, 128], BF16)
    id128f_d = din("ident128f", [128, 128], F32)
    ones_d = din("onescol", [128, 1], BF16)
    sel_d = din("sel", [128, 8], F32)

    out_d = nc.dram_tensor("out", [TOK_PC, HID], F32, kind="ExternalOutput").ap()

    rows_h = nc.dram_tensor("rows_h", [20, T_ALL], F32, kind="Internal").ap()
    h_src = nc.dram_tensor("h_src", [T_ALL, 512], BF16, kind="Internal").ap()
    h_all = nc.dram_tensor("h_all", [NCORE, T_ALL, 512], BF16, kind="Internal",
                           addr_space="Shared").ap()
    h_mine = nc.dram_tensor("h_mine", [TOK_PC, 4096], BF16, kind="Internal").ap()
    rows2_h = nc.dram_tensor("rows2_h", [4, TOK_PC], F32, kind="Internal").ap()
    wo16_d = nc.dram_tensor("wo16", [4096, HID], BF16, kind="Internal").ap()
    h_fm_d = nc.dram_tensor("h_fm", [4096, TOK_PC], BF16, kind="Internal").ap()

    dbg = {}
    def DBG(name, shape, dt=F32):
        if name in debug:
            dbg[name] = nc.dram_tensor("dbg_" + name, shape, dt, kind="ExternalOutput").ap()
        return dbg.get(name)

    DBG("h_src", [T_ALL, 512], BF16)
    DBG("qkv0", [128, 8 * TILE], BF16)
    DBG("xn0", [128, 16 * TILE], BF16)
    DBG("z0", [128, 1024], F32)
    DBG("rows", [20, T_ALL], F32)
    DBG("cp0", [128, 64], F32)
    DBG("stf", [128, 512], F32)
    DBG("o20", [128, 512], F32)
    DBG("vn0", [128, 512], F32)
    DBG("chn0", [128, 256], BF16)
    DBG("kcd0", [128, 256], BF16)
    DBG("pint", [128, 512], BF16)
    DBG("patt", [128, 512], BF16)
    DBG("o2pre", [128, 512], F32)
    DBG("rmsv", [128, 20], F32)
    DBG("h_mine", [TOK_PC, 4096], BF16)
    DBG("xn2", [2 * 128, 8192], BF16)
    DBG("ii", [2 * 128, 64 * 512], BF16)
    DBG("gfin", [2 * 128, 8192], BF16)
    DBG("asm", [2 * 128, 8192], F32)

    DX = Sem(); DR = Sem(); DH = Sem(); ST = Sem(); SV = Sem(); SS = Sem()
    SG = Sem(); CC = Sem()
    SGX = [Sem(), Sem()]; SGW = Sem(); DHS = [Sem(), Sem()]
    DRM = [Sem(), Sem()]; DRL = [Sem(), Sem()]; DRR = [Sem(), Sem()]
    ALL_SEMS = dict(dx=DX, dr=DR, dh=DH, t=ST, v=SV, s=SS, g=SG, cc=CC,
                    sgx0=SGX[0], sgx1=SGX[1], sgw=SGW, dhs0=DHS[0], dhs1=DHS[1],
                    drm0=DRM[0], drm1=DRM[1], drl0=DRL[0], drl1=DRL[1],
                    drr0=DRR[0], drr1=DRR[1])

    ctx = ExitStack()
    with ctx:
        for name, s in ALL_SEMS.items():
            s.h = ctx.enter_context(nc.semaphore("sem_" + name))
        SBt = lambda n, sh, dt: ctx.enter_context(nc.sbuf_tensor(n, sh, dt))
        _psn = [0]
        def PSt(sh, dt=F32):
            _psn[0] += 1
            return ctx.enter_context(nc.psum_tensor(f"ps{_psn[0]}", sh, dt))

        dtb_sb = SBt("dtb_sb", [4, 1], F32)
        nega_sb = SBt("nega_sb", [4, 1], F32)
        gamma_sb = SBt("gamma_sb", [128, 1], F32)
        tlo_sb = SBt("tlo_sb", [128, 64], F32)
        thi_sb = SBt("thi_sb", [128, 64], F32)
        thii_sb = SBt("thii_sb", [128, 64], F32)
        eye2_sb = SBt("eye2_sb", [128, 64], BF16)
        id128_sb = SBt("id128_sb", [128, 128], BF16)
        id128f_sb = SBt("id128f_sb", [128, 128], F32)
        ones_sb = SBt("ones_sb", [128, 1], BF16)
        gs_sb = SBt("gs_sb", [128, 64], F32)
        us_sb = SBt("us_sb", [128, 64], F32)
        ds_sb = SBt("ds_sb", [128, 16], F32)
        sel_sb = SBt("sel_sb", [128, 8], F32)

        PQ = [PSt([128, 512]), PSt([128, 512])]
        PD = [PSt([128, 512]), PSt([128, 512])]
        PV = PSt([128, 512])
        PO = PSt([128, 512])
        PR = PSt([128, 512])
        PT = PSt([128, 512], BF16)

        for src, dst in [
            (dtb_d, dtb_sb), (alog_d, nega_sb), (gamma_d, gamma_sb),
            (tlo_d, tlo_sb), (thi_d, thi_sb), (thii_d, thii_sb),
            (eye2_d, eye2_sb), (id128_d, id128_sb), (id128f_d, id128f_sb),
            (ones_d, ones_sb), (gate_s_d, gs_sb), (up_s_d, us_sb),
            (down_s_d, ds_sb), (sel_d, sel_sb),
        ]:
            E.emit("sync", lambda s, a=src, b=dst: s.dma_start(out=b[:], in_=a), inc=(DX, 16))
        const_tok = DX.n

        p1 = ExitStack()
        with p1:
            SB1 = lambda n, sh, dt: p1.enter_context(nc.sbuf_tensor(n, sh, dt))
            wq_sb = SB1("wq_sb", [128, 16 * 1024], BF16)
            wz_sb = SB1("wz_sb", [128, 16 * 512], BF16)
            wba_sb = SB1("wba_sb", [128, 16 * 8], BF16)
            convw_sb = SB1("convw_sb", [128, 8 * 4], F32)
            xb_sb = [SB1(f"xb{i}", [128, 16 * TILE], BF16) for i in range(2)]
            x2b_sb = SB1("x2b", [128, TILE], BF16)
            t32_sb = SB1("t32", [128, TILE], F32)
            tail_sb = [SB1(f"tail{i}", [128, 32], F32) for i in range(2)]
            rsq32_sb = SB1("rsq32", [128, TILE], BF16)
            q2b_sb = SB1("q2b", [128, TILE], BF16)
            xn_sb = [SB1(f"xn{i}", [128, 16 * TILE], BF16) for i in range(2)]
            mr_sb = [SB1(f"mr{i}", [1, 2 * TILE], F32) for i in range(2)]
            scr_sb = SB1("scr", [4, 2 * TILE], F32)
            bb_sb = [SB1(f"bb{i}", [4, TILE], F32) for i in range(2)]
            aa_sb = [SB1(f"aa{i}", [4, TILE], F32) for i in range(2)]
            lnbp_sb = SB1("lnbp", [4, TILE], F32)
            beta_sb = [SB1(f"beta{i}", [4, TILE], F32) for i in range(2)]
            gc_sb = [SB1(f"gc{i}", [4, TILE], F32) for i in range(2)]
            gl_sb = [SB1(f"gl{i}", [4, TILE], F32) for i in range(2)]
            l2p_sb = SB1("l2p", [1, 4 * TILE], F32)
            l2hv_sb = [SB1(f"l2hv{i}", [4, 2 * TILE], F32) for i in range(2)]
            rws_sb = [SB1(f"rws{i}", [4, 4 * TILE], F32) for i in range(2)]
            mrep_sb = [SB1(f"mrep{i}", [128, TILE], F32) for i in range(2)]
            rrep_sb = [SB1(f"rrep{i}", [128, TILE], F32) for i in range(2)]
            arep_sb = [SB1(f"arep{i}", [128, 4 * TILE], F32) for i in range(2)]
            carep_sb = [SB1(f"carep{i}", [128, 4 * TILE], F32) for i in range(2)]
            srep_sb = [SB1(f"srep{i}", [128, 4 * TILE], F32) for i in range(2)]
            egrep_sb = [SB1(f"egrep{i}", [128, 16], F32) for i in range(2)]
            mixed_sb = SB1("mixed", [128, 8 * (TILE + 4)], F32)
            conv_sb = SB1("conv", [128, 8 * TILE], F32)
            qkv_sb = [SB1(f"qkv{i}", [128, 8 * TILE], BF16) for i in range(2)]
            z_sb = [SB1(f"z{i}", [128, 2 * 512], BF16) for i in range(2)]
            batm_sb = [SB1(f"batm{i}", [128, 2 * 8], F32) for i in range(2)]
            ktm_sb = [SB1(f"ktm{i}", [128, 2 * 2 * 128], BF16) for i in range(2)]
            vtm_sb = [SB1(f"vtm{i}", [128, 2 * 4 * 128], BF16) for i in range(2)]
            cp_sb = [SB1(f"cp{i}", [128, 2 * 8 * 4], F32) for i in range(2)]
            st_sb = [SB1(f"states{i}", [128, 4 * 128], F32) for i in range(2)]
            stbf_sb = [SB1(f"statesbf{i}", [128, 4 * 128], BF16) for i in range(2)]
            hbuf_sb = [SB1(f"hbuf{i}", [128, 2 * 512], BF16) for i in range(2)]
            mka_sb = SB1("mka", [128, 4 * 64], F32)
            mkc_sb = SB1("mkc", [128, 4 * 64], F32)
            mks_sb = SB1("mks", [128, 4 * 64], F32)
            ra_sb = SB1("ra", [128, 4 * 64], BF16)
            ca_sb = SB1("ca", [128, 4 * 64], BF16)
            sc_sb = SB1("sc", [128, 4 * 64], BF16)
            pw_sb = SB1("pw", [128, 8 * 4 * 64], BF16)
            r32_sb = SB1("r32", [128, 4 * 64], BF16)
            chn_sb = SB1("chn", [128, 4 * 64], BF16)
            vb_sb = SB1("vb", [128, 4 * 128], BF16)
            ke_sb = SB1("ke", [128, 4 * 128], BF16)
            k2t_sb = SB1("k2t", [128, 4 * 128], BF16)
            kcd_sb = SB1("kcd", [128, 4 * 64], BF16)
            vn_sb = SB1("vn", [128, 4 * 128], BF16)
            o2_sb = SB1("o2", [128, 4 * 128], F32)
            rms_sb = SB1("rms", [128, 20], F32)
            if "pint" in dbg:
                pint_sb = SB1("pint_sb", [128, 512], BF16)
            if "patt" in dbg:
                patt_sb = SB1("patt_sb", [128, 512], BF16)

            E.emit("sync", lambda s: s.dma_start(
                out=convw_sb[:].rearrange("p (a k) -> p a k", k=4),
                in_=convw_d.rearrange("(a p) k -> p a k", p=128)),
                inc=(DX, 16))

            def load_x(j, p):
                E.emit("gpsimd", lambda g, j=j, p=p: g.dma_start(
                    out=xb_sb[p][:].rearrange("p (a t) -> p a t", t=TILE),
                    in_=x_fm_d.rearrange("(a p) t -> p a t", p=128)[:, :, j * TILE:(j + 1) * TILE]),
                    inc=(SGX[p], 16))

            load_x(0, 0)
            x_tok = [SGX[0].n, 0]
            load_x(NTILE // 2, 1)
            x_tok[1] = SGX[1].n

            E.emit("gpsimd", lambda g: g.dma_start(
                out=wq_sb[:].rearrange("p (a o) -> p a o", o=1024),
                in_=WqkvT_d.rearrange("(a p) o -> p a o", p=128)), inc=(SGW, 16))
            E.emit("gpsimd", lambda g: g.dma_start(
                out=wz_sb[:].rearrange("p (a o) -> p a o", o=512),
                in_=WzT_d.rearrange("(a p) o -> p a o", p=128)), inc=(SGW, 16))
            E.emit("gpsimd", lambda g: g.dma_start(
                out=wba_sb[:].rearrange("p (a o) -> p a o", o=8),
                in_=WbaT_d.rearrange("(a p) o -> p a o", p=128)), inc=(SGW, 16))
            w_tok = SGW.n
            const_tok = DX.n
            E.wait("scalar", DX, const_tok)
            E.emit("scalar", lambda s: s.activation(
                out=nega_sb[:], in_=nega_sb[:], func=AF.Exp), inc=(SS, 1))
            E.wait("vector", SS, SS.n)
            E.emit("vector", lambda v: v.tensor_scalar(
                out=nega_sb[:], in0=nega_sb[:], scalar1=-1.0, scalar2=None, op0=ALU.mult),
                inc=(SV, 1))
            E.wait("vector", DX, const_tok)
            E.emit("vector", lambda v: v.memset(st_sb[0][:], 0.0))
            E.emit("vector", lambda v: v.memset(stbf_sb[0][:], 0.0))
            E.emit("vector", lambda v: v.memset(st_sb[1][:], 0.0))
            E.emit("vector", lambda v: v.memset(stbf_sb[1][:], 0.0), inc=(SV, 1))
            st_ini_tok = SV.n

            tk = dict(xb=[0, 0], xn=[0, 0], mrw=[0, 0], qkv=[0, 0], z=[0, 0],
                      ba=[0, 0], rows=[0, 0], rep=[0, 0], cp=[0, 0], tm=[0, 0],
                      eg=[0, 0], h=[0, 0], hst=[0, 0], zmm=[0, 0], deltaq=[0, 0],
                      opb=[0, 0], maskv=[0, 0], cpT=[0, 0], baT=[0, 0],
                      mrr=[0, 0])
            bank = dict(PQ=[0, 0], PD=[None, None], PV=0, PO=None, PR=0, PT=None)

            def bwait(eng, tok):
                if tok is None:
                    return
                if isinstance(tok, list):
                    for t in tok:
                        bwait(eng, t)
                    return
                if tok[1] > 0:
                    E.wait(eng, tok[0], tok[1])
            state_tok = [st_ini_tok, st_ini_tok]
            statebf_tok = [0, 0]
            FE16 = 16 * TILE

            def v3(buf, dims, off=0):
                return ap_(buf[:], off, dims)

            def ln_stage(j, p):
                E.wait("vector", SGX[p], x_tok[p])
                E.wait("tensor", SGX[p], x_tok[p])
                E.wait("tensor", SV, bank["PR"])
                bwait("tensor", bank["PO"])
                for ft in range(16):
                    if ft > 0:
                        E.wait("vector", ST, ST.n)
                    E.emit("vector", lambda v, ft=ft: v.tensor_mul(
                        out=x2b_sb[:], in0=xb_sb[p][:, ft * TILE:(ft + 1) * TILE],
                        in1=xb_sb[p][:, ft * TILE:(ft + 1) * TILE]), inc=(SV, 1))
                    E.wait("tensor", SV, SV.n)
                    E.emit("tensor", lambda t, ft=ft: t.matmul(
                        PR[0:1, 0:TILE], ones_sb[:], xb_sb[p][:, ft * TILE:(ft + 1) * TILE],
                        start=(ft == 0), stop=(ft == 15)))
                    E.emit("tensor", lambda t, ft=ft: t.matmul(
                        PO[0:1, 0:TILE], ones_sb[:], x2b_sb[:],
                        start=(ft == 0), stop=(ft == 15)), inc=(ST, 1))
                mr = mr_sb[p]
                E.wait("vector", ST, ST.n)
                E.emit("vector", lambda v: v.tensor_scalar(
                    out=mr[:, 0:TILE], in0=PR[0:1, 0:TILE], scalar1=1.0 / HID,
                    scalar2=None, op0=ALU.mult))
                E.emit("vector", lambda v: v.tensor_scalar(
                    out=scr_sb[0:1, 0:TILE], in0=PO[0:1, 0:TILE],
                    scalar1=1.0 / HID, scalar2=None, op0=ALU.mult), inc=(SV, 1))
                bank["PR"] = SV.n
                bank["PO"] = (SV, SV.n)
                E.edge("vector", SV)
                E.emit("vector", lambda v: v.tensor_mul(
                    out=scr_sb[0:1, TILE:2 * TILE], in0=mr[:, 0:TILE], in1=mr[:, 0:TILE]))
                E.edge("vector", SV)
                E.emit("vector", lambda v: v.scalar_tensor_tensor(
                    out=mr[:, TILE:2 * TILE], in0=scr_sb[0:1, 0:TILE], scalar=1e-5,
                    in1=scr_sb[0:1, TILE:2 * TILE], op0=ALU.add, op1=ALU.subtract), inc=(SV, 1))
                E.wait("scalar", SV, SV.n)
                E.emit("scalar", lambda s: s.activation(
                    out=mr[:, TILE:2 * TILE], in_=mr[:, TILE:2 * TILE], func=AF.Sqrt),
                    inc=(SS, 1))
                E.wait("vector", SS, SS.n)
                E.emit("vector", lambda v: v.reciprocal(
                    out=mr[:, TILE:2 * TILE], in_=mr[:, TILE:2 * TILE]))
                E.edge("vector", SV)
                E.emit("vector", lambda v: v.tensor_mul(
                    out=scr_sb[0:1, 0:TILE], in0=mr[:, 0:TILE],
                    in1=mr[:, TILE:2 * TILE]), inc=(SV, 1))
                mr_tok = SV.n
                E.wait("sync", SV, mr_tok)
                E.emit("sync", lambda s: s.dma_start(
                    out=ap_(rows_h, j * TILE, [[T_ALL, 2], [1, TILE]]),
                    in_=ap_(mr[:], 0, [[2 * TILE, 1], [TILE, 2], [1, TILE]])), inc=(DR, 16))
                E.emit("sync", lambda s: s.dma_start(
                    out=ap_(rows_h, 19 * T_ALL + j * TILE, [[T_ALL, 1], [1, TILE]]),
                    in_=scr_sb[0:1, 0:TILE]), inc=(DR, 16))
                tk["mrr"][p] = DR.n
                E.wait("sync", DR, DR.n)
                E.wait("sync", SV, tk["xn"][p])
                E.emit("sync", lambda s: s.dma_start(
                    out=mrep_sb[p][:],
                    in_=ap_(rows_h, 19 * T_ALL + j * TILE, [[0, 128], [1, TILE]])), inc=(DRM[p], 16))
                E.emit("sync", lambda s: s.dma_start(
                    out=rrep_sb[p][:],
                    in_=ap_(rows_h, T_ALL + j * TILE, [[0, 128], [1, TILE]])), inc=(DRM[p], 16))
                tk["mrw"][p] = DRM[p].n

            def xn_stage(j, p):
                E.wait("vector", DRM[p], tk["mrw"][p])
                E.wait("vector", ST, tk["zmm"][p])
                for ft in range(16):
                    E.emit("vector", lambda v, ft=ft: v.tensor_mul(
                        out=t32_sb[:], in0=xb_sb[p][:, ft * TILE:(ft + 1) * TILE],
                        in1=rrep_sb[p][:]))
                    E.emit("vector", lambda v, ft=ft: v.tensor_sub(
                        out=xn_sb[p][:, ft * TILE:(ft + 1) * TILE],
                        in0=t32_sb[:], in1=mrep_sb[p][:]),
                        inc=(SV, 1) if ft == 15 else None)
                tk["xn"][p] = SV.n

            def fe_stage(j, p):
                E.wait("tensor", SV, tk["xn"][p])
                E.wait("tensor", SS, tk["z"][1 - p])
                E.wait("vector", SS, tk["qkv"][1 - p])
                if j == 0:
                    E.wait("tensor", SGW, w_tok)
                for mb in range(8):
                    pq = mb % 2
                    E.wait("tensor", SV, bank["PQ"][pq])
                    for ft in range(16):
                        wsl = wq_sb[:, ft * 1024 + mb * 128: ft * 1024 + (mb + 1) * 128]
                        E.emit("tensor", lambda t, w=wsl: t.ldweights(w))
                        E.emit("tensor", lambda t, w=wsl, ft=ft, mb=mb: t.matmul(
                            PQ[mb % 2][0:128, 0:TILE], w,
                            xn_sb[p][:, ft * TILE:(ft + 1) * TILE],
                            start=(ft == 0), stop=(ft == 15)),
                            inc=(ST, 1) if ft == 15 else None)
                    E.wait("vector", ST, ST.n)
                    if j == 0 or j == NTILE // 2:
                        E.emit("vector", lambda v, mb=mb: v.memset(
                            mixed_sb[:, mb * (TILE + 4): mb * (TILE + 4) + 4], 0.0))
                    else:
                        E.emit("vector", lambda v, mb=mb, p=p: v.tensor_copy(
                            out=mixed_sb[:, mb * (TILE + 4): mb * (TILE + 4) + 4],
                            in_=tail_sb[p][:, mb * 4:(mb + 1) * 4]))
                    E.emit("vector", lambda v, mb=mb: v.tensor_copy(
                        out=mixed_sb[:, mb * (TILE + 4) + 4: mb * (TILE + 4) + 4 + TILE],
                        in_=PQ[mb % 2][0:128, 0:TILE]))
                    E.emit("vector", lambda v, mb=mb, p=p: v.tensor_copy(
                        out=tail_sb[p][:, mb * 4:(mb + 1) * 4],
                        in_=PQ[mb % 2][0:128, TILE - 4:TILE]), inc=(SV, 1))
                    bank["PQ"][pq] = SV.n
                    for tap in range(4):
                        off = mb * (TILE + 4) + 1 + tap
                        if tap == 0:
                            E.emit("vector", lambda v, mb=mb, off=off: v.tensor_scalar(
                                out=conv_sb[:, mb * TILE:(mb + 1) * TILE],
                                in0=mixed_sb[:, off:off + TILE],
                                scalar1=convw_sb[:, mb * 4: mb * 4 + 1],
                                scalar2=None, op0=ALU.mult))
                        else:
                            E.emit("vector", lambda v, mb=mb, off=off, tap=tap: v.scalar_tensor_tensor(
                                out=conv_sb[:, mb * TILE:(mb + 1) * TILE],
                                in0=mixed_sb[:, off:off + TILE],
                                scalar=convw_sb[:, mb * 4 + tap: mb * 4 + tap + 1],
                                in1=conv_sb[:, mb * TILE:(mb + 1) * TILE],
                                op0=ALU.mult, op1=ALU.add),
                                inc=(SV, 1) if tap == 3 else None)
                conv_tok = SV.n
                E.wait("scalar", SV, conv_tok)
                E.emit("scalar", lambda s: s.activation(
                    out=qkv_sb[p][:], in_=conv_sb[:], func=AF.Silu), inc=(SS, 1))
                tk["qkv"][p] = SS.n
                E.wait("scalar", SV, tk["h"][p])
                E.wait("vector", ST, tk["baT"][p])
                for ts in range(2):
                    pq = ts
                    E.wait("tensor", SS, tk["z"][p])
                    E.wait("tensor", SV, bank["PQ"][pq])
                    bwait("tensor", bank["PD"][ts])
                    for ft in range(16):
                        xsl = xn_sb[p][:, ft * TILE + ts * 128: ft * TILE + (ts + 1) * 128]
                        E.emit("tensor", lambda t, x=xsl: t.ldweights(x))
                        E.emit("tensor", lambda t, x=xsl, ft=ft, ts=ts: t.matmul(
                            PQ[ts][0:128, 0:512], x, wz_sb[:, ft * 512:(ft + 1) * 512],
                            start=(ft == 0), stop=(ft == 15)))
                        E.emit("tensor", lambda t, x=xsl, ft=ft, ts=ts: t.matmul(
                            PD[ts][0:128, 0:8], x, wba_sb[:, ft * 8:(ft + 1) * 8],
                            start=(ft == 0), stop=(ft == 15)),
                            inc=(ST, 1) if ft == 15 else None)
                    E.wait("scalar", ST, ST.n)
                    E.emit("scalar", lambda s, ts=ts: s.activation(
                        out=z_sb[p][:, ts * 512:(ts + 1) * 512],
                        in_=PQ[ts][0:128, 0:512], func=AF.Silu), inc=(SS, 1))
                    bank["PQ"][pq] = 0
                    E.wait("vector", ST, ST.n)
                    E.emit("vector", lambda v, ts=ts: v.tensor_copy(
                        out=batm_sb[p][:, ts * 8:(ts + 1) * 8],
                        in_=PD[ts][0:128, 0:8]), inc=(SV, 1))
                    bank["PD"][ts] = (SV, SV.n)
                tk["z"][p] = SS.n
                tk["ba"][p] = SV.n
                tk["zmm"][p] = ST.n

            def rows_stage(j, p):
                E.wait("tensor", SV, tk["ba"][p])
                E.wait("tensor", SV, bank["PR"])
                for ts in range(2):
                    E.emit("tensor", lambda t, ts=ts: t.transpose(
                        PR[0:4, ts * 128:(ts + 1) * 128],
                        batm_sb[p][:, ts * 8: ts * 8 + 4], id128f_sb[:]))
                    E.emit("tensor", lambda t, ts=ts: t.transpose(
                        PR[0:4, 256 + ts * 128: 256 + (ts + 1) * 128],
                        batm_sb[p][:, ts * 8 + 4: ts * 8 + 8], id128f_sb[:]),
                        inc=(ST, 1) if ts == 1 else None)
                tk["baT"][p] = ST.n
                E.wait("vector", ST, ST.n)
                E.emit("vector", lambda v: v.tensor_copy(out=bb_sb[p][:], in_=PR[0:4, 0:256]))
                E.emit("vector", lambda v: v.tensor_copy(out=aa_sb[p][:], in_=PR[0:4, 256:512]),
                       inc=(SV, 1))
                bank["PR"] = SV.n
                E.wait("vector", ST, tk["cpT"][p])
                E.wait("scalar", SV, SV.n)
                E.wait("scalar", DR, tk["mrr"][p])
                E.emit("scalar", lambda s: s.activation(
                    out=scr_sb[0:4, 0:TILE], in_=bb_sb[p][:], func=AF.Tanh, scale=0.5),
                    inc=(SS, 1))
                E.wait("vector", SS, SS.n)
                E.emit("vector", lambda v: v.tensor_scalar(
                    out=beta_sb[p][:], in0=scr_sb[0:4, 0:TILE], scalar1=0.5, scalar2=0.5,
                    op0=ALU.mult, op1=ALU.add), inc=(SV, 1))
                E.emit("vector", lambda v: v.tensor_scalar(
                    out=scr_sb[0:4, 0:TILE], in0=bb_sb[p][:], scalar1=-1.0, scalar2=CLAMP,
                    op0=ALU.mult, op1=ALU.min), inc=(SV, 1))
                E.wait("scalar", SV, SV.n)
                E.emit("scalar", lambda s: s.activation(
                    out=scr_sb[0:4, 0:TILE], in_=scr_sb[0:4, 0:TILE], func=AF.Exp))
                E.emit("scalar", lambda s: s.activation(
                    out=lnbp_sb[:], in_=scr_sb[0:4, 0:TILE], func=AF.Ln, bias=1.0),
                    inc=(SS, 1))
                E.emit("vector", lambda v: v.tensor_scalar(
                    out=scr_sb[0:4, TILE:2 * TILE], in0=aa_sb[p][:], scalar1=dtb_sb[:, :],
                    scalar2=CLAMP, op0=ALU.add, op1=ALU.min), inc=(SV, 1))
                E.wait("scalar", SV, SV.n)
                E.emit("scalar", lambda s: s.activation(
                    out=scr_sb[0:4, TILE:2 * TILE], in_=scr_sb[0:4, TILE:2 * TILE],
                    func=AF.Exp))
                E.emit("scalar", lambda s: s.activation(
                    out=scr_sb[0:4, TILE:2 * TILE], in_=scr_sb[0:4, TILE:2 * TILE],
                    func=AF.Ln, bias=1.0), inc=(SS, 1))
                E.wait("vector", SS, SS.n)
                E.emit("vector", lambda v: v.tensor_scalar(
                    out=scr_sb[0:4, TILE:2 * TILE], in0=scr_sb[0:4, TILE:2 * TILE],
                    scalar1=nega_sb[:, :], scalar2=None, op0=ALU.mult), inc=(SV, 1))
                E.edge("vector", SV)
                for cc in range(4):
                    E.emit("vector", lambda v, cc=cc: v.tensor_tensor_scan(
                        out=gc_sb[p][:, cc * 64:(cc + 1) * 64],
                        data0=scr_sb[0:4, TILE + cc * 64: TILE + (cc + 1) * 64],
                        data1=scr_sb[0:4, TILE + cc * 64: TILE + (cc + 1) * 64],
                        initial=0.0, op0=ALU.add, op1=ALU.bypass))
                E.edge("vector", SV)
                E.emit("vector", lambda v: v.tensor_copy(
                    out=gl_sb[p][:],
                    in_=ap_(gc_sb[p][:], 63, [[TILE, 4], [64, 4], [0, 64]])), inc=(SV, 1))
                E.wait("vector", SS, tk["qkv"][p])
                E.wait("tensor", SS, tk["qkv"][p])
                for rnd in range(2):
                    E.wait("tensor", SV, bank["PR"])
                    for hh in range(2):
                        hqk = rnd * 2 + hh
                        E.wait("vector", ST, ST.n)
                        E.emit("vector", lambda v, hqk=hqk: v.tensor_mul(
                            out=q2b_sb[:], in0=qkv_sb[p][:, hqk * TILE:(hqk + 1) * TILE],
                            in1=qkv_sb[p][:, hqk * TILE:(hqk + 1) * TILE]), inc=(SV, 1))
                        E.wait("tensor", SV, SV.n)
                        E.emit("tensor", lambda t, hh=hh: t.matmul(
                            PR[0:1, hh * TILE:(hh + 1) * TILE],
                            ones_sb[:], q2b_sb[:],
                            start=True, stop=True), inc=(ST, 1))
                    E.wait("vector", ST, ST.n)
                    E.emit("vector", lambda v, rnd=rnd: v.tensor_copy(
                        out=l2p_sb[:, rnd * 2 * TILE:(rnd + 1) * 2 * TILE],
                        in_=PR[0:1, 0:2 * TILE]), inc=(SV, 1))
                    bank["PR"] = SV.n
                E.wait("scalar", SV, SV.n)
                E.emit("scalar", lambda s: s.activation(
                    out=l2p_sb[:], in_=l2p_sb[:], func=AF.Sqrt), inc=(SS, 1))
                E.wait("vector", SS, SS.n)
                E.emit("vector", lambda v: v.tensor_scalar(
                    out=l2p_sb[:], in0=l2p_sb[:], scalar1=1e-6, scalar2=None,
                    op0=ALU.add), inc=(SV, 1))
                E.wait("scalar", SV, SV.n)
                E.emit("scalar", lambda s: s.activation(
                    out=l2p_sb[:], in_=l2p_sb[:], func=AF.Ln), inc=(SS, 1))
                lnn_tok = SS.n
                E.wait("sync", SS, lnn_tok)
                E.wait("sync", SV, tk["rows"][p])
                for g in range(2):
                    for khx in range(2):
                        E.emit("sync", lambda s, g=g, khx=khx: s.dma_start(
                            out=l2hv_sb[p][2 * khx: 2 * khx + 2, g * TILE:(g + 1) * TILE],
                            in_=ap_(l2p_sb[:], g * 2 * TILE + khx * TILE,
                                    [[4 * TILE, 1], [0, 2], [1, TILE]])), inc=(DRL[p], 16))
                l2hv_tok = DRL[p].n
                E.wait("vector", DRL[p], l2hv_tok)
                rw = rws_sb[p]
                lnnq = l2hv_sb[p][:, 0:TILE]
                lnnk = l2hv_sb[p][:, TILE:2 * TILE]
                E.emit("vector", lambda v: v.scalar_tensor_tensor(
                    out=rw[:, 0:TILE], in0=gc_sb[p][:], scalar=-1.0, in1=lnnk,
                    op0=ALU.mult, op1=ALU.subtract))
                E.emit("vector", lambda v: v.tensor_sub(
                    out=scr_sb[0:4, 0:TILE], in0=gc_sb[p][:], in1=lnnk))
                E.emit("vector", lambda v: v.tensor_sub(
                    out=rw[:, TILE:2 * TILE], in0=scr_sb[0:4, 0:TILE], in1=lnbp_sb[:]))
                E.emit("vector", lambda v: v.tensor_sub(
                    out=scr_sb[0:4, 0:TILE], in0=gc_sb[p][:], in1=lnnq))
                E.emit("vector", lambda v: v.tensor_scalar(
                    out=rw[:, 2 * TILE:3 * TILE], in0=scr_sb[0:4, 0:TILE], scalar1=LNDKI,
                    scalar2=None, op0=ALU.add))
                E.emit("vector", lambda v: v.tensor_sub(
                    out=scr_sb[0:4, 0:TILE], in0=gl_sb[p][:], in1=gc_sb[p][:]))
                E.emit("vector", lambda v: v.tensor_sub(
                    out=rw[:, 3 * TILE:4 * TILE], in0=scr_sb[0:4, 0:TILE], in1=lnnk),
                    inc=(SV, 1))
                tk["rows"][p] = SV.n
                E.wait("sync", SV, tk["rows"][p])
                E.emit("sync", lambda s: s.dma_start(
                    out=ap_(rows_h, 2 * T_ALL + j * TILE, [[3 * T_ALL, 4], [T_ALL, 3], [1, TILE]]),
                    in_=ap_(rw[:], 0, [[4 * TILE, 4], [TILE, 3], [1, TILE]])), inc=(DR, 16))
                E.emit("sync", lambda s: s.dma_start(
                    out=ap_(rows_h, 14 * T_ALL + j * TILE, [[T_ALL, 4], [1, TILE]]),
                    in_=gl_sb[p][:]), inc=(DR, 16))
                E.wait("sync", DR, DR.n)
                E.wait("sync", SV, tk["maskv"][p])
                E.wait("sync", SV, tk["opb"][p])
                E.emit("sync", lambda s: s.dma_start(
                    out=arep_sb[p][:].rearrange("q (h t) -> q h t", t=TILE),
                    in_=ap_(rows_h, 2 * T_ALL + j * TILE, [[0, 128], [3 * T_ALL, 4], [1, TILE]])),
                    inc=(DRR[p], 16))
                E.emit("sync", lambda s: s.dma_start(
                    out=carep_sb[p][:].rearrange("q (h t) -> q h t", t=TILE),
                    in_=ap_(rows_h, 3 * T_ALL + j * TILE, [[0, 128], [3 * T_ALL, 4], [1, TILE]])),
                    inc=(DRR[p], 16))
                E.emit("sync", lambda s: s.dma_start(
                    out=srep_sb[p][:].rearrange("q (h t) -> q h t", t=TILE),
                    in_=ap_(rows_h, 4 * T_ALL + j * TILE, [[0, 128], [3 * T_ALL, 4], [1, TILE]])),
                    inc=(DRR[p], 16))
                E.emit("sync", lambda s: s.dma_start(
                    out=ap_(rows_h, 18 * T_ALL + j * 16, [[4, 4], [1, 4]]),
                    in_=ap_(gl_sb[p][:], 0, [[TILE, 4], [64, 4], [1, 1]]),
                    allow_slow_non_contiguous=True), inc=(DR, 16))
                E.wait("sync", DR, DR.n)
                E.emit("sync", lambda s: s.dma_start(
                    out=egrep_sb[p][:],
                    in_=ap_(rows_h, 18 * T_ALL + j * 16, [[0, 128], [1, 16]])),
                    inc=(DRR[p], 16))
                tk["rep"][p] = DRR[p].n
                E.wait("scalar", DRR[p], tk["rep"][p])
                E.emit("scalar", lambda s: s.activation(
                    out=egrep_sb[p][:], in_=egrep_sb[p][:], func=AF.Exp), inc=(SS, 1))
                tk["eg"][p] = SS.n

            def cp_stage(j, p):
                E.wait("tensor", SV, tk["rows"][p])
                E.wait("tensor", SV, bank["PR"])
                rw = rws_sb[p]
                srcs = [rw[:, TILE:2 * TILE], rw[:, 2 * TILE:3 * TILE],
                        rw[:, 0:TILE], beta_sb[p][:], rw[:, 3 * TILE:4 * TILE]]
                slots = [0, 1, 2, 3, 6]
                nt = 0
                for ts in range(2):
                    for b_i, src in enumerate(srcs):
                        slot = slots[b_i]
                        nt += 1
                        E.emit("tensor", lambda t, ts=ts, slot=slot, src=src: t.transpose(
                            PR[0:128, ts * 32 + slot * 4: ts * 32 + slot * 4 + 4],
                            src[:, ts * 128:(ts + 1) * 128], id128f_sb[0:4, 0:4]),
                            inc=(ST, 1) if nt == 10 else None)
                tk["cpT"][p] = ST.n
                E.wait("vector", ST, ST.n)
                E.wait("vector", SV, tk["opb"][p])
                E.emit("vector", lambda v: v.tensor_copy(
                    out=ap_(cp_sb[p][:], 0, [[64, 128], [32, 2], [1, 16]]),
                    in_=ap_(PR[:], 0, [[512, 128], [32, 2], [1, 16]])))
                E.emit("vector", lambda v: v.tensor_copy(
                    out=ap_(cp_sb[p][:], 24, [[64, 128], [32, 2], [1, 4]]),
                    in_=ap_(PR[:], 24, [[512, 128], [32, 2], [1, 4]])), inc=(SV, 1))
                bank["PR"] = SV.n
                E.wait("scalar", SV, SV.n)
                for ts in range(2):
                    for dst_b, src_b in [(4, 0), (5, 1), (6, 6)]:
                        E.emit("scalar", lambda s, ts=ts, dst_b=dst_b, src_b=src_b: s.activation(
                            out=cp_sb[p][:, ts * 32 + dst_b * 4: ts * 32 + dst_b * 4 + 4],
                            in_=cp_sb[p][:, ts * 32 + src_b * 4: ts * 32 + src_b * 4 + 4],
                            func=AF.Exp), inc=(SS, 1))
                tk["cp"][p] = SS.n

            def tm_stage(j, p):
                E.wait("tensor", SS, tk["qkv"][p])
                E.wait("scalar", SV, tk["opb"][p])
                for ts in range(2):
                    bwait("tensor", bank["PT"])
                    for kh in range(2):
                        E.emit("tensor", lambda t, ts=ts, kh=kh: t.transpose(
                            PT[0:128, kh * 128:(kh + 1) * 128],
                            qkv_sb[p][:, (2 + kh) * TILE + ts * 128: (2 + kh) * TILE + (ts + 1) * 128],
                            id128_sb[:]), inc=(ST, 1) if kh == 1 else None)
                    E.wait("scalar", ST, ST.n)
                    E.emit("scalar", lambda s, ts=ts: s.activation(
                        out=ktm_sb[p][:, ts * 256:(ts + 1) * 256],
                        in_=PT[0:128, 0:256], func=AF.Copy), inc=(SS, 1))
                    bank["PT"] = (SS, SS.n)
                    bwait("tensor", bank["PT"])
                    for hv in range(4):
                        E.emit("tensor", lambda t, ts=ts, hv=hv: t.transpose(
                            PT[0:128, hv * 128:(hv + 1) * 128],
                            qkv_sb[p][:, (4 + hv) * TILE + ts * 128: (4 + hv) * TILE + (ts + 1) * 128],
                            id128_sb[:]), inc=(ST, 1) if hv == 3 else None)
                    E.wait("scalar", ST, ST.n)
                    E.emit("scalar", lambda s, ts=ts: s.activation(
                        out=vtm_sb[p][:, ts * 512:(ts + 1) * 512],
                        in_=PT[0:128, 0:512], func=AF.Copy), inc=(SS, 1))
                    bank["PT"] = (SS, SS.n)
                tk["tm"][p] = SS.n

            def delta_parts(j, p, cc):
                dt = {}
                ts = cc // 2
                q0 = 64 * (cc % 2)
                pdp = cc % 2
                tok0 = cc * 64

                def kblk(kh):
                    return qkv_sb[p][:, (2 + kh) * TILE + tok0: (2 + kh) * TILE + tok0 + 64]

                def qblk(kh):
                    return qkv_sb[p][:, kh * TILE + tok0: kh * TILE + tok0 + 64]

                def hv4(buf, w, lo=None):
                    lo = q0 if lo is None else lo
                    return ap_(buf[:], lo * (4 * w), [[4 * w, 64], [w, 4], [1, w]])

                def cpc(b_i, w):
                    return ap_(cp_sb[p][:], q0 * 64 + ts * 32 + b_i * 4,
                               [[64, 64], [1, 4], [0, w]])

                def pre():
                 E.wait("tensor", SS, tk["qkv"][p])
                 bwait("tensor", bank["PD"][pdp])
                 for kh in range(2):
                     E.emit("tensor", lambda t, kh=kh: t.ldweights(kblk(kh), tile_position=(0, q0)))
                     E.emit("tensor", lambda t, kh=kh: t.matmul(
                         PD[pdp][q0:q0 + 64, kh * 64:(kh + 1) * 64], kblk(kh), kblk(kh),
                         start=True, stop=True, tile_position=(0, q0)))
                     E.emit("tensor", lambda t, kh=kh: t.matmul(
                         PD[pdp][q0:q0 + 64, 128 + kh * 64: 128 + (kh + 1) * 64],
                         kblk(kh), qblk(kh),
                         start=True, stop=True, tile_position=(0, q0)),
                         inc=(ST, 1) if kh == 1 else None)
                 gs_tok = ST.n

                 E.wait("vector", DRR[p], tk["rep"][p])
                 E.wait("vector", SS, tk["cp"][p])
                 for mk, rep, colb, tmpl in [
                         (mka_sb, arep_sb[p], 0, tlo_sb),
                         (mkc_sb, carep_sb[p], 2, thi_sb),
                         (mks_sb, srep_sb[p], 2, thii_sb)]:
                     repv = ap_(rep[:], q0 * (4 * TILE) + tok0,
                                [[4 * TILE, 64], [TILE, 4], [1, 64]])
                     E.emit("vector", lambda v, mk=mk, repv=repv, colb=colb: v.tensor_tensor(
                         out=hv4(mk, 64), in0=repv, in1=cpc(colb, 64), op=ALU.add))
                     E.emit("vector", lambda v, mk=mk: v.tensor_scalar(
                         out=mk[q0:q0 + 64, :], in0=mk[q0:q0 + 64, :], scalar1=CLAMP,
                         scalar2=None, op0=ALU.min), inc=(SV, 1))
                     E.wait("scalar", SV, SV.n)
                     E.emit("scalar", lambda s, mk=mk: s.activation(
                         out=mk[q0:q0 + 64, :], in_=mk[q0:q0 + 64, :], func=AF.Exp),
                         inc=(SS, 1))
                     E.wait("vector", SS, SS.n)
                     tmv = ap_(tmpl[:], q0 * 64, [[64, 64], [0, 4], [1, 64]])
                     E.emit("vector", lambda v, mk=mk, tmv=tmv: v.tensor_tensor(
                         out=hv4(mk, 64), in0=hv4(mk, 64), in1=tmv, op=ALU.mult),
                         inc=(SV, 1))
                 tk["maskv"][p] = SV.n

                 E.wait("vector", ST, gs_tok)

                 def khpair(buf, kh, w=64):
                     return ap_(buf[:], q0 * (4 * w) + kh * 2 * w,
                                [[4 * w, 64], [w, 2], [1, w]])

                 for kh in range(2):
                     gd = ap_(PD[pdp][:], q0 * 512 + kh * 64, [[512, 64], [0, 2], [1, 64]])
                     sd = ap_(PD[pdp][:], q0 * 512 + 128 + kh * 64, [[512, 64], [0, 2], [1, 64]])
                     E.emit("vector", lambda v, kh=kh, gd=gd: v.tensor_tensor(
                         out=khpair(ra_sb, kh), in0=gd, in1=khpair(mka_sb, kh), op=ALU.mult))
                     E.emit("vector", lambda v, kh=kh, gd=gd: v.tensor_tensor(
                         out=khpair(ca_sb, kh), in0=gd, in1=khpair(mkc_sb, kh), op=ALU.mult))
                     E.emit("vector", lambda v, kh=kh, sd=sd: v.tensor_tensor(
                         out=khpair(sc_sb, kh), in0=sd, in1=khpair(mks_sb, kh), op=ALU.mult),
                         inc=(SV, 1) if kh == 1 else None)
                 bank["PD"][pdp] = (SV, SV.n)
                 racasc_tok = SV.n

                 def lvlR(lvl, hv):
                     if lvl < 0:
                         return ap_(ra_sb[:], q0 * 256 + hv * 64, [[256, 64], [1, 64]])
                     return ap_(pw_sb[:], q0 * 2048 + lvl * 512 + hv * 128, [[2048, 64], [1, 64]])

                 def lvlC(lvl, hv):
                     if lvl < 0:
                         return ap_(ca_sb[:], q0 * 256 + hv * 64, [[256, 64], [1, 64]])
                     return ap_(pw_sb[:], q0 * 2048 + lvl * 512 + hv * 128 + 64, [[2048, 64], [1, 64]])

                 E.wait("tensor", SV, racasc_tok)
                 for lvl in range(4):
                     lvl_bank = 1 - pdp if lvl % 2 == 0 else pdp
                     bwait("tensor", bank["PD"][lvl_bank])
                     if lvl > 0:
                         E.wait("tensor", SS, SS.n)
                     for hv in range(4):
                         lC = lvlC(lvl - 1, hv)
                         lR = lvlR(lvl - 1, hv)
                         E.emit("tensor", lambda t, lC=lC: t.ldweights(lC, tile_position=(q0, q0)))
                         E.emit("tensor", lambda t, lC=lC, lR=lR, hv=hv, lvl_bank=lvl_bank: t.matmul(
                             PD[lvl_bank][q0:q0 + 64, hv * 128: hv * 128 + 64], lC, lR,
                             start=True, stop=True, tile_position=(q0, q0)))
                         E.emit("tensor", lambda t, lR=lR: t.ldweights(lR, tile_position=(q0, q0)))
                         E.emit("tensor", lambda t, lC=lC, lR=lR, hv=hv, lvl_bank=lvl_bank: t.matmul(
                             PD[lvl_bank][q0:q0 + 64, hv * 128 + 64: hv * 128 + 128], lR, lC,
                             start=True, stop=True, tile_position=(q0, q0)),
                             inc=(ST, 1) if hv == 3 else None)
                     E.wait("scalar", ST, ST.n)
                     E.emit("scalar", lambda s, lvl=lvl, lvl_bank=lvl_bank: s.activation(
                         out=pw_sb[q0:q0 + 64, lvl * 512:(lvl + 1) * 512],
                         in_=PD[lvl_bank][q0:q0 + 64, 0:512], func=AF.Copy), inc=(SS, 1))
                     bank["PD"][lvl_bank] = (SS, SS.n)

                 def pwR(lvl, hv):
                     return lvlR(lvl, hv)

                 def pwC(lvl, hv):
                     return lvlC(lvl, hv)

                 bwait("tensor", bank["PD"][pdp])
                 E.wait("tensor", SS, SS.n)
                 for hv in range(4):
                     E.emit("tensor", lambda t, hv=hv: t.ldweights(pwC(3, hv), tile_position=(q0, q0)))
                     E.emit("tensor", lambda t, hv=hv: t.matmul(
                         PD[pdp][q0:q0 + 64, hv * 64:(hv + 1) * 64], pwC(3, hv), pwR(3, hv),
                         start=True, stop=True, tile_position=(q0, q0)),
                         inc=(ST, 1) if hv == 3 else None)
                 E.wait("scalar", ST, ST.n)
                 E.emit("scalar", lambda s: s.activation(
                     out=r32_sb[q0:q0 + 64, :], in_=PD[pdp][q0:q0 + 64, 0:256],
                     func=AF.Copy), inc=(SS, 1))
                 bank["PD"][pdp] = (SS, SS.n)

                 E.wait("vector", SS, SS.n)
                 eyev = ap_(eye2_sb[:], q0 * 64, [[64, 64], [0, 4], [1, 64]])
                 E.emit("vector", lambda v: v.tensor_tensor(
                     out=hv4(chn_sb, 64), in0=hv4(ca_sb, 64), in1=eyev, op=ALU.add),
                     inc=(SV, 1))
                 for step in range(5):
                     cb = step % 2
                     E.wait("tensor", SV, SV.n)
                     bwait("tensor", bank["PD"][cb])
                     for hv in range(4):
                         if step < 4:
                             lhs = ap_(pw_sb[:], q0 * 2048 + step * 512 + hv * 128,
                                       [[2048, 64], [1, 64]])
                         else:
                             lhs = ap_(r32_sb[:], q0 * 256 + hv * 64, [[256, 64], [1, 64]])
                         rhs = ap_(chn_sb[:], q0 * 256 + hv * 64, [[256, 64], [1, 64]])
                         E.emit("tensor", lambda t, lhs=lhs: t.ldweights(lhs, tile_position=(q0, q0)))
                         E.emit("tensor", lambda t, lhs=lhs, rhs=rhs, hv=hv, cb=cb: t.matmul(
                             PD[cb][q0:q0 + 64, hv * 64:(hv + 1) * 64], lhs, rhs,
                             start=True, stop=True, tile_position=(q0, q0)),
                             inc=(ST, 1) if hv == 3 else None)
                     E.wait("vector", ST, ST.n)
                     E.emit("vector", lambda v, cb=cb: v.scalar_tensor_tensor(
                         out=chn_sb[q0:q0 + 64, :], in0=PD[cb][q0:q0 + 64, 0:256],
                         scalar=0.0, in1=chn_sb[q0:q0 + 64, :],
                         op0=ALU.bypass, op1=ALU.add), inc=(SV, 1))
                     bank["PD"][cb] = (SV, SV.n)
                 chn_tok = SV.n

                 E.wait("vector", SS, tk["tm"][p])
                 vtmv = ap_(vtm_sb[p][:], q0 * 1024 + ts * 512, [[1024, 64], [128, 4], [1, 128]])
                 E.emit("vector", lambda v: v.tensor_tensor(
                     out=hv4(vb_sb, 128), in0=vtmv, in1=cpc(3, 128), op=ALU.mult))

                 def cpc2(b_i, kh, w):
                     return ap_(cp_sb[p][:], q0 * 64 + ts * 32 + b_i * 4 + kh * 2,
                                [[64, 64], [1, 2], [0, w]])

                 for kh in range(2):
                     ktd = ap_(ktm_sb[p][:], q0 * 512 + ts * 256 + kh * 128,
                               [[512, 64], [0, 2], [1, 128]])
                     kep = ap_(ke_sb[:], q0 * 512 + kh * 2 * 128, [[512, 64], [128, 2], [1, 128]])
                     k2p = ap_(k2t_sb[:], q0 * 512 + kh * 2 * 128, [[512, 64], [128, 2], [1, 128]])
                     E.emit("vector", lambda v, kh=kh, ktd=ktd, kep=kep: v.scalar_tensor_tensor(
                         out=kep, in0=ktd, scalar=-1.0, in1=cpc2(4, kh, 128),
                         op0=ALU.mult, op1=ALU.mult))
                     E.emit("vector", lambda v, kh=kh, ktd=ktd, k2p=k2p: v.tensor_tensor(
                         out=k2p, in0=ktd, in1=cpc2(6, kh, 128), op=ALU.mult),
                         inc=(SV, 1) if kh == 1 else None)
                 opb_tok = SV.n
                 tk["opb"][p] = SV.n

                 E.wait("tensor", SV, opb_tok)
                 bwait("tensor", bank["PD"][pdp])
                 for hv in range(4):
                     ksl = ap_(ke_sb[:], q0 * 512 + hv * 128, [[512, 64], [1, 128]])
                     tsl = ap_(chn_sb[:], q0 * 256 + hv * 64, [[256, 64], [1, 64]])
                     E.emit("tensor", lambda t, ksl=ksl: t.ldweights(ksl, tile_position=(q0, 0)))
                     E.emit("tensor", lambda t, ksl=ksl, tsl=tsl, hv=hv: t.matmul(
                         PD[pdp][0:128, hv * 64:(hv + 1) * 64], ksl, tsl,
                         start=True, stop=True, tile_position=(q0, 0)),
                         inc=(ST, 1) if hv == 3 else None)
                 E.wait("scalar", ST, ST.n)
                 E.emit("scalar", lambda s: s.activation(
                     out=kcd_sb[:, 0:256], in_=PD[pdp][0:128, 0:256], func=AF.Copy),
                     inc=(SS, 1))
                 bank["PD"][pdp] = (SS, SS.n)
                 dt["kcd_tok"] = SS.n

                def spine():
                 kcd_tok = dt["kcd_tok"]
                 E.wait("tensor", SS, kcd_tok)
                 E.wait("tensor", SV, bank["PV"])
                 E.wait("tensor", SV, state_tok[p])
                 E.wait("tensor", SS, statebf_tok[p])
                 for hv in range(4):
                     tsl = ap_(chn_sb[:], q0 * 256 + hv * 64, [[256, 64], [1, 64]])
                     vsl = ap_(vb_sb[:], q0 * 512 + hv * 128, [[512, 64], [1, 128]])
                     E.emit("tensor", lambda t, tsl=tsl: t.ldweights(tsl, tile_position=(q0, q0)))
                     E.emit("tensor", lambda t, tsl=tsl, vsl=vsl, hv=hv: t.matmul(
                         PV[q0:q0 + 64, hv * 128:(hv + 1) * 128], tsl, vsl,
                         start=True, stop=False, tile_position=(q0, q0)))
                     ksl = kcd_sb[:, hv * 64:(hv + 1) * 64]
                     E.emit("tensor", lambda t, ksl=ksl: t.ldweights(ksl, tile_position=(0, q0)))
                     E.emit("tensor", lambda t, ksl=ksl, hv=hv: t.matmul(
                         PV[q0:q0 + 64, hv * 128:(hv + 1) * 128], ksl,
                         stbf_sb[p][:, hv * 128:(hv + 1) * 128],
                         start=False, stop=True, tile_position=(0, q0)),
                         inc=(ST, 1) if hv == 3 else None)
                 E.wait("vector", ST, ST.n)
                 E.emit("vector", lambda v: v.tensor_copy(
                     out=vn_sb[q0:q0 + 64, :], in_=PV[q0:q0 + 64, 0:512]), inc=(SV, 1))
                 bank["PV"] = SV.n
                 vn_tok = SV.n
                 bwait("tensor", bank["PD"][pdp])
                 for hv in range(4):
                     qb = qblk(hv // 2)
                     E.emit("tensor", lambda t, qb=qb: t.ldweights(qb, tile_position=(0, q0)))
                     E.emit("tensor", lambda t, qb=qb, hv=hv: t.matmul(
                         PD[pdp][q0:q0 + 64, hv * 128:(hv + 1) * 128], qb,
                         stbf_sb[p][:, hv * 128:(hv + 1) * 128],
                         start=True, stop=True, tile_position=(0, q0)),
                         inc=(ST, 1) if hv == 3 else None)
                 inter_tok = ST.n
                 tk["deltaq"][p] = ST.n
                 pint_tok = 0
                 if "pint" in dbg and j == ntile - 1:
                     E.wait("scalar", ST, inter_tok)
                     E.emit("scalar", lambda s: s.activation(
                         out=pint_sb[q0:q0 + 64, :], in_=PD[pdp][q0:q0 + 64, 0:512],
                         func=AF.Copy), inc=(SS, 1))
                     pint_tok = SS.n
                 E.wait("tensor", SV, vn_tok)
                 bwait("tensor", bank["PO"])
                 for hv in range(4):
                     ssl = ap_(sc_sb[:], q0 * 256 + hv * 64, [[256, 64], [1, 64]])
                     vsl = ap_(vn_sb[:], q0 * 512 + hv * 128, [[512, 64], [1, 128]])
                     E.emit("tensor", lambda t, ssl=ssl: t.ldweights(ssl, tile_position=(q0, q0)))
                     E.emit("tensor", lambda t, ssl=ssl, vsl=vsl, hv=hv: t.matmul(
                         PO[q0:q0 + 64, hv * 128:(hv + 1) * 128], ssl, vsl,
                         start=True, stop=True, tile_position=(q0, q0)),
                         inc=(ST, 1) if hv == 3 else None)
                 attnv_tok = ST.n
                 patt_tok = 0
                 if "patt" in dbg and j == ntile - 1:
                     E.wait("scalar", ST, attnv_tok)
                     E.emit("scalar", lambda s: s.activation(
                         out=patt_sb[q0:q0 + 64, :], in_=PO[q0:q0 + 64, 0:512],
                         func=AF.Copy), inc=(SS, 1))
                     patt_tok = SS.n
                 E.wait("vector", ST, inter_tok)
                 pdv = ap_(PD[pdp][:], q0 * 512, [[512, 64], [128, 4], [1, 128]])
                 E.emit("vector", lambda v: v.tensor_tensor(
                     out=hv4(o2_sb, 128), in0=pdv, in1=cpc(5, 128), op=ALU.mult),
                     inc=(SV, 1))
                 bank["PD"][pdp] = ([(SV, SV.n), (SS, pint_tok)]
                                    if pint_tok else (SV, SV.n))
                 E.wait("vector", ST, attnv_tok)
                 E.emit("vector", lambda v: v.tensor_tensor(
                     out=o2_sb[q0:q0 + 64, :], in0=o2_sb[q0:q0 + 64, :],
                     in1=PO[q0:q0 + 64, 0:512], op=ALU.add), inc=(SV, 1))
                 bank["PO"] = ([(SV, SV.n), (SS, patt_tok)]
                               if patt_tok else (SV, SV.n))
                 oi_tok = SV.n
                 dt["oi_tok"] = oi_tok
                 if "o2pre" in dbg and j == ntile - 1:
                     E.wait("sync", SV, oi_tok)
                     E.emit("sync", lambda s: s.dma_start(
                         out=ap_(dbg["o2pre"], q0 * 512, [[512, 64], [1, 512]]),
                         in_=o2_sb[q0:q0 + 64, :]), inc=(DR, 16))
                     dt["o2pre_tok"] = DR.n
                 E.wait("tensor", SV, vn_tok)
                 E.wait("tensor", SV, bank["PV"])
                 for hv in range(4):
                     k2sl = ap_(k2t_sb[:], q0 * 512 + hv * 128, [[512, 64], [1, 128]])
                     vsl = ap_(vn_sb[:], q0 * 512 + hv * 128, [[512, 64], [1, 128]])
                     E.emit("tensor", lambda t, k2sl=k2sl: t.ldweights(k2sl, tile_position=(q0, 0)))
                     E.emit("tensor", lambda t, k2sl=k2sl, vsl=vsl, hv=hv: t.matmul(
                         PV[0:128, hv * 128:(hv + 1) * 128], k2sl, vsl,
                         start=True, stop=True, tile_position=(q0, 0)),
                         inc=(ST, 1) if hv == 3 else None)
                 ds_tok = ST.n
                 E.wait("vector", SS, tk["eg"][p])
                 E.wait("vector", ST, ds_tok)
                 for hv in range(4):
                     E.emit("vector", lambda v, hv=hv: v.tensor_scalar(
                         out=st_sb[p][:, hv * 128:(hv + 1) * 128],
                         in0=st_sb[p][:, hv * 128:(hv + 1) * 128],
                         scalar1=egrep_sb[p][:, cc + 4 * hv: cc + 4 * hv + 1],
                         scalar2=None, op0=ALU.mult))
                 E.edge("vector", SV)
                 E.emit("vector", lambda v: v.tensor_add(
                     out=st_sb[p][:], in0=st_sb[p][:], in1=PV[0:128, 0:512]),
                     inc=(SV, 1))
                 bank["PV"] = SV.n
                 E.wait("scalar", SV, SV.n)
                 E.emit("scalar", lambda s: s.activation(
                     out=stbf_sb[p][:], in_=st_sb[p][:], func=AF.Copy), inc=(SS, 1))
                 state_tok[p] = SV.n
                 statebf_tok[p] = SS.n

                def tail():
                 oi_tok = dt["oi_tok"]
                 E.wait("scalar", SV, oi_tok)
                 for rnd in range(2):
                     if rnd == 1:
                         E.wait("scalar", SV, SV.n)
                     E.emit("scalar", lambda s, rnd=rnd: s.activation(
                         out=rsq32_sb[q0:q0 + 64, :],
                         in_=o2_sb[q0:q0 + 64, rnd * 256:(rnd + 1) * 256],
                         func=AF.Square), inc=(SS, 1))
                     E.wait("vector", SS, SS.n)
                     for hh in range(2):
                         E.emit("vector", lambda v, rnd=rnd, hh=hh: v.reduce_sum(
                             out=rms_sb[q0:q0 + 64, rnd * 2 + hh: rnd * 2 + hh + 1],
                             in_=rsq32_sb[q0:q0 + 64, hh * 128:(hh + 1) * 128],
                             axis=mybir.AxisListType.X),
                             inc=(SV, 1) if hh == 1 else None)
                 E.edge("vector", SV)
                 E.emit("vector", lambda v: v.tensor_scalar(
                     out=rms_sb[q0:q0 + 64, 4:8], in0=rms_sb[q0:q0 + 64, 0:4],
                     scalar1=1.0 / 128.0, scalar2=1e-6, op0=ALU.mult, op1=ALU.add),
                     inc=(SV, 1))
                 if "rmsv" in dbg:
                     E.emit("vector", lambda v: v.tensor_copy(
                         out=rms_sb[q0:q0 + 64, 8:12], in_=rms_sb[q0:q0 + 64, 4:8]))
                 E.wait("scalar", SV, SV.n)
                 E.emit("scalar", lambda s: s.activation(
                     out=rms_sb[q0:q0 + 64, 4:8], in_=rms_sb[q0:q0 + 64, 4:8],
                     func=AF.Sqrt), inc=(SS, 1))
                 E.wait("vector", SS, SS.n)
                 if "rmsv" in dbg:
                     E.emit("vector", lambda v: v.tensor_copy(
                         out=rms_sb[q0:q0 + 64, 12:16], in_=rms_sb[q0:q0 + 64, 4:8]))
                 E.emit("vector", lambda v: v.reciprocal(
                     out=rms_sb[q0:q0 + 64, 4:8], in_=rms_sb[q0:q0 + 64, 4:8]))
                 E.edge("vector", SV)
                 if "o2pre" in dbg and j == ntile - 1:
                     E.wait("vector", DR, dt["o2pre_tok"])
                 rsqv = ap_(rms_sb[:], q0 * 20 + 4, [[20, 64], [1, 4], [0, 128]])
                 E.emit("vector", lambda v: v.tensor_tensor(
                     out=hv4(o2_sb, 128), in0=hv4(o2_sb, 128), in1=rsqv, op=ALU.mult))
                 E.wait("vector", SS, tk["z"][p])
                 E.wait("vector", DHS[p], tk["hst"][p])
                 E.emit("vector", lambda v: v.tensor_tensor(
                     out=hbuf_sb[p][q0:q0 + 64, ts * 512:(ts + 1) * 512],
                     in0=o2_sb[q0:q0 + 64, :],
                     in1=z_sb[p][q0:q0 + 64, ts * 512:(ts + 1) * 512], op=ALU.mult),
                     inc=(SV, 1))
                 tk["h"][p] = SV.n
                return pre, spine, tail

            half = ntile // 2 if ntile >= 2 else 1
            if ntile >= 2:
                order = []
                for k in range(half):
                    order.append(k)
                    order.append(half + k)
                order += list(range(2 * half, ntile))
            else:
                order = list(range(ntile))
            def h_store(jj, p, ts):
                E.wait("gpsimd", SV, tk["h"][p])
                E.emit("gpsimd", lambda g, jj=jj, ts=ts, p=p: g.dma_start(
                    out=h_src[jj * TILE + ts * 128: jj * TILE + (ts + 1) * 128, :],
                    in_=hbuf_sb[p][:, ts * 512:(ts + 1) * 512]), inc=(DHS[p], 16))
                tk["hst"][p] = DHS[p].n

            prev = None
            for s, jj in enumerate(order):
                p = s % 2
                ln_stage(jj, p)
                xn_stage(jj, p)
                if s + 2 < len(order):
                    E.wait("gpsimd", SV, tk["xn"][p])
                    load_x(order[s + 2], p)
                    x_tok[p] = SGX[p].n
                fe_stage(jj, p)
                rows_stage(jj, p)
                cp_stage(jj, p)
                tm_stage(jj, p)
                parts = [delta_parts(jj, p, cc) for cc in range(4)]
                parts[0][0]()
                parts[0][1]()
                parts[1][0]()
                parts[0][2]()
                parts[1][1]()
                parts[2][0]()
                parts[1][2]()
                h_store(jj, p, 0)
                parts[2][1]()
                parts[3][0]()
                parts[2][2]()
                parts[3][1]()
                parts[3][2]()
                h_store(jj, p, 1)
                prev = (jj, p)

            E.wait("gpsimd", DHS[0], DHS[0].n)
            E.wait("gpsimd", DHS[1], DHS[1].n)
            if "h_src" in dbg:
                E.wait("gpsimd", DH, DH.n)
                E.emit("gpsimd", lambda g: g.dma_start(
                    out=ap_(dbg["h_src"], 0, [[512, ntile * TILE], [1, 512]]),
                    in_=h_src[0:ntile * TILE, :]), inc=(DH, 16))
            if "qkv0" in dbg:
                E.emit("gpsimd", lambda g: g.dma_start(
                    out=dbg["qkv0"], in_=qkv_sb[(ntile - 1) % 2][:]), inc=(DH, 16))
            lpar = (ntile - 1) % 2
            for nm, src in [("xn0", lambda: xn_sb[lpar][:]),
                            ("z0", lambda: z_sb[lpar][:]),
                            ("cp0", lambda: cp_sb[lpar][:]),
                            ("stf", lambda: st_sb[(ntile - 1) % 2][:]),
                            ("o20", lambda: o2_sb[:]),
                            ("vn0", lambda: vn_sb[:]),
                            ("chn0", lambda: chn_sb[:]),
                            ("kcd0", lambda: kcd_sb[:]),
                            ("pint", lambda: pint_sb[:]),
                            ("patt", lambda: patt_sb[:]),
                            ("rmsv", lambda: rms_sb[:])]:
                if nm in dbg:
                    E.wait("gpsimd", SV, SV.n)
                    E.wait("gpsimd", SS, SS.n)
                    E.wait("gpsimd", ST, ST.n)
                    E.emit("gpsimd", lambda g, nm=nm, src=src: g.dma_start(
                        out=dbg[nm], in_=src()), inc=(DH, 16))
            if "rows" in dbg:
                E.wait("gpsimd", DR, DR.n)
                E.wait("gpsimd", SV, SV.n)
                E.emit("gpsimd", lambda g: g.dma_start(
                    out=ap_(dbg["rows"], 0, [[T_ALL, 20], [1, ntile * TILE]]),
                    in_=ap_(rows_h, 0, [[T_ALL, 20], [1, ntile * TILE]])), inc=(DH, 16))

            E.wait("gpsimd", DH, DH.n)
            if phase2:
                E.emit("gpsimd", lambda g: g.collective_compute(
                    "AllGather", ALU.bypass,
                    ins=[h_src[:, :]], outs=[h_all[:, :, :]],
                    replica_groups=[list(range(NCORE))]), inc=(CC, 1))

            p1_join = dict(t=ST.n, v=SV.n, s=SS.n, dx=DX.n, dr=DR.n, dh=DH.n, g=SG.n,
                           sgx0=SGX[0].n, sgx1=SGX[1].n, sgw=SGW.n,
                           dhs0=DHS[0].n, dhs1=DHS[1].n,
                           drm0=DRM[0].n, drm1=DRM[1].n, drl0=DRL[0].n,
                           drl1=DRL[1].n, drr0=DRR[0].n, drr1=DRR[1].n)

        if not phase2:
            E.wait("gpsimd", DH, DH.n)
            E.emit("gpsimd", lambda g: g.memset(sel_sb[:], 0.0))
        else:
            p2 = ExitStack()
            with p2:
                SB2 = lambda n, sh, dt: p2.enter_context(nc.sbuf_tensor(n, sh, dt))
                xb2_sb = SB2("xb2", [128, 16 * 256], BF16)
                x2b2_sb = SB2("x2b2", [128, 256], BF16)
                xn2_sb = SB2("xn2", [128, 16 * 512], BF16)
                mr2_sb = SB2("mr2", [1, 2 * 256], F32)
                scr2_sb = SB2("scr2", [4, 2 * 256], F32)
                mrep2_sb = SB2("mrep2", [128, 256], F32)
                rrep2_sb = SB2("rrep2", [128, 256], F32)
                csrep_sb = SB2("csrep", [128, 512], F32)
                cirep_sb = SB2("cirep", [128, 512], F32)
                wg_sb = [SB2(f"wg{i}", [128, 16 * 128], BF16) for i in range(2)]
                wu_sb = [SB2(f"wu{i}", [128, 16 * 128], BF16) for i in range(2)]
                wd_sb = [SB2("wd0", [128, 64 * 128], BF16)]
                wo_sb = [SB2(f"wo{i}", [128, 4 * 128], BF16) for i in range(2)]
                hfm_sb = [SB2(f"hfm{i}", [128, 1024], BF16) for i in range(2)]
                ii_sb = SB2("ii", [128, 64 * 512], BF16)
                gfin_sb = SB2("gfin", [128, 16 * 512], BF16)
                sgbuf_sb = SB2("sgbuf", [128, 512], F32)
                asm_sb = SB2("asm", [128, 4 * 2048], F32)
                xtm_sb = SB2("xtm", [128, 2048], F32)
                big1_sb = SB2("big1", [128, 512], F32)
                big2_sb = SB2("big2", [128, 512], F32)
                sil_sb = SB2("sil", [128, 512], F32)

                for eng in ("sync", "scalar", "vector", "tensor", "gpsimd"):
                    for snm, sobj in [("t", ST), ("v", SV), ("s", SS), ("dr", DR),
                                      ("dh", DH), ("dx", DX), ("g", SG),
                                      ("sgx0", SGX[0]), ("sgx1", SGX[1]), ("sgw", SGW),
                                      ("dhs0", DHS[0]), ("dhs1", DHS[1]),
                                      ("drm0", DRM[0]), ("drm1", DRM[1]),
                                      ("drl0", DRL[0]), ("drl1", DRL[1]),
                                      ("drr0", DRR[0]), ("drr1", DRR[1])]:
                        E.wait(eng, sobj, p1_join[snm])

                E.emit("gpsimd", lambda g: g.dma_start(
                    out=xb2_sb[:].rearrange("p (a t) -> p a t", t=256),
                    in_=x2_fm_d.rearrange("(a p) t -> p a t", p=128)[:, :, 0:256]),
                    inc=(SG, 16))
                xb2_ld_tok = SG.n
                xb2_use_tok = 0

                for wk in range(8):
                    E.emit("gpsimd", lambda g, wk=wk: g.dma_start(
                        out=ap_(wo16_d, wk * 512 * HID, [[HID, 512], [1, HID]]),
                        in_=ap_(WoutT_d, wk * 512 * HID, [[HID, 512], [1, HID]])),
                        inc=(SG, 16))
                wo16_tok = SG.n

                E.wait("gpsimd", CC, CC.n)
                E.wait("sync", CC, CC.n)
                IIW = 64 * 512
                cbv = lambda pb, off, dims: ap_(ii_sb[:], pb * 4096 + off, dims)
                accv = lambda pb: ap_(ii_sb[:], 8192 + pb * 512, [[IIW, 128], [1, 512]])
                ac2v = lambda pb: ap_(ii_sb[:], 9216 + pb * 512, [[IIW, 128], [1, 512]])
                cb_tok = [0, 0]
                acc_tok = [0, 0]
                for i in range(NCORE * 8):
                    rr, tb = i // 8, i % 8
                    pb = i % 2
                    if cb_tok[pb]:
                        E.wait("gpsimd", SV, cb_tok[pb])
                    E.emit("gpsimd", lambda g, rr=rr, tb=tb, pb=pb: g.dma_start(
                        out=cbv(pb, 0, [[IIW, 128], [512, 8], [1, 512]]),
                        in_=ap_(h_all, rr * T_ALL * 512 + tb * 128 * 512,
                                [[512, 128], [TOK_PC * 512, 8], [1, 512]])),
                        inc=(SG, 16))
                    E.wait("vector", SG, SG.n)
                    if acc_tok[pb]:
                        E.wait("vector", DH, acc_tok[pb])
                    for cand in range(NCORE):
                        dst = accv(pb) if cand % 2 == 0 else ac2v(pb)
                        srcv = cbv(pb, cand * 512, [[IIW, 128], [1, 512]])
                        if cand < 2:
                            E.emit("vector", lambda v, cand=cand, dst=dst, srcv=srcv: v.tensor_scalar(
                                out=dst, in0=srcv,
                                scalar1=sel_sb[:, cand:cand + 1], scalar2=None,
                                op0=ALU.mult))
                        else:
                            E.emit("vector", lambda v, cand=cand, dst=dst, srcv=srcv: v.scalar_tensor_tensor(
                                out=dst, in0=srcv,
                                scalar=sel_sb[:, cand:cand + 1], in1=dst,
                                op0=ALU.mult, op1=ALU.add))
                    E.edge("vector", SV)
                    E.emit("vector", lambda v, pb=pb: v.tensor_add(
                        out=accv(pb), in0=accv(pb), in1=ac2v(pb)), inc=(SV, 1))
                    cb_tok[pb] = SV.n
                    E.wait("gpsimd", SV, SV.n)
                    E.emit("gpsimd", lambda g, rr=rr, tb=tb, pb=pb: g.dma_start(
                        out=ap_(h_mine, tb * 128 * 4096 + rr * 512, [[4096, 128], [1, 512]]),
                        in_=accv(pb)), inc=(DH, 16))
                    acc_tok[pb] = DH.n
                hsel_tok = DH.n
                ht_tok = {}
                for vt in range(32):
                    if vt >= 2:
                        E.wait("sync", DH, ht_tok[vt - 2])
                    E.wait("sync", DH, hsel_tok)
                    E.emit("sync", lambda s, vt=vt: s.dma_start(
                        out=hfm_sb[vt % 2][:],
                        in_=h_mine[:, vt * 128:(vt + 1) * 128], transpose=True),
                        inc=(DR, 16))
                    E.wait("vector", DR, DR.n)
                    E.emit("vector", lambda v, vt=vt: v.tensor_scalar(
                        out=hfm_sb[vt % 2][:], in0=hfm_sb[vt % 2][:],
                        scalar1=gamma_sb[:, :], scalar2=None, op0=ALU.mult),
                        inc=(SV, 1))
                    E.wait("gpsimd", SV, SV.n)
                    E.emit("gpsimd", lambda g, vt=vt: g.dma_start(
                        out=ap_(h_fm_d, vt * 128 * TOK_PC, [[TOK_PC, 128], [1, TOK_PC]]),
                        in_=hfm_sb[vt % 2][:]), inc=(DH, 16))
                    ht_tok[vt] = DH.n
                hfm_done = DH.n
                if "h_mine" in dbg:
                    E.wait("gpsimd", DH, hsel_tok)
                    E.emit("gpsimd", lambda g: g.dma_start(
                        out=dbg["h_mine"], in_=h_mine[:, :]), inc=(DH, 16))
                    hsel_tok = DH.n


                vt_tok = {}
                gu_hist = {}
                dn_hist = {}
                for qq in range(2):
                    tq = qq * 512
                    for tt2 in range(2):
                        tt = qq * 2 + tt2
                        if not (qq == 0 and tt2 == 0):
                            E.wait("gpsimd", SV, xb2_use_tok)
                            E.emit("gpsimd", lambda g, tt=tt: g.dma_start(
                                out=xb2_sb[:].rearrange("p (a t) -> p a t", t=256),
                                in_=x2_fm_d.rearrange("(a p) t -> p a t", p=128)[:, :, tt * 256:(tt + 1) * 256]),
                                inc=(SG, 16))
                            xb2_ld_tok = SG.n
                        E.wait("vector", SG, xb2_ld_tok)
                        E.wait("tensor", SG, xb2_ld_tok)
                        E.wait("tensor", SV, SV.n)
                        for ft in range(16):
                            if ft > 0:
                                E.wait("vector", ST, ST.n)
                            E.emit("vector", lambda v, ft=ft: v.tensor_mul(
                                out=x2b2_sb[:], in0=xb2_sb[:, ft * 256:(ft + 1) * 256],
                                in1=xb2_sb[:, ft * 256:(ft + 1) * 256]), inc=(SV, 1))
                            E.wait("tensor", SV, SV.n)
                            E.emit("tensor", lambda t, ft=ft: t.matmul(
                                PR[0:1, 0:256], ones_sb[:], xb2_sb[:, ft * 256:(ft + 1) * 256],
                                start=(ft == 0), stop=(ft == 15)))
                            E.emit("tensor", lambda t, ft=ft: t.matmul(
                                PO[0:1, 0:256], ones_sb[:], x2b2_sb[:],
                                start=(ft == 0), stop=(ft == 15)), inc=(ST, 1))
                        E.wait("vector", ST, ST.n)
                        E.emit("vector", lambda v: v.tensor_scalar(
                            out=mr2_sb[:, 0:256], in0=PR[0:1, 0:256], scalar1=1.0 / HID,
                            scalar2=None, op0=ALU.mult))
                        E.emit("vector", lambda v: v.tensor_scalar(
                            out=scr2_sb[0:1, 0:256], in0=PO[0:1, 0:256], scalar1=1.0 / HID,
                            scalar2=None, op0=ALU.mult), inc=(SV, 1))
                        E.edge("vector", SV)
                        E.emit("vector", lambda v: v.tensor_mul(
                            out=scr2_sb[0:1, 256:512], in0=mr2_sb[:, 0:256], in1=mr2_sb[:, 0:256]))
                        E.edge("vector", SV)
                        E.emit("vector", lambda v: v.scalar_tensor_tensor(
                            out=mr2_sb[:, 256:512], in0=scr2_sb[0:1, 0:256], scalar=1e-5,
                            in1=scr2_sb[0:1, 256:512], op0=ALU.add, op1=ALU.subtract), inc=(SV, 1))
                        E.wait("scalar", SV, SV.n)
                        E.emit("scalar", lambda s: s.activation(
                            out=mr2_sb[:, 256:512], in_=mr2_sb[:, 256:512], func=AF.Sqrt),
                            inc=(SS, 1))
                        E.wait("vector", SS, SS.n)
                        E.emit("vector", lambda v: v.reciprocal(
                            out=mr2_sb[:, 256:512], in_=mr2_sb[:, 256:512]), inc=(SV, 1))
                        E.wait("sync", SV, SV.n)
                        E.emit("sync", lambda s, tt=tt: s.dma_start(
                            out=ap_(rows2_h, tt * 256, [[TOK_PC, 2], [1, 256]]),
                            in_=ap_(mr2_sb[:], 0, [[512, 1], [256, 2], [1, 256]])), inc=(DR, 16))
                        E.wait("sync", DR, DR.n)
                        E.emit("sync", lambda s, tt=tt: s.dma_start(
                            out=mrep2_sb[:], in_=ap_(rows2_h, tt * 256, [[0, 128], [1, 256]])),
                            inc=(DR, 16))
                        E.emit("sync", lambda s, tt=tt: s.dma_start(
                            out=rrep2_sb[:], in_=ap_(rows2_h, TOK_PC + tt * 256, [[0, 128], [1, 256]])),
                            inc=(DR, 16))
                        E.wait("vector", DR, DR.n)
                        xbv = ap_(xb2_sb[:], 0, [[16 * 256, 128], [256, 16], [1, 256]])
                        xnv = ap_(xn2_sb[:], tt2 * 256, [[16 * 512, 128], [512, 16], [1, 256]])
                        mrv = ap_(mrep2_sb[:], 0, [[256, 128], [0, 16], [1, 256]])
                        rrv = ap_(rrep2_sb[:], 0, [[256, 128], [0, 16], [1, 256]])
                        E.emit("vector", lambda v, xbv=xbv, xnv=xnv, mrv=mrv: v.tensor_tensor(
                            out=xnv, in0=xbv, in1=mrv, op=ALU.subtract))
                        E.emit("vector", lambda v, xnv=xnv, rrv=rrv: v.tensor_tensor(
                            out=xnv, in0=xnv, in1=rrv, op=ALU.mult), inc=(SV, 1))
                        xb2_use_tok = SV.n
                    xn2_tok = SV.n
                    if "xn2" in dbg:
                        E.wait("gpsimd", SV, xn2_tok)
                        E.emit("gpsimd", lambda g, qq=qq: g.dma_start(
                            out=ap_(dbg["xn2"], qq * 128 * 8192, [[8192, 128], [1, 8192]]),
                            in_=xn2_sb[:]), inc=(DH, 16))
                        E.wait("vector", DH, DH.n)
                    E.wait("tensor", SV, xn2_tok)
                    for ft in range(16):
                        E.emit("tensor", lambda t, ft=ft: t.matmul(
                            PR[0:1, 0:512], ones_sb[:], xn2_sb[:, ft * 512:(ft + 1) * 512],
                            start=(ft == 0), stop=(ft == 15)),
                            inc=(ST, 1) if ft == 15 else None)
                    E.wait("vector", ST, ST.n)
                    E.emit("vector", lambda v: v.tensor_copy(
                        out=scr2_sb[0:1, 0:512], in_=PR[0:1, 0:512]), inc=(SV, 1))
                    E.wait("sync", SV, SV.n)
                    E.emit("sync", lambda s: s.dma_start(
                        out=ap_(rows2_h, 2 * TOK_PC, [[TOK_PC, 1], [1, 512]]),
                        in_=scr2_sb[0:1, 0:512]), inc=(DR, 16))
                    E.wait("sync", DR, DR.n)
                    E.emit("sync", lambda s: s.dma_start(
                        out=csrep_sb[:], in_=ap_(rows2_h, 2 * TOK_PC, [[0, 128], [1, 512]])),
                        inc=(DR, 16))
                    cs_tok = DR.n
                    for ot in range(64):
                        got = qq * 64 + ot
                        if got >= 2:
                            E.wait("sync", ST, gu_hist[got - 2])
                        E.emit("sync", lambda s, ot=ot: s.dma_start(
                            out=wg_sb[ot % 2][:].rearrange("p (a o) -> p a o", o=128),
                            in_=ap_(gate_wT_d, ot * 128,
                                    [[8192, 128], [128 * 8192, 16], [1, 128]])), inc=(DX, 16))
                        if got >= 2:
                            E.wait("scalar", ST, gu_hist[got - 2])
                        E.emit("scalar", lambda s, ot=ot: s.dma_start(
                            out=wu_sb[ot % 2][:].rearrange("p (a o) -> p a o", o=128),
                            in_=ap_(up_wT_d, ot * 128,
                                    [[8192, 128], [128 * 8192, 16], [1, 128]])), inc=(DR, 16))
                        E.wait("tensor", DX, DX.n)
                        E.wait("tensor", DR, DR.n)
                        E.wait("tensor", SV, SV.n)
                        for ft in range(16):
                            E.emit("tensor", lambda t, ot=ot, ft=ft: t.ldweights(
                                wg_sb[ot % 2][:, ft * 128:(ft + 1) * 128]))
                            E.emit("tensor", lambda t, ot=ot, ft=ft: t.matmul(
                                PQ[0][0:128, 0:512], wg_sb[ot % 2][:, ft * 128:(ft + 1) * 128],
                                xn2_sb[:, ft * 512:(ft + 1) * 512],
                                start=(ft == 0), stop=(ft == 15)))
                            E.emit("tensor", lambda t, ot=ot, ft=ft: t.ldweights(
                                wu_sb[ot % 2][:, ft * 128:(ft + 1) * 128]))
                            E.emit("tensor", lambda t, ot=ot, ft=ft: t.matmul(
                                PQ[1][0:128, 0:512], wu_sb[ot % 2][:, ft * 128:(ft + 1) * 128],
                                xn2_sb[:, ft * 512:(ft + 1) * 512],
                                start=(ft == 0), stop=(ft == 15)),
                                inc=(ST, 1) if ft == 15 else None)
                        gu_tok = ST.n
                        gu_hist[qq * 64 + ot] = ST.n
                        E.wait("vector", ST, gu_tok)
                        E.wait("vector", DR, cs_tok)
                        E.emit("vector", lambda v: v.scalar_tensor_tensor(
                            out=big1_sb[:], in0=csrep_sb[:], scalar=-128.0,
                            in1=PQ[0][0:128, 0:512], op0=ALU.mult, op1=ALU.add), inc=(SV, 1))
                        E.wait("scalar", SV, SV.n)
                        E.emit("scalar", lambda s, ot=ot: s.activation(
                            out=sil_sb[:], in_=big1_sb[:], func=AF.Silu,
                            scale=gs_sb[:, ot:ot + 1]), inc=(SS, 1))
                        E.emit("vector", lambda v: v.scalar_tensor_tensor(
                            out=big2_sb[:], in0=csrep_sb[:], scalar=-128.0,
                            in1=PQ[1][0:128, 0:512], op0=ALU.mult, op1=ALU.add), inc=(SV, 1))
                        E.wait("vector", SS, SS.n)
                        E.emit("vector", lambda v, ot=ot: v.scalar_tensor_tensor(
                            out=ii_sb[:, ot * 512:(ot + 1) * 512], in0=big2_sb[:],
                            scalar=us_sb[:, ot:ot + 1], in1=sil_sb[:],
                            op0=ALU.mult, op1=ALU.mult), inc=(SV, 1))
                        ii_tok = SV.n
                        E.wait("tensor", SV, ii_tok)
                        E.emit("tensor", lambda t, ot=ot: t.matmul(
                            PR[0:1, 0:512], ones_sb[:], ii_sb[:, ot * 512:(ot + 1) * 512],
                            start=(ot == 0), stop=(ot == 63)),
                            inc=(ST, 1) if ot == 63 else None)
                    if "ii" in dbg:
                        E.wait("gpsimd", SV, ii_tok)
                        E.emit("gpsimd", lambda g, qq=qq: g.dma_start(
                            out=ap_(dbg["ii"], qq * 128 * 32768, [[32768, 128], [1, 32768]]),
                            in_=ii_sb[:]), inc=(DH, 16))
                        E.wait("vector", DH, DH.n)
                    E.wait("vector", ST, ST.n)
                    E.emit("vector", lambda v: v.tensor_copy(
                        out=scr2_sb[0:1, 0:512], in_=PR[0:1, 0:512]), inc=(SV, 1))
                    E.wait("sync", SV, SV.n)
                    E.emit("sync", lambda s: s.dma_start(
                        out=ap_(rows2_h, 3 * TOK_PC, [[TOK_PC, 1], [1, 512]]),
                        in_=scr2_sb[0:1, 0:512]), inc=(DR, 16))
                    E.wait("sync", DR, DR.n)
                    E.emit("sync", lambda s: s.dma_start(
                        out=cirep_sb[:], in_=ap_(rows2_h, 3 * TOK_PC, [[0, 128], [1, 512]])),
                        inc=(DR, 16))
                    ci_tok = DR.n

                    for ht in range(16):
                        ght = qq * 16 + ht
                        if ght >= 1:
                            E.wait("sync", ST, dn_hist[(ght - 1, 0)])
                        E.emit("sync", lambda s, ht=ht: s.dma_start(
                            out=ap_(wd_sb[0][:], 0, [[64 * 128, 128], [128, 32], [1, 128]]),
                            in_=ap_(down_wT_d, ht * 128,
                                    [[2048, 128], [128 * 2048, 32], [1, 128]])), inc=(DX, 16))
                        if ght >= 1:
                            E.wait("scalar", ST, dn_hist[(ght - 1, 1)])
                        E.emit("scalar", lambda s, ht=ht: s.dma_start(
                            out=ap_(wd_sb[0][:], 32 * 128, [[64 * 128, 128], [128, 32], [1, 128]]),
                            in_=ap_(down_wT_d, ht * 128 + 32 * 128 * 2048,
                                    [[2048, 128], [128 * 2048, 32], [1, 128]])), inc=(DR, 16))
                        E.wait("tensor", DX, DX.n)
                        E.wait("tensor", SV, SV.n)
                        E.wait("tensor", SS, SS.n)
                        for it in range(32):
                            E.emit("tensor", lambda t, ht=ht, it=it: t.ldweights(
                                wd_sb[0][:, it * 128:(it + 1) * 128]))
                            E.emit("tensor", lambda t, ht=ht, it=it: t.matmul(
                                PD[ht % 2][0:128, 0:512],
                                wd_sb[0][:, it * 128:(it + 1) * 128],
                                ii_sb[:, it * 512:(it + 1) * 512],
                                start=(it == 0), stop=False),
                                inc=(ST, 1) if it == 31 else None)
                        dn_hist[(ght, 0)] = ST.n
                        E.wait("tensor", DR, DR.n)
                        for it in range(32, 64):
                            E.emit("tensor", lambda t, ht=ht, it=it: t.ldweights(
                                wd_sb[0][:, it * 128:(it + 1) * 128]))
                            E.emit("tensor", lambda t, ht=ht, it=it: t.matmul(
                                PD[ht % 2][0:128, 0:512],
                                wd_sb[0][:, it * 128:(it + 1) * 128],
                                ii_sb[:, it * 512:(it + 1) * 512],
                                start=False, stop=(it == 63)),
                                inc=(ST, 1) if it == 63 else None)
                        dn_hist[(ght, 1)] = ST.n
                        dmm_tok = ST.n
                        E.wait("vector", ST, dmm_tok)
                        E.wait("vector", DR, ci_tok)
                        E.emit("vector", lambda v, ht=ht: v.scalar_tensor_tensor(
                            out=big1_sb[:], in0=cirep_sb[:], scalar=-128.0,
                            in1=PD[ht % 2][0:128, 0:512], op0=ALU.mult, op1=ALU.add),
                            inc=(SV, 1))
                        E.wait("scalar", SV, SV.n)
                        E.emit("scalar", lambda s, ht=ht: s.activation(
                            out=gfin_sb[:, ht * 512:(ht + 1) * 512], in_=big1_sb[:],
                            func=AF.Silu, scale=ds_sb[:, ht:ht + 1]), inc=(SS, 1))
                    gfin_tok = SS.n
                    if "gfin" in dbg:
                        E.wait("gpsimd", SS, gfin_tok)
                        E.emit("gpsimd", lambda g, qq=qq: g.dma_start(
                            out=ap_(dbg["gfin"], qq * 128 * 8192, [[8192, 128], [1, 8192]]),
                            in_=gfin_sb[:]), inc=(DH, 16))
                        E.wait("scalar", DH, DH.n)

                    for og in range(4):
                        for vt in range(32):
                            gvt = (qq * 4 + og) * 32 + vt
                            if gvt >= 2:
                                E.wait("sync", ST, vt_tok[gvt - 2])
                            if gvt == 0:
                                E.wait("sync", SG, wo16_tok)
                                E.wait("sync", DH, hfm_done)
                            E.emit("sync", lambda s, og=og, vt=vt: s.dma_start(
                                out=wo_sb[vt % 2][:],
                                in_=ap_(wo16_d, vt * 128 * HID + og * 512,
                                        [[HID, 128], [1, 512]])), inc=(DX, 16))
                            if gvt >= 2:
                                E.wait("scalar", ST, vt_tok[gvt - 2])
                            E.emit("scalar", lambda s, vt=vt, tq=tq: s.dma_start(
                                out=ap_(hfm_sb[vt % 2][:], 0, [[1024, 128], [1, 512]]),
                                in_=ap_(h_fm_d, vt * 128 * TOK_PC + tq,
                                        [[TOK_PC, 128], [1, 512]])), inc=(DR, 16))
                            E.wait("tensor", DX, DX.n)
                            E.wait("tensor", DR, DR.n)
                            for o4 in range(4):
                                pbk = [PQ[0], PQ[1], PD[0], PD[1]][o4]
                                E.emit("tensor", lambda t, vt=vt, o4=o4: t.ldweights(
                                    wo_sb[vt % 2][:, o4 * 128:(o4 + 1) * 128]))
                                E.emit("tensor", lambda t, vt=vt, o4=o4, pbk=pbk: t.matmul(
                                    pbk[0:128, 0:512],
                                    wo_sb[vt % 2][:, o4 * 128:(o4 + 1) * 128],
                                    ap_(hfm_sb[vt % 2][:], 0, [[1024, 128], [1, 512]]),
                                    start=(vt == 0), stop=(vt == 31)),
                                    inc=(ST, 1) if o4 == 3 else None)
                            vt_tok[gvt] = ST.n
                        wout_tok = ST.n
                        E.wait("vector", ST, wout_tok)
                        E.wait("vector", SS, gfin_tok)
                        E.wait("vector", DH, DH.n)
                        for o4 in range(4):
                            ot2 = og * 4 + o4
                            pbk = [PQ[0], PQ[1], PD[0], PD[1]][o4]
                            E.emit("vector", lambda v, pbk=pbk, ot2=ot2: v.tensor_tensor(
                                out=sgbuf_sb[:], in0=pbk[0:128, 0:512],
                                in1=gfin_sb[:, ot2 * 512:(ot2 + 1) * 512], op=ALU.mult),
                                inc=(SV, 1))
                            E.wait("tensor", SV, SV.n)
                            for tb in range(4):
                                E.emit("tensor", lambda t, tb=tb: t.transpose(
                                    PR[0:128, tb * 128:(tb + 1) * 128],
                                    sgbuf_sb[:, tb * 128:(tb + 1) * 128], id128f_sb[:]),
                                    inc=(ST, 1) if tb == 3 else None)
                            E.wait("vector", ST, ST.n)
                            E.emit("vector", lambda v, ot2=ot2: v.tensor_copy(
                                out=ap_(asm_sb[:], ot2 * 128,
                                        [[4 * 2048, 128], [2048, 4], [1, 128]]),
                                in_=PR[0:128, 0:512]), inc=(SV, 1))
                    asm_tok = SV.n

                    for tb in range(4):
                        E.wait("sync", SV, SV.n)
                        E.emit("sync", lambda s, qq=qq, tb=tb, tq=tq: s.dma_start(
                            out=xtm_sb[:],
                            in_=x_tm_d[tq + tb * 128: tq + (tb + 1) * 128, :]), inc=(DX, 16))
                        E.wait("vector", DX, DX.n)
                        E.wait("vector", SV, asm_tok)
                        E.emit("vector", lambda v, tb=tb: v.tensor_add(
                            out=asm_sb[:, tb * 2048:(tb + 1) * 2048], in0=xtm_sb[:],
                            in1=asm_sb[:, tb * 2048:(tb + 1) * 2048]), inc=(SV, 1))
                        E.wait("gpsimd", SV, SV.n)
                        E.emit("gpsimd", lambda g, qq=qq, tb=tb, tq=tq: g.dma_start(
                            out=out_d[tq + tb * 128: tq + (tb + 1) * 128, :],
                            in_=asm_sb[:, tb * 2048:(tb + 1) * 2048]), inc=(DH, 16))
                        E.wait("vector", DH, DH.n)
                    if "asm" in dbg:
                        E.wait("gpsimd", SV, SV.n)
                        E.emit("gpsimd", lambda g, qq=qq: g.dma_start(
                            out=ap_(dbg["asm"], qq * 128 * 8192, [[8192, 128], [1, 8192]]),
                            in_=asm_sb[:]), inc=(DH, 16))
                        E.wait("vector", DH, DH.n)

                E.wait("gpsimd", DH, DH.n)

        with nc.Block() as block:
            @block.sync
            def _(sync):
                E.run("sync", sync)

            @block.scalar
            def _(scalar):
                E.run("scalar", scalar)

            @block.vector
            def _(vector):
                E.run("vector", vector)

            @block.tensor
            def _(tensor):
                E.run("tensor", tensor)

            @block.gpsimd
            def _(gpsimd):
                E.run("gpsimd", gpsimd)

    return nc, dbg


_BUILD_CACHE = {}


def kernel(**inputs):
    percore, x = _host_prep(inputs)
    key = "full"
    if key not in _BUILD_CACHE:
        _BUILD_CACHE[key] = build_program()
    nc, _ = _BUILD_CACHE[key]
    names = ["x_fm", "x2_fm", "x_tm", "WqkvT", "WzT", "WbaT", "convw", "dtb",
             "alog", "gamma", "WoutT", "gate_wT", "up_wT", "down_wT", "gate_s",
             "up_s", "down_s", "t_lo", "t_hi", "t_hii", "eye2", "ident128",
             "ident128f", "onescol", "sel"]
    in_maps = [{n: percore[c][n] for n in names} for c in range(NCORE)]
    res = run_bass_kernel_spmd(nc, in_maps, core_ids=list(range(NCORE)))
    out = np.concatenate([np.asarray(res.results[c]["out"]) for c in range(NCORE)], axis=0)
    return out.reshape(2, 4096, 2048)


# version 10
# speedup vs baseline: 1.0035x; 1.0035x over previous
import math
import numpy as np
from contextlib import ExitStack

import concourse.bass as bass
import concourse.mybir as mybir
from concourse.bass_utils import run_bass_kernel_spmd
import ml_dtypes

F32 = mybir.dt.float32
BF16 = mybir.dt.bfloat16
U8 = mybir.dt.uint8
U32 = mybir.dt.uint32
AF = mybir.ActivationFunctionType
ALU = mybir.AluOpType
BF = ml_dtypes.bfloat16

HID = 2048
T_ALL = 8192
CH = 64
TILE = 256
NTILE = T_ALL // TILE
NCORE = 8
TOK_PC = 1024
LNDKI = float(np.float32(math.log(128.0 ** -0.5)))
CLAMP = 80.0


class Sem:
    def __init__(self):
        self.h = None
        self.n = 0


class Emitter:
    ENGINES = ("sync", "scalar", "vector", "tensor", "gpsimd")

    def __init__(self):
        self.ops = {e: [] for e in self.ENGINES}

    def wait(self, eng, sem, val=None):
        v = sem.n if val is None else val
        if v > 0:
            self.ops[eng].append(("w", sem, v))

    def emit(self, eng, fn, inc=None):
        incs = [] if inc is None else ([inc] if isinstance(inc, tuple) else list(inc))
        for s, k in incs:
            s.n += k
        self.ops[eng].append(("e", fn, incs))

    def edge(self, eng, sem):
        for it in reversed(self.ops[eng]):
            if it[0] == "e":
                if not any(s is sem for s, _k in it[2]):
                    sem.n += 1
                    it[2].append((sem, 1))
                break
        else:
            raise RuntimeError("edge: no prior op on " + eng)
        self.ops[eng].append(("w", sem, sem.n))

    def run(self, eng, engobj):
        for it in self.ops[eng]:
            if it[0] == "w":
                engobj.wait_ge(it[1].h, it[2])
            else:
                ins = it[1](engobj)
                for s, k in it[2]:
                    ins = ins.then_inc(s.h, k)


def ap_(base_ap, off, dims):
    return bass.AP(base_ap.tensor, base_ap.offset + off, dims)


def _host_prep(inputs):
    x = np.ascontiguousarray(
        np.asarray(inputs["L0_attn_hidden_states"], np.float32).reshape(T_ALL, HID))
    x_fm = np.ascontiguousarray(x.T)
    Wq = np.asarray(inputs["L0_attn_W_qkvz"], np.float32)
    Wba = np.asarray(inputs["L0_attn_W_ba"], np.float32)
    convw = np.asarray(inputs["L0_attn_conv_w"], np.float32).reshape(8192, 4)
    dtb = np.asarray(inputs["L0_attn_dt_bias"], np.float32)
    Alog = np.asarray(inputs["L0_attn_A_log"], np.float32)
    gamma = np.asarray(inputs["L0_attn_norm_weight"], np.float32)
    WoutT = np.ascontiguousarray(np.asarray(inputs["L0_attn_W_out"], np.float32).T)
    gate_wT = np.ascontiguousarray(np.asarray(inputs["L0_ffn_gate_w"], np.int32).astype(np.uint8).T.astype(BF))
    up_wT = np.ascontiguousarray(np.asarray(inputs["L0_ffn_up_w"], np.int32).astype(np.uint8).T.astype(BF))
    down_wT = np.ascontiguousarray(np.asarray(inputs["L0_ffn_down_w"], np.int32).astype(np.uint8).T.astype(BF))

    idx = np.arange(CH)
    t_lo = np.ascontiguousarray(np.tile(-(idx[:, None] > idx[None, :]).astype(np.float32), (2, 1)))
    t_hi = np.ascontiguousarray(np.tile(-(idx[:, None] < idx[None, :]).astype(np.float32), (2, 1)))
    t_hii = np.ascontiguousarray(np.tile((idx[:, None] <= idx[None, :]).astype(np.float32), (2, 1)))
    eye2 = np.ascontiguousarray(np.tile(np.eye(CH, dtype=np.float32), (2, 1)).astype(BF))
    ident128 = np.ascontiguousarray(np.eye(128, dtype=np.float32).astype(BF))
    ident128f = np.ascontiguousarray(np.eye(128, dtype=np.float32))
    onescol = np.ascontiguousarray(np.ones((128, 1), dtype=np.float32).astype(BF))

    percore = []
    for c in range(NCORE):
        kh = [2 * c, 2 * c + 1]
        vh = [4 * c + i for i in range(4)]
        rows = []
        for h in kh:
            rows += list(range(h * 128, (h + 1) * 128))
        for h in kh:
            rows += list(range(2048 + h * 128, 2048 + (h + 1) * 128))
        for h in vh:
            rows += list(range(4096 + h * 128, 4096 + (h + 1) * 128))
        zrows = [8192 + h * 128 + i for h in vh for i in range(128)]
        barows = list(vh) + [32 + h for h in vh]
        sel = np.zeros((128, 8), np.float32)
        sel[:, c] = 1
        percore.append(dict(
            x_fm=x_fm,
            x2_fm=np.ascontiguousarray(x_fm[:, c * TOK_PC:(c + 1) * TOK_PC]),
            x_tm=np.ascontiguousarray(x[c * TOK_PC:(c + 1) * TOK_PC, :]),
            WqkvT=np.ascontiguousarray(Wq[rows, :].T),
            WzT=np.ascontiguousarray(Wq[zrows, :].T),
            WbaT=np.ascontiguousarray(Wba[barows, :].T),
            convw=np.ascontiguousarray(convw[rows, :]),
            dtb=np.ascontiguousarray(dtb[vh].reshape(4, 1)),
            alog=np.ascontiguousarray(Alog[vh].reshape(4, 1)),
            gamma=np.ascontiguousarray(gamma.reshape(128, 1)),
            WoutT=WoutT,
            gate_wT=gate_wT, up_wT=up_wT, down_wT=down_wT,
            gate_s=np.ascontiguousarray(np.asarray(inputs["L0_ffn_gate_scale"], np.float32).reshape(64, 128).T),
            up_s=np.ascontiguousarray(np.asarray(inputs["L0_ffn_up_scale"], np.float32).reshape(64, 128).T),
            down_s=np.ascontiguousarray(np.asarray(inputs["L0_ffn_down_scale"], np.float32).reshape(16, 128).T),
            t_lo=t_lo, t_hi=t_hi, t_hii=t_hii, eye2=eye2,
            ident128=ident128, ident128f=ident128f, onescol=onescol,
            sel=sel,
        ))
    return percore, x


def build_program(ntile=NTILE, phase2=True, debug=()):
    nc = bass.Bass("TRN2", target_bir_lowering=False,
                   detect_race_conditions=False)
    E = Emitter()

    din = lambda name, shape, dt: nc.dram_tensor(name, shape, dt, kind="ExternalInput").ap()
    x_fm_d = din("x_fm", [HID, T_ALL], F32)
    x2_fm_d = din("x2_fm", [HID, TOK_PC], F32)
    x_tm_d = din("x_tm", [TOK_PC, HID], F32)
    WqkvT_d = din("WqkvT", [HID, 1024], F32)
    WzT_d = din("WzT", [HID, 512], F32)
    WbaT_d = din("WbaT", [HID, 8], F32)
    convw_d = din("convw", [1024, 4], F32)
    dtb_d = din("dtb", [4, 1], F32)
    alog_d = din("alog", [4, 1], F32)
    gamma_d = din("gamma", [128, 1], F32)
    WoutT_d = din("WoutT", [4096, 2048], F32)
    gate_wT_d = din("gate_wT", [HID, 8192], BF16)
    up_wT_d = din("up_wT", [HID, 8192], BF16)
    down_wT_d = din("down_wT", [8192, HID], BF16)
    gate_s_d = din("gate_s", [128, 64], F32)
    up_s_d = din("up_s", [128, 64], F32)
    down_s_d = din("down_s", [128, 16], F32)
    tlo_d = din("t_lo", [128, 64], F32)
    thi_d = din("t_hi", [128, 64], F32)
    thii_d = din("t_hii", [128, 64], F32)
    eye2_d = din("eye2", [128, 64], BF16)
    id128_d = din("ident128", [128, 128], BF16)
    id128f_d = din("ident128f", [128, 128], F32)
    ones_d = din("onescol", [128, 1], BF16)
    sel_d = din("sel", [128, 8], F32)

    out_d = nc.dram_tensor("out", [TOK_PC, HID], F32, kind="ExternalOutput").ap()

    rows_h = nc.dram_tensor("rows_h", [20, T_ALL], F32, kind="Internal").ap()
    h_src = nc.dram_tensor("h_src", [T_ALL, 512], BF16, kind="Internal").ap()
    h_all = nc.dram_tensor("h_all", [NCORE, T_ALL, 512], BF16, kind="Internal",
                           addr_space="Shared").ap()
    h_mine = nc.dram_tensor("h_mine", [TOK_PC, 4096], BF16, kind="Internal").ap()
    rows2_h = nc.dram_tensor("rows2_h", [4, TOK_PC], F32, kind="Internal").ap()
    wo16_d = nc.dram_tensor("wo16", [4096, HID], BF16, kind="Internal").ap()
    h_fm_d = nc.dram_tensor("h_fm", [4096, TOK_PC], BF16, kind="Internal").ap()

    dbg = {}
    def DBG(name, shape, dt=F32):
        if name in debug:
            dbg[name] = nc.dram_tensor("dbg_" + name, shape, dt, kind="ExternalOutput").ap()
        return dbg.get(name)

    DBG("h_src", [T_ALL, 512], BF16)
    DBG("qkv0", [128, 8 * TILE], BF16)
    DBG("xn0", [128, 16 * TILE], BF16)
    DBG("z0", [128, 1024], F32)
    DBG("rows", [20, T_ALL], F32)
    DBG("cp0", [128, 64], F32)
    DBG("stf", [128, 512], F32)
    DBG("o20", [128, 512], F32)
    DBG("vn0", [128, 512], F32)
    DBG("chn0", [128, 256], BF16)
    DBG("kcd0", [128, 256], BF16)
    DBG("pint", [128, 512], BF16)
    DBG("patt", [128, 512], BF16)
    DBG("o2pre", [128, 512], F32)
    DBG("rmsv", [128, 20], F32)
    DBG("h_mine", [TOK_PC, 4096], BF16)
    DBG("xn2", [2 * 128, 8192], BF16)
    DBG("ii", [2 * 128, 64 * 512], BF16)
    DBG("gfin", [2 * 128, 8192], BF16)
    DBG("asm", [2 * 128, 8192], F32)

    DX = Sem(); DR = Sem(); DH = Sem(); ST = Sem(); SV = Sem(); SS = Sem()
    SG = Sem(); CC = Sem()
    SGX = [Sem(), Sem()]; SGW = Sem(); DHS = [Sem(), Sem()]
    DRM = [Sem(), Sem()]; DRL = [Sem(), Sem()]; DRR = [Sem(), Sem()]
    ALL_SEMS = dict(dx=DX, dr=DR, dh=DH, t=ST, v=SV, s=SS, g=SG, cc=CC,
                    sgx0=SGX[0], sgx1=SGX[1], sgw=SGW, dhs0=DHS[0], dhs1=DHS[1],
                    drm0=DRM[0], drm1=DRM[1], drl0=DRL[0], drl1=DRL[1],
                    drr0=DRR[0], drr1=DRR[1])

    ctx = ExitStack()
    with ctx:
        for name, s in ALL_SEMS.items():
            s.h = ctx.enter_context(nc.semaphore("sem_" + name))
        SBt = lambda n, sh, dt: ctx.enter_context(nc.sbuf_tensor(n, sh, dt))
        _psn = [0]
        def PSt(sh, dt=F32):
            _psn[0] += 1
            return ctx.enter_context(nc.psum_tensor(f"ps{_psn[0]}", sh, dt))

        dtb_sb = SBt("dtb_sb", [4, 1], F32)
        nega_sb = SBt("nega_sb", [4, 1], F32)
        gamma_sb = SBt("gamma_sb", [128, 1], F32)
        tlo_sb = SBt("tlo_sb", [128, 64], F32)
        thi_sb = SBt("thi_sb", [128, 64], F32)
        thii_sb = SBt("thii_sb", [128, 64], F32)
        eye2_sb = SBt("eye2_sb", [128, 64], BF16)
        id128_sb = SBt("id128_sb", [128, 128], BF16)
        id128f_sb = SBt("id128f_sb", [128, 128], F32)
        ones_sb = SBt("ones_sb", [128, 1], BF16)
        gs_sb = SBt("gs_sb", [128, 64], F32)
        us_sb = SBt("us_sb", [128, 64], F32)
        ds_sb = SBt("ds_sb", [128, 16], F32)
        sel_sb = SBt("sel_sb", [128, 8], F32)

        PQ = [PSt([128, 512]), PSt([128, 512])]
        PD = [PSt([128, 512]), PSt([128, 512])]
        PV = PSt([128, 512])
        PO = PSt([128, 512])
        PR = PSt([128, 512])
        PT = PSt([128, 512], BF16)

        for src, dst in [
            (dtb_d, dtb_sb), (alog_d, nega_sb), (gamma_d, gamma_sb),
            (tlo_d, tlo_sb), (thi_d, thi_sb), (thii_d, thii_sb),
            (eye2_d, eye2_sb), (id128_d, id128_sb), (id128f_d, id128f_sb),
            (ones_d, ones_sb), (gate_s_d, gs_sb), (up_s_d, us_sb),
            (down_s_d, ds_sb), (sel_d, sel_sb),
        ]:
            E.emit("sync", lambda s, a=src, b=dst: s.dma_start(out=b[:], in_=a), inc=(DX, 16))
        const_tok = DX.n

        p1 = ExitStack()
        with p1:
            SB1 = lambda n, sh, dt: p1.enter_context(nc.sbuf_tensor(n, sh, dt))
            wq_sb = SB1("wq_sb", [128, 16 * 1024], BF16)
            wz_sb = SB1("wz_sb", [128, 16 * 512], BF16)
            wba_sb = SB1("wba_sb", [128, 16 * 8], BF16)
            convw_sb = SB1("convw_sb", [128, 8 * 4], F32)
            xb_sb = [SB1(f"xb{i}", [128, 16 * TILE], BF16) for i in range(2)]
            x2b_sb = SB1("x2b", [128, TILE], BF16)
            t32_sb = SB1("t32", [128, TILE], F32)
            tail_sb = [SB1(f"tail{i}", [128, 32], F32) for i in range(2)]
            rsq32_sb = SB1("rsq32", [128, TILE], BF16)
            q2b_sb = SB1("q2b", [128, TILE], BF16)
            xn_sb = [SB1(f"xn{i}", [128, 16 * TILE], BF16) for i in range(2)]
            mr_sb = [SB1(f"mr{i}", [1, 2 * TILE], F32) for i in range(2)]
            scr_sb = SB1("scr", [4, 2 * TILE], F32)
            bb_sb = [SB1(f"bb{i}", [4, TILE], F32) for i in range(2)]
            aa_sb = [SB1(f"aa{i}", [4, TILE], F32) for i in range(2)]
            lnbp_sb = SB1("lnbp", [4, TILE], F32)
            beta_sb = [SB1(f"beta{i}", [4, TILE], F32) for i in range(2)]
            gc_sb = [SB1(f"gc{i}", [4, TILE], F32) for i in range(2)]
            gl_sb = [SB1(f"gl{i}", [4, TILE], F32) for i in range(2)]
            l2p_sb = SB1("l2p", [1, 4 * TILE], F32)
            l2hv_sb = [SB1(f"l2hv{i}", [4, 2 * TILE], F32) for i in range(2)]
            rws_sb = [SB1(f"rws{i}", [4, 4 * TILE], F32) for i in range(2)]
            mrep_sb = [SB1(f"mrep{i}", [128, TILE], F32) for i in range(2)]
            rrep_sb = [SB1(f"rrep{i}", [128, TILE], F32) for i in range(2)]
            arep_sb = [SB1(f"arep{i}", [128, 4 * TILE], F32) for i in range(2)]
            carep_sb = [SB1(f"carep{i}", [128, 4 * TILE], F32) for i in range(2)]
            srep_sb = [SB1(f"srep{i}", [128, 4 * TILE], F32) for i in range(2)]
            egrep_sb = [SB1(f"egrep{i}", [128, 16], F32) for i in range(2)]
            mixed_sb = SB1("mixed", [128, 8 * (TILE + 4)], F32)
            conv_sb = SB1("conv", [128, 8 * TILE], F32)
            qkv_sb = [SB1(f"qkv{i}", [128, 8 * TILE], BF16) for i in range(2)]
            z_sb = [SB1(f"z{i}", [128, 2 * 512], BF16) for i in range(2)]
            batm_sb = [SB1(f"batm{i}", [128, 2 * 8], F32) for i in range(2)]
            ktm_sb = [SB1(f"ktm{i}", [128, 2 * 2 * 128], BF16) for i in range(2)]
            vtm_sb = [SB1(f"vtm{i}", [128, 2 * 4 * 128], BF16) for i in range(2)]
            cp_sb = [SB1(f"cp{i}", [128, 2 * 8 * 4], F32) for i in range(2)]
            st_sb = [SB1(f"states{i}", [128, 4 * 128], F32) for i in range(2)]
            stbf_sb = [SB1(f"statesbf{i}", [128, 4 * 128], BF16) for i in range(2)]
            hbuf_sb = [SB1(f"hbuf{i}", [128, 2 * 512], BF16) for i in range(2)]
            mka_sb = SB1("mka", [128, 4 * 64], F32)
            mkc_sb = SB1("mkc", [128, 4 * 64], F32)
            mks_sb = SB1("mks", [128, 4 * 64], F32)
            ra_sb = SB1("ra", [128, 4 * 64], BF16)
            ca_sb = SB1("ca", [128, 4 * 64], BF16)
            sc_sb = SB1("sc", [128, 4 * 64], BF16)
            pw_sb = SB1("pw", [128, 8 * 4 * 64], BF16)
            r32_sb = SB1("r32", [128, 4 * 64], BF16)
            chn_sb = SB1("chn", [128, 4 * 64], BF16)
            vb_sb = SB1("vb", [128, 4 * 128], BF16)
            ke_sb = SB1("ke", [128, 4 * 128], BF16)
            k2t_sb = SB1("k2t", [128, 4 * 128], BF16)
            kcd_sb = SB1("kcd", [128, 4 * 64], BF16)
            vn_sb = SB1("vn", [128, 4 * 128], BF16)
            o2_sb = SB1("o2", [128, 4 * 128], F32)
            rms_sb = SB1("rms", [128, 20], F32)
            if "pint" in dbg:
                pint_sb = SB1("pint_sb", [128, 512], BF16)
            if "patt" in dbg:
                patt_sb = SB1("patt_sb", [128, 512], BF16)

            E.emit("sync", lambda s: s.dma_start(
                out=convw_sb[:].rearrange("p (a k) -> p a k", k=4),
                in_=convw_d.rearrange("(a p) k -> p a k", p=128)),
                inc=(DX, 16))

            def load_x(j, p):
                E.emit("gpsimd", lambda g, j=j, p=p: g.dma_start(
                    out=xb_sb[p][:].rearrange("p (a t) -> p a t", t=TILE),
                    in_=x_fm_d.rearrange("(a p) t -> p a t", p=128)[:, :, j * TILE:(j + 1) * TILE]),
                    inc=(SGX[p], 16))

            load_x(0, 0)
            x_tok = [SGX[0].n, 0]
            load_x(NTILE // 2, 1)
            x_tok[1] = SGX[1].n

            E.emit("gpsimd", lambda g: g.dma_start(
                out=wq_sb[:].rearrange("p (a o) -> p a o", o=1024),
                in_=WqkvT_d.rearrange("(a p) o -> p a o", p=128)), inc=(SGW, 16))
            E.emit("gpsimd", lambda g: g.dma_start(
                out=wz_sb[:].rearrange("p (a o) -> p a o", o=512),
                in_=WzT_d.rearrange("(a p) o -> p a o", p=128)), inc=(SGW, 16))
            E.emit("gpsimd", lambda g: g.dma_start(
                out=wba_sb[:].rearrange("p (a o) -> p a o", o=8),
                in_=WbaT_d.rearrange("(a p) o -> p a o", p=128)), inc=(SGW, 16))
            w_tok = SGW.n
            const_tok = DX.n
            E.wait("scalar", DX, const_tok)
            E.emit("scalar", lambda s: s.activation(
                out=nega_sb[:], in_=nega_sb[:], func=AF.Exp), inc=(SS, 1))
            E.wait("vector", SS, SS.n)
            E.emit("vector", lambda v: v.tensor_scalar(
                out=nega_sb[:], in0=nega_sb[:], scalar1=-1.0, scalar2=None, op0=ALU.mult),
                inc=(SV, 1))
            E.wait("vector", DX, const_tok)
            E.emit("vector", lambda v: v.memset(st_sb[0][:], 0.0))
            E.emit("vector", lambda v: v.memset(stbf_sb[0][:], 0.0))
            E.emit("vector", lambda v: v.memset(st_sb[1][:], 0.0))
            E.emit("vector", lambda v: v.memset(stbf_sb[1][:], 0.0), inc=(SV, 1))
            st_ini_tok = SV.n

            tk = dict(xb=[0, 0], xn=[0, 0], mrw=[0, 0], qkv=[0, 0], z=[0, 0],
                      ba=[0, 0], rows=[0, 0], rep=[0, 0], cp=[0, 0], tm=[0, 0],
                      eg=[0, 0], h=[0, 0], hst=[0, 0], zmm=[0, 0], deltaq=[0, 0],
                      opb=[0, 0], maskv=[0, 0], cpT=[0, 0], baT=[0, 0],
                      mrr=[0, 0])
            bank = dict(PQ=[0, 0], PD=[None, None], PV=0, PO=None, PR=0, PT=None)

            def bwait(eng, tok):
                if tok is None:
                    return
                if isinstance(tok, list):
                    for t in tok:
                        bwait(eng, t)
                    return
                if tok[1] > 0:
                    E.wait(eng, tok[0], tok[1])
            state_tok = [st_ini_tok, st_ini_tok]
            statebf_tok = [0, 0]
            FE16 = 16 * TILE

            def v3(buf, dims, off=0):
                return ap_(buf[:], off, dims)

            def ln_stage(j, p):
                E.wait("vector", SGX[p], x_tok[p])
                E.wait("tensor", SGX[p], x_tok[p])
                E.wait("tensor", SV, bank["PR"])
                bwait("tensor", bank["PO"])
                for ft in range(16):
                    if ft > 0:
                        E.wait("vector", ST, ST.n)
                    E.emit("vector", lambda v, ft=ft: v.tensor_mul(
                        out=x2b_sb[:], in0=xb_sb[p][:, ft * TILE:(ft + 1) * TILE],
                        in1=xb_sb[p][:, ft * TILE:(ft + 1) * TILE]), inc=(SV, 1))
                    E.wait("tensor", SV, SV.n)
                    E.emit("tensor", lambda t, ft=ft: t.matmul(
                        PR[0:1, 0:TILE], ones_sb[:], xb_sb[p][:, ft * TILE:(ft + 1) * TILE],
                        start=(ft == 0), stop=(ft == 15)))
                    E.emit("tensor", lambda t, ft=ft: t.matmul(
                        PO[0:1, 0:TILE], ones_sb[:], x2b_sb[:],
                        start=(ft == 0), stop=(ft == 15)), inc=(ST, 1))
                mr = mr_sb[p]
                E.wait("vector", ST, ST.n)
                E.emit("vector", lambda v: v.tensor_scalar(
                    out=mr[:, 0:TILE], in0=PR[0:1, 0:TILE], scalar1=1.0 / HID,
                    scalar2=None, op0=ALU.mult))
                E.emit("vector", lambda v: v.tensor_scalar(
                    out=scr_sb[0:1, 0:TILE], in0=PO[0:1, 0:TILE],
                    scalar1=1.0 / HID, scalar2=None, op0=ALU.mult), inc=(SV, 1))
                bank["PR"] = SV.n
                bank["PO"] = (SV, SV.n)
                E.edge("vector", SV)
                E.emit("vector", lambda v: v.tensor_mul(
                    out=scr_sb[0:1, TILE:2 * TILE], in0=mr[:, 0:TILE], in1=mr[:, 0:TILE]))
                E.edge("vector", SV)
                E.emit("vector", lambda v: v.scalar_tensor_tensor(
                    out=mr[:, TILE:2 * TILE], in0=scr_sb[0:1, 0:TILE], scalar=1e-5,
                    in1=scr_sb[0:1, TILE:2 * TILE], op0=ALU.add, op1=ALU.subtract), inc=(SV, 1))
                E.wait("scalar", SV, SV.n)
                E.emit("scalar", lambda s: s.activation(
                    out=mr[:, TILE:2 * TILE], in_=mr[:, TILE:2 * TILE], func=AF.Sqrt),
                    inc=(SS, 1))
                E.wait("vector", SS, SS.n)
                E.emit("vector", lambda v: v.reciprocal(
                    out=mr[:, TILE:2 * TILE], in_=mr[:, TILE:2 * TILE]))
                E.edge("vector", SV)
                E.emit("vector", lambda v: v.tensor_mul(
                    out=scr_sb[0:1, 0:TILE], in0=mr[:, 0:TILE],
                    in1=mr[:, TILE:2 * TILE]), inc=(SV, 1))
                mr_tok = SV.n
                E.wait("sync", SV, mr_tok)
                E.emit("sync", lambda s: s.dma_start(
                    out=ap_(rows_h, j * TILE, [[T_ALL, 2], [1, TILE]]),
                    in_=ap_(mr[:], 0, [[2 * TILE, 1], [TILE, 2], [1, TILE]])), inc=(DR, 16))
                E.emit("sync", lambda s: s.dma_start(
                    out=ap_(rows_h, 19 * T_ALL + j * TILE, [[T_ALL, 1], [1, TILE]]),
                    in_=scr_sb[0:1, 0:TILE]), inc=(DR, 16))
                tk["mrr"][p] = DR.n
                E.wait("sync", DR, DR.n)
                E.wait("sync", SV, tk["xn"][p])
                E.emit("sync", lambda s: s.dma_start(
                    out=mrep_sb[p][:],
                    in_=ap_(rows_h, 19 * T_ALL + j * TILE, [[0, 128], [1, TILE]])), inc=(DRM[p], 16))
                E.emit("sync", lambda s: s.dma_start(
                    out=rrep_sb[p][:],
                    in_=ap_(rows_h, T_ALL + j * TILE, [[0, 128], [1, TILE]])), inc=(DRM[p], 16))
                tk["mrw"][p] = DRM[p].n

            def xn_stage(j, p):
                E.wait("vector", DRM[p], tk["mrw"][p])
                E.wait("vector", ST, tk["zmm"][p])
                for ft in range(16):
                    E.emit("vector", lambda v, ft=ft: v.tensor_mul(
                        out=t32_sb[:], in0=xb_sb[p][:, ft * TILE:(ft + 1) * TILE],
                        in1=rrep_sb[p][:]))
                    E.emit("vector", lambda v, ft=ft: v.tensor_sub(
                        out=xn_sb[p][:, ft * TILE:(ft + 1) * TILE],
                        in0=t32_sb[:], in1=mrep_sb[p][:]),
                        inc=(SV, 1) if ft == 15 else None)
                tk["xn"][p] = SV.n

            def fe_stage(j, p):
                E.wait("tensor", SV, tk["xn"][p])
                E.wait("tensor", SS, tk["z"][1 - p])
                E.wait("vector", SS, tk["qkv"][1 - p])
                if j == 0:
                    E.wait("tensor", SGW, w_tok)
                for mb in range(8):
                    pq = mb % 2
                    E.wait("tensor", SV, bank["PQ"][pq])
                    for ft in range(16):
                        wsl = wq_sb[:, ft * 1024 + mb * 128: ft * 1024 + (mb + 1) * 128]
                        E.emit("tensor", lambda t, w=wsl: t.ldweights(w))
                        E.emit("tensor", lambda t, w=wsl, ft=ft, mb=mb: t.matmul(
                            PQ[mb % 2][0:128, 0:TILE], w,
                            xn_sb[p][:, ft * TILE:(ft + 1) * TILE],
                            start=(ft == 0), stop=(ft == 15)),
                            inc=(ST, 1) if ft == 15 else None)
                    E.wait("vector", ST, ST.n)
                    if j == 0 or j == NTILE // 2:
                        E.emit("vector", lambda v, mb=mb: v.memset(
                            mixed_sb[:, mb * (TILE + 4): mb * (TILE + 4) + 4], 0.0))
                    else:
                        E.emit("vector", lambda v, mb=mb, p=p: v.tensor_copy(
                            out=mixed_sb[:, mb * (TILE + 4): mb * (TILE + 4) + 4],
                            in_=tail_sb[p][:, mb * 4:(mb + 1) * 4]))
                    E.emit("vector", lambda v, mb=mb: v.tensor_copy(
                        out=mixed_sb[:, mb * (TILE + 4) + 4: mb * (TILE + 4) + 4 + TILE],
                        in_=PQ[mb % 2][0:128, 0:TILE]))
                    E.emit("vector", lambda v, mb=mb, p=p: v.tensor_copy(
                        out=tail_sb[p][:, mb * 4:(mb + 1) * 4],
                        in_=PQ[mb % 2][0:128, TILE - 4:TILE]), inc=(SV, 1))
                    bank["PQ"][pq] = SV.n
                    for tap in range(4):
                        off = mb * (TILE + 4) + 1 + tap
                        if tap == 0:
                            E.emit("vector", lambda v, mb=mb, off=off: v.tensor_scalar(
                                out=conv_sb[:, mb * TILE:(mb + 1) * TILE],
                                in0=mixed_sb[:, off:off + TILE],
                                scalar1=convw_sb[:, mb * 4: mb * 4 + 1],
                                scalar2=None, op0=ALU.mult))
                        else:
                            E.emit("vector", lambda v, mb=mb, off=off, tap=tap: v.scalar_tensor_tensor(
                                out=conv_sb[:, mb * TILE:(mb + 1) * TILE],
                                in0=mixed_sb[:, off:off + TILE],
                                scalar=convw_sb[:, mb * 4 + tap: mb * 4 + tap + 1],
                                in1=conv_sb[:, mb * TILE:(mb + 1) * TILE],
                                op0=ALU.mult, op1=ALU.add),
                                inc=(SV, 1) if tap == 3 else None)
                conv_tok = SV.n
                E.wait("scalar", SV, conv_tok)
                E.emit("scalar", lambda s: s.activation(
                    out=qkv_sb[p][:], in_=conv_sb[:], func=AF.Silu), inc=(SS, 1))
                tk["qkv"][p] = SS.n
                E.wait("scalar", SV, tk["h"][p])
                E.wait("vector", ST, tk["baT"][p])
                for ts in range(2):
                    pq = ts
                    E.wait("tensor", SS, tk["z"][p])
                    E.wait("tensor", SV, bank["PQ"][pq])
                    bwait("tensor", bank["PD"][ts])
                    for ft in range(16):
                        xsl = xn_sb[p][:, ft * TILE + ts * 128: ft * TILE + (ts + 1) * 128]
                        E.emit("tensor", lambda t, x=xsl: t.ldweights(x))
                        E.emit("tensor", lambda t, x=xsl, ft=ft, ts=ts: t.matmul(
                            PQ[ts][0:128, 0:512], x, wz_sb[:, ft * 512:(ft + 1) * 512],
                            start=(ft == 0), stop=(ft == 15)))
                        E.emit("tensor", lambda t, x=xsl, ft=ft, ts=ts: t.matmul(
                            PD[ts][0:128, 0:8], x, wba_sb[:, ft * 8:(ft + 1) * 8],
                            start=(ft == 0), stop=(ft == 15)),
                            inc=(ST, 1) if ft == 15 else None)
                    E.wait("scalar", ST, ST.n)
                    E.emit("scalar", lambda s, ts=ts: s.activation(
                        out=z_sb[p][:, ts * 512:(ts + 1) * 512],
                        in_=PQ[ts][0:128, 0:512], func=AF.Silu), inc=(SS, 1))
                    bank["PQ"][pq] = 0
                    E.wait("vector", ST, ST.n)
                    E.emit("vector", lambda v, ts=ts: v.tensor_copy(
                        out=batm_sb[p][:, ts * 8:(ts + 1) * 8],
                        in_=PD[ts][0:128, 0:8]), inc=(SV, 1))
                    bank["PD"][ts] = (SV, SV.n)
                tk["z"][p] = SS.n
                tk["ba"][p] = SV.n
                tk["zmm"][p] = ST.n

            def rows_stage(j, p):
                E.wait("tensor", SV, tk["ba"][p])
                E.wait("tensor", SV, bank["PR"])
                for ts in range(2):
                    E.emit("tensor", lambda t, ts=ts: t.transpose(
                        PR[0:4, ts * 128:(ts + 1) * 128],
                        batm_sb[p][:, ts * 8: ts * 8 + 4], id128f_sb[:]))
                    E.emit("tensor", lambda t, ts=ts: t.transpose(
                        PR[0:4, 256 + ts * 128: 256 + (ts + 1) * 128],
                        batm_sb[p][:, ts * 8 + 4: ts * 8 + 8], id128f_sb[:]),
                        inc=(ST, 1) if ts == 1 else None)
                tk["baT"][p] = ST.n
                E.wait("vector", ST, ST.n)
                E.emit("vector", lambda v: v.tensor_copy(out=bb_sb[p][:], in_=PR[0:4, 0:256]))
                E.emit("vector", lambda v: v.tensor_copy(out=aa_sb[p][:], in_=PR[0:4, 256:512]),
                       inc=(SV, 1))
                bank["PR"] = SV.n
                E.wait("vector", ST, tk["cpT"][p])
                E.wait("scalar", SV, SV.n)
                E.wait("scalar", DR, tk["mrr"][p])
                E.emit("scalar", lambda s: s.activation(
                    out=scr_sb[0:4, 0:TILE], in_=bb_sb[p][:], func=AF.Tanh, scale=0.5),
                    inc=(SS, 1))
                E.wait("vector", SS, SS.n)
                E.emit("vector", lambda v: v.tensor_scalar(
                    out=beta_sb[p][:], in0=scr_sb[0:4, 0:TILE], scalar1=0.5, scalar2=0.5,
                    op0=ALU.mult, op1=ALU.add), inc=(SV, 1))
                E.emit("vector", lambda v: v.tensor_scalar(
                    out=scr_sb[0:4, 0:TILE], in0=bb_sb[p][:], scalar1=-1.0, scalar2=CLAMP,
                    op0=ALU.mult, op1=ALU.min), inc=(SV, 1))
                E.wait("scalar", SV, SV.n)
                E.emit("scalar", lambda s: s.activation(
                    out=scr_sb[0:4, 0:TILE], in_=scr_sb[0:4, 0:TILE], func=AF.Exp))
                E.emit("scalar", lambda s: s.activation(
                    out=lnbp_sb[:], in_=scr_sb[0:4, 0:TILE], func=AF.Ln, bias=1.0),
                    inc=(SS, 1))
                E.emit("vector", lambda v: v.tensor_scalar(
                    out=scr_sb[0:4, TILE:2 * TILE], in0=aa_sb[p][:], scalar1=dtb_sb[:, :],
                    scalar2=CLAMP, op0=ALU.add, op1=ALU.min), inc=(SV, 1))
                E.wait("scalar", SV, SV.n)
                E.emit("scalar", lambda s: s.activation(
                    out=scr_sb[0:4, TILE:2 * TILE], in_=scr_sb[0:4, TILE:2 * TILE],
                    func=AF.Exp))
                E.emit("scalar", lambda s: s.activation(
                    out=scr_sb[0:4, TILE:2 * TILE], in_=scr_sb[0:4, TILE:2 * TILE],
                    func=AF.Ln, bias=1.0), inc=(SS, 1))
                E.wait("vector", SS, SS.n)
                E.emit("vector", lambda v: v.tensor_scalar(
                    out=scr_sb[0:4, TILE:2 * TILE], in0=scr_sb[0:4, TILE:2 * TILE],
                    scalar1=nega_sb[:, :], scalar2=None, op0=ALU.mult), inc=(SV, 1))
                E.edge("vector", SV)
                for cc in range(4):
                    E.emit("vector", lambda v, cc=cc: v.tensor_tensor_scan(
                        out=gc_sb[p][:, cc * 64:(cc + 1) * 64],
                        data0=scr_sb[0:4, TILE + cc * 64: TILE + (cc + 1) * 64],
                        data1=scr_sb[0:4, TILE + cc * 64: TILE + (cc + 1) * 64],
                        initial=0.0, op0=ALU.add, op1=ALU.bypass))
                E.edge("vector", SV)
                E.emit("vector", lambda v: v.tensor_copy(
                    out=gl_sb[p][:],
                    in_=ap_(gc_sb[p][:], 63, [[TILE, 4], [64, 4], [0, 64]])), inc=(SV, 1))
                E.wait("vector", SS, tk["qkv"][p])
                E.wait("tensor", SS, tk["qkv"][p])
                for rnd in range(2):
                    E.wait("tensor", SV, bank["PR"])
                    for hh in range(2):
                        hqk = rnd * 2 + hh
                        E.wait("vector", ST, ST.n)
                        E.emit("vector", lambda v, hqk=hqk: v.tensor_mul(
                            out=q2b_sb[:], in0=qkv_sb[p][:, hqk * TILE:(hqk + 1) * TILE],
                            in1=qkv_sb[p][:, hqk * TILE:(hqk + 1) * TILE]), inc=(SV, 1))
                        E.wait("tensor", SV, SV.n)
                        E.emit("tensor", lambda t, hh=hh: t.matmul(
                            PR[0:1, hh * TILE:(hh + 1) * TILE],
                            ones_sb[:], q2b_sb[:],
                            start=True, stop=True), inc=(ST, 1))
                    E.wait("vector", ST, ST.n)
                    E.emit("vector", lambda v, rnd=rnd: v.tensor_copy(
                        out=l2p_sb[:, rnd * 2 * TILE:(rnd + 1) * 2 * TILE],
                        in_=PR[0:1, 0:2 * TILE]), inc=(SV, 1))
                    bank["PR"] = SV.n
                E.wait("scalar", SV, SV.n)
                E.emit("scalar", lambda s: s.activation(
                    out=l2p_sb[:], in_=l2p_sb[:], func=AF.Sqrt), inc=(SS, 1))
                E.wait("vector", SS, SS.n)
                E.emit("vector", lambda v: v.tensor_scalar(
                    out=l2p_sb[:], in0=l2p_sb[:], scalar1=1e-6, scalar2=None,
                    op0=ALU.add), inc=(SV, 1))
                E.wait("scalar", SV, SV.n)
                E.emit("scalar", lambda s: s.activation(
                    out=l2p_sb[:], in_=l2p_sb[:], func=AF.Ln), inc=(SS, 1))
                lnn_tok = SS.n
                E.wait("sync", SS, lnn_tok)
                E.wait("sync", SV, tk["rows"][p])
                for g in range(2):
                    for khx in range(2):
                        E.emit("sync", lambda s, g=g, khx=khx: s.dma_start(
                            out=l2hv_sb[p][2 * khx: 2 * khx + 2, g * TILE:(g + 1) * TILE],
                            in_=ap_(l2p_sb[:], g * 2 * TILE + khx * TILE,
                                    [[4 * TILE, 1], [0, 2], [1, TILE]])), inc=(DRL[p], 16))
                l2hv_tok = DRL[p].n
                E.wait("vector", DRL[p], l2hv_tok)
                rw = rws_sb[p]
                lnnq = l2hv_sb[p][:, 0:TILE]
                lnnk = l2hv_sb[p][:, TILE:2 * TILE]
                E.emit("vector", lambda v: v.scalar_tensor_tensor(
                    out=rw[:, 0:TILE], in0=gc_sb[p][:], scalar=-1.0, in1=lnnk,
                    op0=ALU.mult, op1=ALU.subtract))
                E.emit("vector", lambda v: v.tensor_sub(
                    out=scr_sb[0:4, 0:TILE], in0=gc_sb[p][:], in1=lnnk))
                E.emit("vector", lambda v: v.tensor_sub(
                    out=rw[:, TILE:2 * TILE], in0=scr_sb[0:4, 0:TILE], in1=lnbp_sb[:]))
                E.emit("vector", lambda v: v.tensor_sub(
                    out=scr_sb[0:4, 0:TILE], in0=gc_sb[p][:], in1=lnnq))
                E.emit("vector", lambda v: v.tensor_scalar(
                    out=rw[:, 2 * TILE:3 * TILE], in0=scr_sb[0:4, 0:TILE], scalar1=LNDKI,
                    scalar2=None, op0=ALU.add))
                E.emit("vector", lambda v: v.tensor_sub(
                    out=scr_sb[0:4, 0:TILE], in0=gl_sb[p][:], in1=gc_sb[p][:]))
                E.emit("vector", lambda v: v.tensor_sub(
                    out=rw[:, 3 * TILE:4 * TILE], in0=scr_sb[0:4, 0:TILE], in1=lnnk),
                    inc=(SV, 1))
                tk["rows"][p] = SV.n
                E.wait("sync", SV, tk["rows"][p])
                E.emit("sync", lambda s: s.dma_start(
                    out=ap_(rows_h, 2 * T_ALL + j * TILE, [[3 * T_ALL, 4], [T_ALL, 3], [1, TILE]]),
                    in_=ap_(rw[:], 0, [[4 * TILE, 4], [TILE, 3], [1, TILE]])), inc=(DR, 16))
                E.emit("sync", lambda s: s.dma_start(
                    out=ap_(rows_h, 14 * T_ALL + j * TILE, [[T_ALL, 4], [1, TILE]]),
                    in_=gl_sb[p][:]), inc=(DR, 16))
                E.wait("sync", DR, DR.n)
                E.wait("sync", SV, tk["maskv"][p])
                E.wait("sync", SV, tk["opb"][p])
                E.emit("sync", lambda s: s.dma_start(
                    out=arep_sb[p][:].rearrange("q (h t) -> q h t", t=TILE),
                    in_=ap_(rows_h, 2 * T_ALL + j * TILE, [[0, 128], [3 * T_ALL, 4], [1, TILE]])),
                    inc=(DRR[p], 16))
                E.emit("sync", lambda s: s.dma_start(
                    out=carep_sb[p][:].rearrange("q (h t) -> q h t", t=TILE),
                    in_=ap_(rows_h, 3 * T_ALL + j * TILE, [[0, 128], [3 * T_ALL, 4], [1, TILE]])),
                    inc=(DRR[p], 16))
                E.emit("sync", lambda s: s.dma_start(
                    out=srep_sb[p][:].rearrange("q (h t) -> q h t", t=TILE),
                    in_=ap_(rows_h, 4 * T_ALL + j * TILE, [[0, 128], [3 * T_ALL, 4], [1, TILE]])),
                    inc=(DRR[p], 16))
                E.emit("sync", lambda s: s.dma_start(
                    out=ap_(rows_h, 18 * T_ALL + j * 16, [[4, 4], [1, 4]]),
                    in_=ap_(gl_sb[p][:], 0, [[TILE, 4], [64, 4], [1, 1]]),
                    allow_slow_non_contiguous=True), inc=(DR, 16))
                E.wait("sync", DR, DR.n)
                E.emit("sync", lambda s: s.dma_start(
                    out=egrep_sb[p][:],
                    in_=ap_(rows_h, 18 * T_ALL + j * 16, [[0, 128], [1, 16]])),
                    inc=(DRR[p], 16))
                tk["rep"][p] = DRR[p].n
                E.wait("scalar", DRR[p], tk["rep"][p])
                E.emit("scalar", lambda s: s.activation(
                    out=egrep_sb[p][:], in_=egrep_sb[p][:], func=AF.Exp), inc=(SS, 1))
                tk["eg"][p] = SS.n

            def cp_stage(j, p):
                E.wait("tensor", SV, tk["rows"][p])
                E.wait("tensor", SV, bank["PR"])
                rw = rws_sb[p]
                srcs = [rw[:, TILE:2 * TILE], rw[:, 2 * TILE:3 * TILE],
                        rw[:, 0:TILE], beta_sb[p][:], rw[:, 3 * TILE:4 * TILE]]
                slots = [0, 1, 2, 3, 6]
                nt = 0
                for ts in range(2):
                    for b_i, src in enumerate(srcs):
                        slot = slots[b_i]
                        nt += 1
                        E.emit("tensor", lambda t, ts=ts, slot=slot, src=src: t.transpose(
                            PR[0:128, ts * 32 + slot * 4: ts * 32 + slot * 4 + 4],
                            src[:, ts * 128:(ts + 1) * 128], id128f_sb[0:4, 0:4]),
                            inc=(ST, 1) if nt == 10 else None)
                tk["cpT"][p] = ST.n
                E.wait("vector", ST, ST.n)
                E.wait("vector", SV, tk["opb"][p])
                E.emit("vector", lambda v: v.tensor_copy(
                    out=ap_(cp_sb[p][:], 0, [[64, 128], [32, 2], [1, 16]]),
                    in_=ap_(PR[:], 0, [[512, 128], [32, 2], [1, 16]])))
                E.emit("vector", lambda v: v.tensor_copy(
                    out=ap_(cp_sb[p][:], 24, [[64, 128], [32, 2], [1, 4]]),
                    in_=ap_(PR[:], 24, [[512, 128], [32, 2], [1, 4]])), inc=(SV, 1))
                bank["PR"] = SV.n
                E.wait("scalar", SV, SV.n)
                for ts in range(2):
                    for dst_b, src_b in [(4, 0), (5, 1), (6, 6)]:
                        E.emit("scalar", lambda s, ts=ts, dst_b=dst_b, src_b=src_b: s.activation(
                            out=cp_sb[p][:, ts * 32 + dst_b * 4: ts * 32 + dst_b * 4 + 4],
                            in_=cp_sb[p][:, ts * 32 + src_b * 4: ts * 32 + src_b * 4 + 4],
                            func=AF.Exp), inc=(SS, 1))
                tk["cp"][p] = SS.n

            def tm_stage(j, p):
                E.wait("tensor", SS, tk["qkv"][p])
                E.wait("scalar", SV, tk["opb"][p])
                for ts in range(2):
                    bwait("tensor", bank["PT"])
                    for kh in range(2):
                        E.emit("tensor", lambda t, ts=ts, kh=kh: t.transpose(
                            PT[0:128, kh * 128:(kh + 1) * 128],
                            qkv_sb[p][:, (2 + kh) * TILE + ts * 128: (2 + kh) * TILE + (ts + 1) * 128],
                            id128_sb[:]), inc=(ST, 1) if kh == 1 else None)
                    E.wait("scalar", ST, ST.n)
                    E.emit("scalar", lambda s, ts=ts: s.activation(
                        out=ktm_sb[p][:, ts * 256:(ts + 1) * 256],
                        in_=PT[0:128, 0:256], func=AF.Copy), inc=(SS, 1))
                    bank["PT"] = (SS, SS.n)
                    bwait("tensor", bank["PT"])
                    for hv in range(4):
                        E.emit("tensor", lambda t, ts=ts, hv=hv: t.transpose(
                            PT[0:128, hv * 128:(hv + 1) * 128],
                            qkv_sb[p][:, (4 + hv) * TILE + ts * 128: (4 + hv) * TILE + (ts + 1) * 128],
                            id128_sb[:]), inc=(ST, 1) if hv == 3 else None)
                    E.wait("scalar", ST, ST.n)
                    E.emit("scalar", lambda s, ts=ts: s.activation(
                        out=vtm_sb[p][:, ts * 512:(ts + 1) * 512],
                        in_=PT[0:128, 0:512], func=AF.Copy), inc=(SS, 1))
                    bank["PT"] = (SS, SS.n)
                tk["tm"][p] = SS.n

            def delta_parts(j, p, cc):
                dt = {}
                ts = cc // 2
                q0 = 64 * (cc % 2)
                pdp = cc % 2
                tok0 = cc * 64

                def kblk(kh):
                    return qkv_sb[p][:, (2 + kh) * TILE + tok0: (2 + kh) * TILE + tok0 + 64]

                def qblk(kh):
                    return qkv_sb[p][:, kh * TILE + tok0: kh * TILE + tok0 + 64]

                def hv4(buf, w, lo=None):
                    lo = q0 if lo is None else lo
                    return ap_(buf[:], lo * (4 * w), [[4 * w, 64], [w, 4], [1, w]])

                def cpc(b_i, w):
                    return ap_(cp_sb[p][:], q0 * 64 + ts * 32 + b_i * 4,
                               [[64, 64], [1, 4], [0, w]])

                def pre():
                 E.wait("tensor", SS, tk["qkv"][p])
                 bwait("tensor", bank["PD"][pdp])
                 for kh in range(2):
                     E.emit("tensor", lambda t, kh=kh: t.ldweights(kblk(kh), tile_position=(0, q0)))
                     E.emit("tensor", lambda t, kh=kh: t.matmul(
                         PD[pdp][q0:q0 + 64, kh * 64:(kh + 1) * 64], kblk(kh), kblk(kh),
                         start=True, stop=True, tile_position=(0, q0)))
                     E.emit("tensor", lambda t, kh=kh: t.matmul(
                         PD[pdp][q0:q0 + 64, 128 + kh * 64: 128 + (kh + 1) * 64],
                         kblk(kh), qblk(kh),
                         start=True, stop=True, tile_position=(0, q0)),
                         inc=(ST, 1) if kh == 1 else None)
                 gs_tok = ST.n

                 E.wait("vector", DRR[p], tk["rep"][p])
                 E.wait("vector", SS, tk["cp"][p])
                 for mk, rep, colb, tmpl in [
                         (mka_sb, arep_sb[p], 0, tlo_sb),
                         (mkc_sb, carep_sb[p], 2, thi_sb),
                         (mks_sb, srep_sb[p], 2, thii_sb)]:
                     repv = ap_(rep[:], q0 * (4 * TILE) + tok0,
                                [[4 * TILE, 64], [TILE, 4], [1, 64]])
                     E.emit("vector", lambda v, mk=mk, repv=repv, colb=colb: v.tensor_tensor(
                         out=hv4(mk, 64), in0=repv, in1=cpc(colb, 64), op=ALU.add))
                     E.emit("vector", lambda v, mk=mk: v.tensor_scalar(
                         out=mk[q0:q0 + 64, :], in0=mk[q0:q0 + 64, :], scalar1=CLAMP,
                         scalar2=None, op0=ALU.min), inc=(SV, 1))
                     E.wait("scalar", SV, SV.n)
                     E.emit("scalar", lambda s, mk=mk: s.activation(
                         out=mk[q0:q0 + 64, :], in_=mk[q0:q0 + 64, :], func=AF.Exp),
                         inc=(SS, 1))
                     E.wait("vector", SS, SS.n)
                     tmv = ap_(tmpl[:], q0 * 64, [[64, 64], [0, 4], [1, 64]])
                     E.emit("vector", lambda v, mk=mk, tmv=tmv: v.tensor_tensor(
                         out=hv4(mk, 64), in0=hv4(mk, 64), in1=tmv, op=ALU.mult),
                         inc=(SV, 1))
                 tk["maskv"][p] = SV.n

                 E.wait("vector", ST, gs_tok)

                 def khpair(buf, kh, w=64):
                     return ap_(buf[:], q0 * (4 * w) + kh * 2 * w,
                                [[4 * w, 64], [w, 2], [1, w]])

                 for kh in range(2):
                     gd = ap_(PD[pdp][:], q0 * 512 + kh * 64, [[512, 64], [0, 2], [1, 64]])
                     sd = ap_(PD[pdp][:], q0 * 512 + 128 + kh * 64, [[512, 64], [0, 2], [1, 64]])
                     E.emit("vector", lambda v, kh=kh, gd=gd: v.tensor_tensor(
                         out=khpair(ra_sb, kh), in0=gd, in1=khpair(mka_sb, kh), op=ALU.mult))
                     E.emit("vector", lambda v, kh=kh, gd=gd: v.tensor_tensor(
                         out=khpair(ca_sb, kh), in0=gd, in1=khpair(mkc_sb, kh), op=ALU.mult))
                     E.emit("vector", lambda v, kh=kh, sd=sd: v.tensor_tensor(
                         out=khpair(sc_sb, kh), in0=sd, in1=khpair(mks_sb, kh), op=ALU.mult),
                         inc=(SV, 1) if kh == 1 else None)
                 bank["PD"][pdp] = (SV, SV.n)
                 racasc_tok = SV.n

                 def lvlR(lvl, hv):
                     if lvl < 0:
                         return ap_(ra_sb[:], q0 * 256 + hv * 64, [[256, 64], [1, 64]])
                     return ap_(pw_sb[:], q0 * 2048 + lvl * 512 + hv * 128, [[2048, 64], [1, 64]])

                 def lvlC(lvl, hv):
                     if lvl < 0:
                         return ap_(ca_sb[:], q0 * 256 + hv * 64, [[256, 64], [1, 64]])
                     return ap_(pw_sb[:], q0 * 2048 + lvl * 512 + hv * 128 + 64, [[2048, 64], [1, 64]])

                 E.wait("tensor", SV, racasc_tok)
                 for lvl in range(4):
                     lvl_bank = 1 - pdp if lvl % 2 == 0 else pdp
                     bwait("tensor", bank["PD"][lvl_bank])
                     if lvl > 0:
                         E.wait("tensor", SS, SS.n)
                     for hv in range(4):
                         lC = lvlC(lvl - 1, hv)
                         lR = lvlR(lvl - 1, hv)
                         E.emit("tensor", lambda t, lC=lC: t.ldweights(lC, tile_position=(q0, q0)))
                         E.emit("tensor", lambda t, lC=lC, lR=lR, hv=hv, lvl_bank=lvl_bank: t.matmul(
                             PD[lvl_bank][q0:q0 + 64, hv * 128: hv * 128 + 64], lC, lR,
                             start=True, stop=True, tile_position=(q0, q0)))
                         E.emit("tensor", lambda t, lR=lR: t.ldweights(lR, tile_position=(q0, q0)))
                         E.emit("tensor", lambda t, lC=lC, lR=lR, hv=hv, lvl_bank=lvl_bank: t.matmul(
                             PD[lvl_bank][q0:q0 + 64, hv * 128 + 64: hv * 128 + 128], lR, lC,
                             start=True, stop=True, tile_position=(q0, q0)),
                             inc=(ST, 1) if hv == 3 else None)
                     E.wait("scalar", ST, ST.n)
                     E.emit("scalar", lambda s, lvl=lvl, lvl_bank=lvl_bank: s.activation(
                         out=pw_sb[q0:q0 + 64, lvl * 512:(lvl + 1) * 512],
                         in_=PD[lvl_bank][q0:q0 + 64, 0:512], func=AF.Copy), inc=(SS, 1))
                     bank["PD"][lvl_bank] = (SS, SS.n)

                 def pwR(lvl, hv):
                     return lvlR(lvl, hv)

                 def pwC(lvl, hv):
                     return lvlC(lvl, hv)

                 bwait("tensor", bank["PD"][pdp])
                 E.wait("tensor", SS, SS.n)
                 for hv in range(4):
                     E.emit("tensor", lambda t, hv=hv: t.ldweights(pwC(3, hv), tile_position=(q0, q0)))
                     E.emit("tensor", lambda t, hv=hv: t.matmul(
                         PD[pdp][q0:q0 + 64, hv * 64:(hv + 1) * 64], pwC(3, hv), pwR(3, hv),
                         start=True, stop=True, tile_position=(q0, q0)),
                         inc=(ST, 1) if hv == 3 else None)
                 E.wait("scalar", ST, ST.n)
                 E.emit("scalar", lambda s: s.activation(
                     out=r32_sb[q0:q0 + 64, :], in_=PD[pdp][q0:q0 + 64, 0:256],
                     func=AF.Copy), inc=(SS, 1))
                 bank["PD"][pdp] = (SS, SS.n)

                 E.wait("vector", SS, SS.n)
                 eyev = ap_(eye2_sb[:], q0 * 64, [[64, 64], [0, 4], [1, 64]])
                 E.emit("vector", lambda v: v.tensor_tensor(
                     out=hv4(chn_sb, 64), in0=hv4(ca_sb, 64), in1=eyev, op=ALU.add),
                     inc=(SV, 1))
                 for step in range(5):
                     cb = step % 2
                     E.wait("tensor", SV, SV.n)
                     bwait("tensor", bank["PD"][cb])
                     for hv in range(4):
                         if step < 4:
                             lhs = ap_(pw_sb[:], q0 * 2048 + step * 512 + hv * 128,
                                       [[2048, 64], [1, 64]])
                         else:
                             lhs = ap_(r32_sb[:], q0 * 256 + hv * 64, [[256, 64], [1, 64]])
                         rhs = ap_(chn_sb[:], q0 * 256 + hv * 64, [[256, 64], [1, 64]])
                         E.emit("tensor", lambda t, lhs=lhs: t.ldweights(lhs, tile_position=(q0, q0)))
                         E.emit("tensor", lambda t, lhs=lhs, rhs=rhs, hv=hv, cb=cb: t.matmul(
                             PD[cb][q0:q0 + 64, hv * 64:(hv + 1) * 64], lhs, rhs,
                             start=True, stop=True, tile_position=(q0, q0)),
                             inc=(ST, 1) if hv == 3 else None)
                     E.wait("vector", ST, ST.n)
                     E.emit("vector", lambda v, cb=cb: v.scalar_tensor_tensor(
                         out=chn_sb[q0:q0 + 64, :], in0=PD[cb][q0:q0 + 64, 0:256],
                         scalar=0.0, in1=chn_sb[q0:q0 + 64, :],
                         op0=ALU.bypass, op1=ALU.add), inc=(SV, 1))
                     bank["PD"][cb] = (SV, SV.n)
                 chn_tok = SV.n

                 E.wait("vector", SS, tk["tm"][p])
                 vtmv = ap_(vtm_sb[p][:], q0 * 1024 + ts * 512, [[1024, 64], [128, 4], [1, 128]])
                 E.emit("vector", lambda v: v.tensor_tensor(
                     out=hv4(vb_sb, 128), in0=vtmv, in1=cpc(3, 128), op=ALU.mult))

                 def cpc2(b_i, kh, w):
                     return ap_(cp_sb[p][:], q0 * 64 + ts * 32 + b_i * 4 + kh * 2,
                                [[64, 64], [1, 2], [0, w]])

                 for kh in range(2):
                     ktd = ap_(ktm_sb[p][:], q0 * 512 + ts * 256 + kh * 128,
                               [[512, 64], [0, 2], [1, 128]])
                     kep = ap_(ke_sb[:], q0 * 512 + kh * 2 * 128, [[512, 64], [128, 2], [1, 128]])
                     k2p = ap_(k2t_sb[:], q0 * 512 + kh * 2 * 128, [[512, 64], [128, 2], [1, 128]])
                     E.emit("vector", lambda v, kh=kh, ktd=ktd, kep=kep: v.scalar_tensor_tensor(
                         out=kep, in0=ktd, scalar=-1.0, in1=cpc2(4, kh, 128),
                         op0=ALU.mult, op1=ALU.mult))
                     E.emit("vector", lambda v, kh=kh, ktd=ktd, k2p=k2p: v.tensor_tensor(
                         out=k2p, in0=ktd, in1=cpc2(6, kh, 128), op=ALU.mult),
                         inc=(SV, 1) if kh == 1 else None)
                 opb_tok = SV.n
                 tk["opb"][p] = SV.n

                 E.wait("tensor", SV, opb_tok)
                 bwait("tensor", bank["PD"][pdp])
                 for hv in range(4):
                     ksl = ap_(ke_sb[:], q0 * 512 + hv * 128, [[512, 64], [1, 128]])
                     tsl = ap_(chn_sb[:], q0 * 256 + hv * 64, [[256, 64], [1, 64]])
                     E.emit("tensor", lambda t, ksl=ksl: t.ldweights(ksl, tile_position=(q0, 0)))
                     E.emit("tensor", lambda t, ksl=ksl, tsl=tsl, hv=hv: t.matmul(
                         PD[pdp][0:128, hv * 64:(hv + 1) * 64], ksl, tsl,
                         start=True, stop=True, tile_position=(q0, 0)),
                         inc=(ST, 1) if hv == 3 else None)
                 E.wait("scalar", ST, ST.n)
                 E.emit("scalar", lambda s: s.activation(
                     out=kcd_sb[:, 0:256], in_=PD[pdp][0:128, 0:256], func=AF.Copy),
                     inc=(SS, 1))
                 bank["PD"][pdp] = (SS, SS.n)
                 dt["kcd_tok"] = SS.n

                def spine():
                 kcd_tok = dt["kcd_tok"]
                 E.wait("tensor", SS, kcd_tok)
                 E.wait("tensor", SV, bank["PV"])
                 E.wait("tensor", SV, state_tok[p])
                 E.wait("tensor", SS, statebf_tok[p])
                 for hv in range(4):
                     tsl = ap_(chn_sb[:], q0 * 256 + hv * 64, [[256, 64], [1, 64]])
                     vsl = ap_(vb_sb[:], q0 * 512 + hv * 128, [[512, 64], [1, 128]])
                     E.emit("tensor", lambda t, tsl=tsl: t.ldweights(tsl, tile_position=(q0, q0)))
                     E.emit("tensor", lambda t, tsl=tsl, vsl=vsl, hv=hv: t.matmul(
                         PV[q0:q0 + 64, hv * 128:(hv + 1) * 128], tsl, vsl,
                         start=True, stop=False, tile_position=(q0, q0)))
                     ksl = kcd_sb[:, hv * 64:(hv + 1) * 64]
                     E.emit("tensor", lambda t, ksl=ksl: t.ldweights(ksl, tile_position=(0, q0)))
                     E.emit("tensor", lambda t, ksl=ksl, hv=hv: t.matmul(
                         PV[q0:q0 + 64, hv * 128:(hv + 1) * 128], ksl,
                         stbf_sb[p][:, hv * 128:(hv + 1) * 128],
                         start=False, stop=True, tile_position=(0, q0)),
                         inc=(ST, 1) if hv == 3 else None)
                 E.wait("vector", ST, ST.n)
                 E.emit("vector", lambda v: v.tensor_copy(
                     out=vn_sb[q0:q0 + 64, :], in_=PV[q0:q0 + 64, 0:512]), inc=(SV, 1))
                 bank["PV"] = SV.n
                 vn_tok = SV.n
                 bwait("tensor", bank["PD"][pdp])
                 for hv in range(4):
                     qb = qblk(hv // 2)
                     E.emit("tensor", lambda t, qb=qb: t.ldweights(qb, tile_position=(0, q0)))
                     E.emit("tensor", lambda t, qb=qb, hv=hv: t.matmul(
                         PD[pdp][q0:q0 + 64, hv * 128:(hv + 1) * 128], qb,
                         stbf_sb[p][:, hv * 128:(hv + 1) * 128],
                         start=True, stop=True, tile_position=(0, q0)),
                         inc=(ST, 1) if hv == 3 else None)
                 inter_tok = ST.n
                 tk["deltaq"][p] = ST.n
                 pint_tok = 0
                 if "pint" in dbg and j == ntile - 1:
                     E.wait("scalar", ST, inter_tok)
                     E.emit("scalar", lambda s: s.activation(
                         out=pint_sb[q0:q0 + 64, :], in_=PD[pdp][q0:q0 + 64, 0:512],
                         func=AF.Copy), inc=(SS, 1))
                     pint_tok = SS.n
                 E.wait("tensor", SV, vn_tok)
                 bwait("tensor", bank["PO"])
                 for hv in range(4):
                     ssl = ap_(sc_sb[:], q0 * 256 + hv * 64, [[256, 64], [1, 64]])
                     vsl = ap_(vn_sb[:], q0 * 512 + hv * 128, [[512, 64], [1, 128]])
                     E.emit("tensor", lambda t, ssl=ssl: t.ldweights(ssl, tile_position=(q0, q0)))
                     E.emit("tensor", lambda t, ssl=ssl, vsl=vsl, hv=hv: t.matmul(
                         PO[q0:q0 + 64, hv * 128:(hv + 1) * 128], ssl, vsl,
                         start=True, stop=True, tile_position=(q0, q0)),
                         inc=(ST, 1) if hv == 3 else None)
                 attnv_tok = ST.n
                 patt_tok = 0
                 if "patt" in dbg and j == ntile - 1:
                     E.wait("scalar", ST, attnv_tok)
                     E.emit("scalar", lambda s: s.activation(
                         out=patt_sb[q0:q0 + 64, :], in_=PO[q0:q0 + 64, 0:512],
                         func=AF.Copy), inc=(SS, 1))
                     patt_tok = SS.n
                 E.wait("vector", ST, inter_tok)
                 pdv = ap_(PD[pdp][:], q0 * 512, [[512, 64], [128, 4], [1, 128]])
                 E.emit("vector", lambda v: v.tensor_tensor(
                     out=hv4(o2_sb, 128), in0=pdv, in1=cpc(5, 128), op=ALU.mult),
                     inc=(SV, 1))
                 bank["PD"][pdp] = ([(SV, SV.n), (SS, pint_tok)]
                                    if pint_tok else (SV, SV.n))
                 E.wait("vector", ST, attnv_tok)
                 E.emit("vector", lambda v: v.tensor_tensor(
                     out=o2_sb[q0:q0 + 64, :], in0=o2_sb[q0:q0 + 64, :],
                     in1=PO[q0:q0 + 64, 0:512], op=ALU.add), inc=(SV, 1))
                 bank["PO"] = ([(SV, SV.n), (SS, patt_tok)]
                               if patt_tok else (SV, SV.n))
                 oi_tok = SV.n
                 dt["oi_tok"] = oi_tok
                 if "o2pre" in dbg and j == ntile - 1:
                     E.wait("sync", SV, oi_tok)
                     E.emit("sync", lambda s: s.dma_start(
                         out=ap_(dbg["o2pre"], q0 * 512, [[512, 64], [1, 512]]),
                         in_=o2_sb[q0:q0 + 64, :]), inc=(DR, 16))
                     dt["o2pre_tok"] = DR.n
                 E.wait("tensor", SV, vn_tok)
                 E.wait("tensor", SV, bank["PV"])
                 for hv in range(4):
                     k2sl = ap_(k2t_sb[:], q0 * 512 + hv * 128, [[512, 64], [1, 128]])
                     vsl = ap_(vn_sb[:], q0 * 512 + hv * 128, [[512, 64], [1, 128]])
                     E.emit("tensor", lambda t, k2sl=k2sl: t.ldweights(k2sl, tile_position=(q0, 0)))
                     E.emit("tensor", lambda t, k2sl=k2sl, vsl=vsl, hv=hv: t.matmul(
                         PV[0:128, hv * 128:(hv + 1) * 128], k2sl, vsl,
                         start=True, stop=True, tile_position=(q0, 0)),
                         inc=(ST, 1) if hv == 3 else None)
                 ds_tok = ST.n
                 E.wait("vector", SS, tk["eg"][p])
                 E.wait("vector", ST, ds_tok)
                 for hv in range(4):
                     E.emit("vector", lambda v, hv=hv: v.tensor_scalar(
                         out=st_sb[p][:, hv * 128:(hv + 1) * 128],
                         in0=st_sb[p][:, hv * 128:(hv + 1) * 128],
                         scalar1=egrep_sb[p][:, cc + 4 * hv: cc + 4 * hv + 1],
                         scalar2=None, op0=ALU.mult))
                 E.edge("vector", SV)
                 E.emit("vector", lambda v: v.tensor_add(
                     out=st_sb[p][:], in0=st_sb[p][:], in1=PV[0:128, 0:512]),
                     inc=(SV, 1))
                 bank["PV"] = SV.n
                 E.wait("scalar", SV, SV.n)
                 E.emit("scalar", lambda s: s.activation(
                     out=stbf_sb[p][:], in_=st_sb[p][:], func=AF.Copy), inc=(SS, 1))
                 state_tok[p] = SV.n
                 statebf_tok[p] = SS.n

                def tail():
                 oi_tok = dt["oi_tok"]
                 E.wait("scalar", SV, oi_tok)
                 for rnd in range(2):
                     if rnd == 1:
                         E.wait("scalar", SV, SV.n)
                     E.emit("scalar", lambda s, rnd=rnd: s.activation(
                         out=rsq32_sb[q0:q0 + 64, :],
                         in_=o2_sb[q0:q0 + 64, rnd * 256:(rnd + 1) * 256],
                         func=AF.Square), inc=(SS, 1))
                     E.wait("vector", SS, SS.n)
                     for hh in range(2):
                         E.emit("vector", lambda v, rnd=rnd, hh=hh: v.reduce_sum(
                             out=rms_sb[q0:q0 + 64, rnd * 2 + hh: rnd * 2 + hh + 1],
                             in_=rsq32_sb[q0:q0 + 64, hh * 128:(hh + 1) * 128],
                             axis=mybir.AxisListType.X),
                             inc=(SV, 1) if hh == 1 else None)
                 E.edge("vector", SV)
                 E.emit("vector", lambda v: v.tensor_scalar(
                     out=rms_sb[q0:q0 + 64, 4:8], in0=rms_sb[q0:q0 + 64, 0:4],
                     scalar1=1.0 / 128.0, scalar2=1e-6, op0=ALU.mult, op1=ALU.add),
                     inc=(SV, 1))
                 if "rmsv" in dbg:
                     E.emit("vector", lambda v: v.tensor_copy(
                         out=rms_sb[q0:q0 + 64, 8:12], in_=rms_sb[q0:q0 + 64, 4:8]))
                 E.wait("scalar", SV, SV.n)
                 E.emit("scalar", lambda s: s.activation(
                     out=rms_sb[q0:q0 + 64, 4:8], in_=rms_sb[q0:q0 + 64, 4:8],
                     func=AF.Sqrt), inc=(SS, 1))
                 E.wait("vector", SS, SS.n)
                 if "rmsv" in dbg:
                     E.emit("vector", lambda v: v.tensor_copy(
                         out=rms_sb[q0:q0 + 64, 12:16], in_=rms_sb[q0:q0 + 64, 4:8]))
                 E.emit("vector", lambda v: v.reciprocal(
                     out=rms_sb[q0:q0 + 64, 4:8], in_=rms_sb[q0:q0 + 64, 4:8]))
                 E.edge("vector", SV)
                 if "o2pre" in dbg and j == ntile - 1:
                     E.wait("vector", DR, dt["o2pre_tok"])
                 rsqv = ap_(rms_sb[:], q0 * 20 + 4, [[20, 64], [1, 4], [0, 128]])
                 E.emit("vector", lambda v: v.tensor_tensor(
                     out=hv4(o2_sb, 128), in0=hv4(o2_sb, 128), in1=rsqv, op=ALU.mult))
                 E.wait("vector", SS, tk["z"][p])
                 E.wait("vector", DHS[p], tk["hst"][p])
                 E.emit("vector", lambda v: v.tensor_tensor(
                     out=hbuf_sb[p][q0:q0 + 64, ts * 512:(ts + 1) * 512],
                     in0=o2_sb[q0:q0 + 64, :],
                     in1=z_sb[p][q0:q0 + 64, ts * 512:(ts + 1) * 512], op=ALU.mult),
                     inc=(SV, 1))
                 tk["h"][p] = SV.n
                return pre, spine, tail

            half = ntile // 2 if ntile >= 2 else 1
            if ntile >= 2:
                order = []
                for k in range(half):
                    order.append(k)
                    order.append(half + k)
                order += list(range(2 * half, ntile))
            else:
                order = list(range(ntile))
            def h_store(jj, p, ts):
                E.wait("gpsimd", SV, tk["h"][p])
                E.emit("gpsimd", lambda g, jj=jj, ts=ts, p=p: g.dma_start(
                    out=h_src[jj * TILE + ts * 128: jj * TILE + (ts + 1) * 128, :],
                    in_=hbuf_sb[p][:, ts * 512:(ts + 1) * 512]), inc=(DHS[p], 16))
                tk["hst"][p] = DHS[p].n

            prev = None
            for s, jj in enumerate(order):
                p = s % 2
                ln_stage(jj, p)
                xn_stage(jj, p)
                if s + 2 < len(order):
                    E.wait("gpsimd", SV, tk["xn"][p])
                    load_x(order[s + 2], p)
                    x_tok[p] = SGX[p].n
                fe_stage(jj, p)
                rows_stage(jj, p)
                cp_stage(jj, p)
                tm_stage(jj, p)
                parts = [delta_parts(jj, p, cc) for cc in range(4)]
                parts[0][0]()
                parts[0][1]()
                parts[1][0]()
                parts[0][2]()
                parts[1][1]()
                parts[2][0]()
                parts[1][2]()
                h_store(jj, p, 0)
                parts[2][1]()
                parts[3][0]()
                parts[2][2]()
                parts[3][1]()
                parts[3][2]()
                h_store(jj, p, 1)
                prev = (jj, p)

            E.wait("gpsimd", DHS[0], DHS[0].n)
            E.wait("gpsimd", DHS[1], DHS[1].n)
            if "h_src" in dbg:
                E.wait("gpsimd", DH, DH.n)
                E.emit("gpsimd", lambda g: g.dma_start(
                    out=ap_(dbg["h_src"], 0, [[512, ntile * TILE], [1, 512]]),
                    in_=h_src[0:ntile * TILE, :]), inc=(DH, 16))
            if "qkv0" in dbg:
                E.emit("gpsimd", lambda g: g.dma_start(
                    out=dbg["qkv0"], in_=qkv_sb[(ntile - 1) % 2][:]), inc=(DH, 16))
            lpar = (ntile - 1) % 2
            for nm, src in [("xn0", lambda: xn_sb[lpar][:]),
                            ("z0", lambda: z_sb[lpar][:]),
                            ("cp0", lambda: cp_sb[lpar][:]),
                            ("stf", lambda: st_sb[(ntile - 1) % 2][:]),
                            ("o20", lambda: o2_sb[:]),
                            ("vn0", lambda: vn_sb[:]),
                            ("chn0", lambda: chn_sb[:]),
                            ("kcd0", lambda: kcd_sb[:]),
                            ("pint", lambda: pint_sb[:]),
                            ("patt", lambda: patt_sb[:]),
                            ("rmsv", lambda: rms_sb[:])]:
                if nm in dbg:
                    E.wait("gpsimd", SV, SV.n)
                    E.wait("gpsimd", SS, SS.n)
                    E.wait("gpsimd", ST, ST.n)
                    E.emit("gpsimd", lambda g, nm=nm, src=src: g.dma_start(
                        out=dbg[nm], in_=src()), inc=(DH, 16))
            if "rows" in dbg:
                E.wait("gpsimd", DR, DR.n)
                E.wait("gpsimd", SV, SV.n)
                E.emit("gpsimd", lambda g: g.dma_start(
                    out=ap_(dbg["rows"], 0, [[T_ALL, 20], [1, ntile * TILE]]),
                    in_=ap_(rows_h, 0, [[T_ALL, 20], [1, ntile * TILE]])), inc=(DH, 16))

            E.wait("gpsimd", DH, DH.n)
            if phase2:
                E.emit("gpsimd", lambda g: g.collective_compute(
                    "AllGather", ALU.bypass,
                    ins=[h_src[:, :]], outs=[h_all[:, :, :]],
                    replica_groups=[list(range(NCORE))]), inc=(CC, 1))

            p1_join = dict(t=ST.n, v=SV.n, s=SS.n, dx=DX.n, dr=DR.n, dh=DH.n, g=SG.n,
                           sgx0=SGX[0].n, sgx1=SGX[1].n, sgw=SGW.n,
                           dhs0=DHS[0].n, dhs1=DHS[1].n,
                           drm0=DRM[0].n, drm1=DRM[1].n, drl0=DRL[0].n,
                           drl1=DRL[1].n, drr0=DRR[0].n, drr1=DRR[1].n)

        if not phase2:
            E.wait("gpsimd", DH, DH.n)
            E.emit("gpsimd", lambda g: g.memset(sel_sb[:], 0.0))
        else:
            p2 = ExitStack()
            with p2:
                SB2 = lambda n, sh, dt: p2.enter_context(nc.sbuf_tensor(n, sh, dt))
                xb2_sb = SB2("xb2", [128, 16 * 256], BF16)
                x2b2_sb = SB2("x2b2", [128, 256], BF16)
                xn2_sb = SB2("xn2", [128, 16 * 512], BF16)
                mr2_sb = SB2("mr2", [1, 2 * 256], F32)
                scr2_sb = SB2("scr2", [4, 2 * 256], F32)
                mrep2_sb = SB2("mrep2", [128, 256], F32)
                rrep2_sb = SB2("rrep2", [128, 256], F32)
                csrep_sb = SB2("csrep", [128, 512], F32)
                cirep_sb = SB2("cirep", [128, 512], F32)
                wg_sb = [SB2(f"wg{i}", [128, 16 * 128], BF16) for i in range(2)]
                wu_sb = [SB2(f"wu{i}", [128, 16 * 128], BF16) for i in range(2)]
                wd_sb = [SB2("wd0", [128, 64 * 128], BF16)]
                wo_sb = [SB2(f"wo{i}", [128, 4 * 128], BF16) for i in range(2)]
                hfm_sb = [SB2(f"hfm{i}", [128, 1024], BF16) for i in range(2)]
                ii_sb = SB2("ii", [128, 64 * 512], BF16)
                gfin_sb = SB2("gfin", [128, 16 * 512], BF16)
                sgbuf_sb = SB2("sgbuf", [128, 512], F32)
                asm_sb = SB2("asm", [128, 4 * 2048], F32)
                xtm_sb = SB2("xtm", [128, 2048], F32)
                big1_sb = SB2("big1", [128, 512], F32)
                big2_sb = SB2("big2", [128, 512], F32)
                sil_sb = SB2("sil", [128, 512], F32)

                for eng in ("sync", "scalar", "vector", "tensor", "gpsimd"):
                    for snm, sobj in [("t", ST), ("v", SV), ("s", SS), ("dr", DR),
                                      ("dh", DH), ("dx", DX), ("g", SG),
                                      ("sgx0", SGX[0]), ("sgx1", SGX[1]), ("sgw", SGW),
                                      ("dhs0", DHS[0]), ("dhs1", DHS[1]),
                                      ("drm0", DRM[0]), ("drm1", DRM[1]),
                                      ("drl0", DRL[0]), ("drl1", DRL[1]),
                                      ("drr0", DRR[0]), ("drr1", DRR[1])]:
                        E.wait(eng, sobj, p1_join[snm])

                E.emit("gpsimd", lambda g: g.dma_start(
                    out=xb2_sb[:].rearrange("p (a t) -> p a t", t=256),
                    in_=x2_fm_d.rearrange("(a p) t -> p a t", p=128)[:, :, 0:256]),
                    inc=(SG, 16))
                xb2_ld_tok = SG.n
                xb2_use_tok = 0

                for wk in range(8):
                    E.emit("gpsimd", lambda g, wk=wk: g.dma_start(
                        out=ap_(wo16_d, wk * 512 * HID, [[HID, 512], [1, HID]]),
                        in_=ap_(WoutT_d, wk * 512 * HID, [[HID, 512], [1, HID]])),
                        inc=(SG, 16))
                wo16_tok = SG.n

                E.wait("gpsimd", CC, CC.n)
                E.wait("sync", CC, CC.n)
                IIW = 64 * 512
                cbv = lambda pb, off, dims: ap_(ii_sb[:], pb * 4096 + off, dims)
                accv = lambda pb: ap_(ii_sb[:], 8192 + pb * 512, [[IIW, 128], [1, 512]])
                ac2v = lambda pb: ap_(ii_sb[:], 9216 + pb * 512, [[IIW, 128], [1, 512]])
                cb_tok = [0, 0]
                acc_tok = [0, 0]
                for i in range(NCORE * 8):
                    rr, tb = i // 8, i % 8
                    pb = i % 2
                    if cb_tok[pb]:
                        E.wait("gpsimd", SV, cb_tok[pb])
                    E.emit("gpsimd", lambda g, rr=rr, tb=tb, pb=pb: g.dma_start(
                        out=cbv(pb, 0, [[IIW, 128], [512, 8], [1, 512]]),
                        in_=ap_(h_all, rr * T_ALL * 512 + tb * 128 * 512,
                                [[512, 128], [TOK_PC * 512, 8], [1, 512]])),
                        inc=(SG, 16))
                    E.wait("vector", SG, SG.n)
                    if acc_tok[pb]:
                        E.wait("vector", DH, acc_tok[pb])
                    for cand in range(NCORE):
                        dst = accv(pb) if cand % 2 == 0 else ac2v(pb)
                        srcv = cbv(pb, cand * 512, [[IIW, 128], [1, 512]])
                        if cand < 2:
                            E.emit("vector", lambda v, cand=cand, dst=dst, srcv=srcv: v.tensor_scalar(
                                out=dst, in0=srcv,
                                scalar1=sel_sb[:, cand:cand + 1], scalar2=None,
                                op0=ALU.mult))
                        else:
                            E.emit("vector", lambda v, cand=cand, dst=dst, srcv=srcv: v.scalar_tensor_tensor(
                                out=dst, in0=srcv,
                                scalar=sel_sb[:, cand:cand + 1], in1=dst,
                                op0=ALU.mult, op1=ALU.add))
                    E.edge("vector", SV)
                    E.emit("vector", lambda v, pb=pb: v.tensor_add(
                        out=accv(pb), in0=accv(pb), in1=ac2v(pb)), inc=(SV, 1))
                    cb_tok[pb] = SV.n
                    E.wait("gpsimd", SV, SV.n)
                    E.emit("gpsimd", lambda g, rr=rr, tb=tb, pb=pb: g.dma_start(
                        out=ap_(h_mine, tb * 128 * 4096 + rr * 512, [[4096, 128], [1, 512]]),
                        in_=accv(pb)), inc=(DH, 16))
                    acc_tok[pb] = DH.n
                hsel_tok = DH.n
                ht_tok = {}
                for vt in range(32):
                    if vt >= 2:
                        E.wait("sync", DH, ht_tok[vt - 2])
                    E.wait("sync", DH, hsel_tok)
                    E.emit("sync", lambda s, vt=vt: s.dma_start(
                        out=hfm_sb[vt % 2][:],
                        in_=h_mine[:, vt * 128:(vt + 1) * 128], transpose=True),
                        inc=(DR, 16))
                    E.wait("vector", DR, DR.n)
                    E.emit("vector", lambda v, vt=vt: v.tensor_scalar(
                        out=hfm_sb[vt % 2][:], in0=hfm_sb[vt % 2][:],
                        scalar1=gamma_sb[:, :], scalar2=None, op0=ALU.mult),
                        inc=(SV, 1))
                    E.wait("gpsimd", SV, SV.n)
                    E.emit("gpsimd", lambda g, vt=vt: g.dma_start(
                        out=ap_(h_fm_d, vt * 128 * TOK_PC, [[TOK_PC, 128], [1, TOK_PC]]),
                        in_=hfm_sb[vt % 2][:]), inc=(DH, 16))
                    ht_tok[vt] = DH.n
                hfm_done = DH.n
                if "h_mine" in dbg:
                    E.wait("gpsimd", DH, hsel_tok)
                    E.emit("gpsimd", lambda g: g.dma_start(
                        out=dbg["h_mine"], in_=h_mine[:, :]), inc=(DH, 16))
                    hsel_tok = DH.n


                vt_tok = {}
                gu_hist = {}
                dn_hist = {}
                for qq in range(2):
                    tq = qq * 512
                    for tt2 in range(2):
                        tt = qq * 2 + tt2
                        if not (qq == 0 and tt2 == 0):
                            E.wait("gpsimd", SV, xb2_use_tok)
                            E.emit("gpsimd", lambda g, tt=tt: g.dma_start(
                                out=xb2_sb[:].rearrange("p (a t) -> p a t", t=256),
                                in_=x2_fm_d.rearrange("(a p) t -> p a t", p=128)[:, :, tt * 256:(tt + 1) * 256]),
                                inc=(SG, 16))
                            xb2_ld_tok = SG.n
                        E.wait("vector", SG, xb2_ld_tok)
                        E.wait("tensor", SG, xb2_ld_tok)
                        E.wait("tensor", SV, SV.n)
                        for ft in range(16):
                            if ft > 0:
                                E.wait("vector", ST, ST.n)
                            E.emit("vector", lambda v, ft=ft: v.tensor_mul(
                                out=x2b2_sb[:], in0=xb2_sb[:, ft * 256:(ft + 1) * 256],
                                in1=xb2_sb[:, ft * 256:(ft + 1) * 256]), inc=(SV, 1))
                            E.wait("tensor", SV, SV.n)
                            E.emit("tensor", lambda t, ft=ft: t.matmul(
                                PR[0:1, 0:256], ones_sb[:], xb2_sb[:, ft * 256:(ft + 1) * 256],
                                start=(ft == 0), stop=(ft == 15)))
                            E.emit("tensor", lambda t, ft=ft: t.matmul(
                                PO[0:1, 0:256], ones_sb[:], x2b2_sb[:],
                                start=(ft == 0), stop=(ft == 15)), inc=(ST, 1))
                        E.wait("vector", ST, ST.n)
                        E.emit("vector", lambda v: v.tensor_scalar(
                            out=mr2_sb[:, 0:256], in0=PR[0:1, 0:256], scalar1=1.0 / HID,
                            scalar2=None, op0=ALU.mult))
                        E.emit("vector", lambda v: v.tensor_scalar(
                            out=scr2_sb[0:1, 0:256], in0=PO[0:1, 0:256], scalar1=1.0 / HID,
                            scalar2=None, op0=ALU.mult), inc=(SV, 1))
                        E.edge("vector", SV)
                        E.emit("vector", lambda v: v.tensor_mul(
                            out=scr2_sb[0:1, 256:512], in0=mr2_sb[:, 0:256], in1=mr2_sb[:, 0:256]))
                        E.edge("vector", SV)
                        E.emit("vector", lambda v: v.scalar_tensor_tensor(
                            out=mr2_sb[:, 256:512], in0=scr2_sb[0:1, 0:256], scalar=1e-5,
                            in1=scr2_sb[0:1, 256:512], op0=ALU.add, op1=ALU.subtract), inc=(SV, 1))
                        E.wait("scalar", SV, SV.n)
                        E.emit("scalar", lambda s: s.activation(
                            out=mr2_sb[:, 256:512], in_=mr2_sb[:, 256:512], func=AF.Sqrt),
                            inc=(SS, 1))
                        E.wait("vector", SS, SS.n)
                        E.emit("vector", lambda v: v.reciprocal(
                            out=mr2_sb[:, 256:512], in_=mr2_sb[:, 256:512]), inc=(SV, 1))
                        E.wait("sync", SV, SV.n)
                        E.emit("sync", lambda s, tt=tt: s.dma_start(
                            out=ap_(rows2_h, tt * 256, [[TOK_PC, 2], [1, 256]]),
                            in_=ap_(mr2_sb[:], 0, [[512, 1], [256, 2], [1, 256]])), inc=(DR, 16))
                        E.wait("sync", DR, DR.n)
                        E.emit("sync", lambda s, tt=tt: s.dma_start(
                            out=mrep2_sb[:], in_=ap_(rows2_h, tt * 256, [[0, 128], [1, 256]])),
                            inc=(DR, 16))
                        E.emit("sync", lambda s, tt=tt: s.dma_start(
                            out=rrep2_sb[:], in_=ap_(rows2_h, TOK_PC + tt * 256, [[0, 128], [1, 256]])),
                            inc=(DR, 16))
                        E.wait("vector", DR, DR.n)
                        xbv = ap_(xb2_sb[:], 0, [[16 * 256, 128], [256, 16], [1, 256]])
                        xnv = ap_(xn2_sb[:], tt2 * 256, [[16 * 512, 128], [512, 16], [1, 256]])
                        mrv = ap_(mrep2_sb[:], 0, [[256, 128], [0, 16], [1, 256]])
                        rrv = ap_(rrep2_sb[:], 0, [[256, 128], [0, 16], [1, 256]])
                        E.emit("vector", lambda v, xbv=xbv, xnv=xnv, mrv=mrv: v.tensor_tensor(
                            out=xnv, in0=xbv, in1=mrv, op=ALU.subtract))
                        E.emit("vector", lambda v, xnv=xnv, rrv=rrv: v.tensor_tensor(
                            out=xnv, in0=xnv, in1=rrv, op=ALU.mult), inc=(SV, 1))
                        xb2_use_tok = SV.n
                    xn2_tok = SV.n
                    if "xn2" in dbg:
                        E.wait("gpsimd", SV, xn2_tok)
                        E.emit("gpsimd", lambda g, qq=qq: g.dma_start(
                            out=ap_(dbg["xn2"], qq * 128 * 8192, [[8192, 128], [1, 8192]]),
                            in_=xn2_sb[:]), inc=(DH, 16))
                        E.wait("vector", DH, DH.n)
                    E.wait("tensor", SV, xn2_tok)
                    for ft in range(16):
                        E.emit("tensor", lambda t, ft=ft: t.matmul(
                            PR[0:1, 0:512], ones_sb[:], xn2_sb[:, ft * 512:(ft + 1) * 512],
                            start=(ft == 0), stop=(ft == 15)),
                            inc=(ST, 1) if ft == 15 else None)
                    E.wait("vector", ST, ST.n)
                    E.emit("vector", lambda v: v.tensor_copy(
                        out=scr2_sb[0:1, 0:512], in_=PR[0:1, 0:512]), inc=(SV, 1))
                    E.wait("sync", SV, SV.n)
                    E.emit("sync", lambda s: s.dma_start(
                        out=ap_(rows2_h, 2 * TOK_PC, [[TOK_PC, 1], [1, 512]]),
                        in_=scr2_sb[0:1, 0:512]), inc=(DR, 16))
                    E.wait("sync", DR, DR.n)
                    E.emit("sync", lambda s: s.dma_start(
                        out=csrep_sb[:], in_=ap_(rows2_h, 2 * TOK_PC, [[0, 128], [1, 512]])),
                        inc=(DR, 16))
                    cs_tok = DR.n
                    gu_bank_tok = SV.n
                    for ot in range(64):
                        got = qq * 64 + ot
                        if got >= 2:
                            E.wait("sync", ST, gu_hist[got - 2])
                        E.emit("sync", lambda s, ot=ot: s.dma_start(
                            out=wg_sb[ot % 2][:].rearrange("p (a o) -> p a o", o=128),
                            in_=ap_(gate_wT_d, ot * 128,
                                    [[8192, 128], [128 * 8192, 16], [1, 128]])), inc=(DX, 16))
                        if got >= 2:
                            E.wait("scalar", ST, gu_hist[got - 2])
                        E.emit("scalar", lambda s, ot=ot: s.dma_start(
                            out=wu_sb[ot % 2][:].rearrange("p (a o) -> p a o", o=128),
                            in_=ap_(up_wT_d, ot * 128,
                                    [[8192, 128], [128 * 8192, 16], [1, 128]])), inc=(DR, 16))
                        E.wait("tensor", DX, DX.n)
                        E.wait("tensor", DR, DR.n)
                        E.wait("tensor", SV, gu_bank_tok)
                        for ft in range(16):
                            E.emit("tensor", lambda t, ot=ot, ft=ft: t.ldweights(
                                wg_sb[ot % 2][:, ft * 128:(ft + 1) * 128]))
                            E.emit("tensor", lambda t, ot=ot, ft=ft: t.matmul(
                                PQ[0][0:128, 0:512], wg_sb[ot % 2][:, ft * 128:(ft + 1) * 128],
                                xn2_sb[:, ft * 512:(ft + 1) * 512],
                                start=(ft == 0), stop=(ft == 15)))
                            E.emit("tensor", lambda t, ot=ot, ft=ft: t.ldweights(
                                wu_sb[ot % 2][:, ft * 128:(ft + 1) * 128]))
                            E.emit("tensor", lambda t, ot=ot, ft=ft: t.matmul(
                                PQ[1][0:128, 0:512], wu_sb[ot % 2][:, ft * 128:(ft + 1) * 128],
                                xn2_sb[:, ft * 512:(ft + 1) * 512],
                                start=(ft == 0), stop=(ft == 15)),
                                inc=(ST, 1) if ft == 15 else None)
                        gu_tok = ST.n
                        gu_hist[qq * 64 + ot] = ST.n
                        E.wait("vector", ST, gu_tok)
                        E.wait("vector", DR, cs_tok)
                        E.wait("vector", SS, SS.n)
                        E.emit("vector", lambda v: v.scalar_tensor_tensor(
                            out=big1_sb[:], in0=csrep_sb[:], scalar=-128.0,
                            in1=PQ[0][0:128, 0:512], op0=ALU.mult, op1=ALU.add), inc=(SV, 1))
                        E.wait("scalar", SV, SV.n)
                        E.emit("scalar", lambda s, ot=ot: s.activation(
                            out=sil_sb[:], in_=big1_sb[:], func=AF.Silu,
                            scale=gs_sb[:, ot:ot + 1]), inc=(SS, 1))
                        E.emit("vector", lambda v: v.scalar_tensor_tensor(
                            out=big2_sb[:], in0=csrep_sb[:], scalar=-128.0,
                            in1=PQ[1][0:128, 0:512], op0=ALU.mult, op1=ALU.add), inc=(SV, 1))
                        gu_bank_tok = SV.n
                        E.wait("vector", SS, SS.n)
                        E.emit("vector", lambda v, ot=ot: v.scalar_tensor_tensor(
                            out=ii_sb[:, ot * 512:(ot + 1) * 512], in0=big2_sb[:],
                            scalar=us_sb[:, ot:ot + 1], in1=sil_sb[:],
                            op0=ALU.mult, op1=ALU.mult), inc=(SV, 1))
                        ii_tok = SV.n
                        E.wait("tensor", SV, ii_tok)
                        E.emit("tensor", lambda t, ot=ot: t.matmul(
                            PR[0:1, 0:512], ones_sb[:], ii_sb[:, ot * 512:(ot + 1) * 512],
                            start=(ot == 0), stop=(ot == 63)),
                            inc=(ST, 1) if ot == 63 else None)
                    if "ii" in dbg:
                        E.wait("gpsimd", SV, ii_tok)
                        E.emit("gpsimd", lambda g, qq=qq: g.dma_start(
                            out=ap_(dbg["ii"], qq * 128 * 32768, [[32768, 128], [1, 32768]]),
                            in_=ii_sb[:]), inc=(DH, 16))
                        E.wait("vector", DH, DH.n)
                    E.wait("vector", ST, ST.n)
                    E.emit("vector", lambda v: v.tensor_copy(
                        out=scr2_sb[0:1, 0:512], in_=PR[0:1, 0:512]), inc=(SV, 1))
                    E.wait("sync", SV, SV.n)
                    E.emit("sync", lambda s: s.dma_start(
                        out=ap_(rows2_h, 3 * TOK_PC, [[TOK_PC, 1], [1, 512]]),
                        in_=scr2_sb[0:1, 0:512]), inc=(DR, 16))
                    E.wait("sync", DR, DR.n)
                    E.emit("sync", lambda s: s.dma_start(
                        out=cirep_sb[:], in_=ap_(rows2_h, 3 * TOK_PC, [[0, 128], [1, 512]])),
                        inc=(DR, 16))
                    ci_tok = DR.n

                    for ht in range(16):
                        ght = qq * 16 + ht
                        if ght >= 1:
                            E.wait("sync", ST, dn_hist[(ght - 1, 0)])
                        E.emit("sync", lambda s, ht=ht: s.dma_start(
                            out=ap_(wd_sb[0][:], 0, [[64 * 128, 128], [128, 32], [1, 128]]),
                            in_=ap_(down_wT_d, ht * 128,
                                    [[2048, 128], [128 * 2048, 32], [1, 128]])), inc=(DX, 16))
                        if ght >= 1:
                            E.wait("scalar", ST, dn_hist[(ght - 1, 1)])
                        E.emit("scalar", lambda s, ht=ht: s.dma_start(
                            out=ap_(wd_sb[0][:], 32 * 128, [[64 * 128, 128], [128, 32], [1, 128]]),
                            in_=ap_(down_wT_d, ht * 128 + 32 * 128 * 2048,
                                    [[2048, 128], [128 * 2048, 32], [1, 128]])), inc=(DR, 16))
                        E.wait("tensor", DX, DX.n)
                        E.wait("tensor", SV, SV.n)
                        E.wait("tensor", SS, SS.n)
                        for it in range(32):
                            E.emit("tensor", lambda t, ht=ht, it=it: t.ldweights(
                                wd_sb[0][:, it * 128:(it + 1) * 128]))
                            E.emit("tensor", lambda t, ht=ht, it=it: t.matmul(
                                PD[ht % 2][0:128, 0:512],
                                wd_sb[0][:, it * 128:(it + 1) * 128],
                                ii_sb[:, it * 512:(it + 1) * 512],
                                start=(it == 0), stop=False),
                                inc=(ST, 1) if it == 31 else None)
                        dn_hist[(ght, 0)] = ST.n
                        E.wait("tensor", DR, DR.n)
                        for it in range(32, 64):
                            E.emit("tensor", lambda t, ht=ht, it=it: t.ldweights(
                                wd_sb[0][:, it * 128:(it + 1) * 128]))
                            E.emit("tensor", lambda t, ht=ht, it=it: t.matmul(
                                PD[ht % 2][0:128, 0:512],
                                wd_sb[0][:, it * 128:(it + 1) * 128],
                                ii_sb[:, it * 512:(it + 1) * 512],
                                start=False, stop=(it == 63)),
                                inc=(ST, 1) if it == 63 else None)
                        dn_hist[(ght, 1)] = ST.n
                        dmm_tok = ST.n
                        E.wait("vector", ST, dmm_tok)
                        E.wait("vector", DR, ci_tok)
                        E.emit("vector", lambda v, ht=ht: v.scalar_tensor_tensor(
                            out=big1_sb[:], in0=cirep_sb[:], scalar=-128.0,
                            in1=PD[ht % 2][0:128, 0:512], op0=ALU.mult, op1=ALU.add),
                            inc=(SV, 1))
                        E.wait("scalar", SV, SV.n)
                        E.emit("scalar", lambda s, ht=ht: s.activation(
                            out=gfin_sb[:, ht * 512:(ht + 1) * 512], in_=big1_sb[:],
                            func=AF.Silu, scale=ds_sb[:, ht:ht + 1]), inc=(SS, 1))
                    gfin_tok = SS.n
                    if "gfin" in dbg:
                        E.wait("gpsimd", SS, gfin_tok)
                        E.emit("gpsimd", lambda g, qq=qq: g.dma_start(
                            out=ap_(dbg["gfin"], qq * 128 * 8192, [[8192, 128], [1, 8192]]),
                            in_=gfin_sb[:]), inc=(DH, 16))
                        E.wait("scalar", DH, DH.n)

                    for og in range(4):
                        for vt in range(32):
                            gvt = (qq * 4 + og) * 32 + vt
                            if gvt >= 2:
                                E.wait("sync", ST, vt_tok[gvt - 2])
                            if gvt == 0:
                                E.wait("sync", SG, wo16_tok)
                                E.wait("sync", DH, hfm_done)
                            E.emit("sync", lambda s, og=og, vt=vt: s.dma_start(
                                out=wo_sb[vt % 2][:],
                                in_=ap_(wo16_d, vt * 128 * HID + og * 512,
                                        [[HID, 128], [1, 512]])), inc=(DX, 16))
                            if gvt >= 2:
                                E.wait("scalar", ST, vt_tok[gvt - 2])
                            E.emit("scalar", lambda s, vt=vt, tq=tq: s.dma_start(
                                out=ap_(hfm_sb[vt % 2][:], 0, [[1024, 128], [1, 512]]),
                                in_=ap_(h_fm_d, vt * 128 * TOK_PC + tq,
                                        [[TOK_PC, 128], [1, 512]])), inc=(DR, 16))
                            E.wait("tensor", DX, DX.n)
                            E.wait("tensor", DR, DR.n)
                            for o4 in range(4):
                                pbk = [PQ[0], PQ[1], PD[0], PD[1]][o4]
                                E.emit("tensor", lambda t, vt=vt, o4=o4: t.ldweights(
                                    wo_sb[vt % 2][:, o4 * 128:(o4 + 1) * 128]))
                                E.emit("tensor", lambda t, vt=vt, o4=o4, pbk=pbk: t.matmul(
                                    pbk[0:128, 0:512],
                                    wo_sb[vt % 2][:, o4 * 128:(o4 + 1) * 128],
                                    ap_(hfm_sb[vt % 2][:], 0, [[1024, 128], [1, 512]]),
                                    start=(vt == 0), stop=(vt == 31)),
                                    inc=(ST, 1) if o4 == 3 else None)
                            vt_tok[gvt] = ST.n
                        wout_tok = ST.n
                        E.wait("vector", ST, wout_tok)
                        E.wait("vector", SS, gfin_tok)
                        E.wait("vector", DH, DH.n)
                        for o4 in range(4):
                            ot2 = og * 4 + o4
                            pbk = [PQ[0], PQ[1], PD[0], PD[1]][o4]
                            E.emit("vector", lambda v, pbk=pbk, ot2=ot2: v.tensor_tensor(
                                out=sgbuf_sb[:], in0=pbk[0:128, 0:512],
                                in1=gfin_sb[:, ot2 * 512:(ot2 + 1) * 512], op=ALU.mult),
                                inc=(SV, 1))
                            E.wait("tensor", SV, SV.n)
                            for tb in range(4):
                                E.emit("tensor", lambda t, tb=tb: t.transpose(
                                    PR[0:128, tb * 128:(tb + 1) * 128],
                                    sgbuf_sb[:, tb * 128:(tb + 1) * 128], id128f_sb[:]),
                                    inc=(ST, 1) if tb == 3 else None)
                            E.wait("vector", ST, ST.n)
                            E.emit("vector", lambda v, ot2=ot2: v.tensor_copy(
                                out=ap_(asm_sb[:], ot2 * 128,
                                        [[4 * 2048, 128], [2048, 4], [1, 128]]),
                                in_=PR[0:128, 0:512]), inc=(SV, 1))
                    asm_tok = SV.n

                    for tb in range(4):
                        E.wait("sync", SV, SV.n)
                        E.emit("sync", lambda s, qq=qq, tb=tb, tq=tq: s.dma_start(
                            out=xtm_sb[:],
                            in_=x_tm_d[tq + tb * 128: tq + (tb + 1) * 128, :]), inc=(DX, 16))
                        E.wait("vector", DX, DX.n)
                        E.wait("vector", SV, asm_tok)
                        E.emit("vector", lambda v, tb=tb: v.tensor_add(
                            out=asm_sb[:, tb * 2048:(tb + 1) * 2048], in0=xtm_sb[:],
                            in1=asm_sb[:, tb * 2048:(tb + 1) * 2048]), inc=(SV, 1))
                        E.wait("gpsimd", SV, SV.n)
                        E.emit("gpsimd", lambda g, qq=qq, tb=tb, tq=tq: g.dma_start(
                            out=out_d[tq + tb * 128: tq + (tb + 1) * 128, :],
                            in_=asm_sb[:, tb * 2048:(tb + 1) * 2048]), inc=(DH, 16))
                        E.wait("vector", DH, DH.n)
                    if "asm" in dbg:
                        E.wait("gpsimd", SV, SV.n)
                        E.emit("gpsimd", lambda g, qq=qq: g.dma_start(
                            out=ap_(dbg["asm"], qq * 128 * 8192, [[8192, 128], [1, 8192]]),
                            in_=asm_sb[:]), inc=(DH, 16))
                        E.wait("vector", DH, DH.n)

                E.wait("gpsimd", DH, DH.n)

        with nc.Block() as block:
            @block.sync
            def _(sync):
                E.run("sync", sync)

            @block.scalar
            def _(scalar):
                E.run("scalar", scalar)

            @block.vector
            def _(vector):
                E.run("vector", vector)

            @block.tensor
            def _(tensor):
                E.run("tensor", tensor)

            @block.gpsimd
            def _(gpsimd):
                E.run("gpsimd", gpsimd)

    return nc, dbg


_BUILD_CACHE = {}


def kernel(**inputs):
    percore, x = _host_prep(inputs)
    key = "full"
    if key not in _BUILD_CACHE:
        _BUILD_CACHE[key] = build_program()
    nc, _ = _BUILD_CACHE[key]
    names = ["x_fm", "x2_fm", "x_tm", "WqkvT", "WzT", "WbaT", "convw", "dtb",
             "alog", "gamma", "WoutT", "gate_wT", "up_wT", "down_wT", "gate_s",
             "up_s", "down_s", "t_lo", "t_hi", "t_hii", "eye2", "ident128",
             "ident128f", "onescol", "sel"]
    in_maps = [{n: percore[c][n] for n in names} for c in range(NCORE)]
    res = run_bass_kernel_spmd(nc, in_maps, core_ids=list(range(NCORE)))
    out = np.concatenate([np.asarray(res.results[c]["out"]) for c in range(NCORE)], axis=0)
    return out.reshape(2, 4096, 2048)


# version 11 (final)
# speedup vs baseline: 1.0204x; 1.0168x over previous
import math
import numpy as np
from contextlib import ExitStack

import concourse.bass as bass
import concourse.mybir as mybir
from concourse.bass_utils import run_bass_kernel_spmd
import ml_dtypes

F32 = mybir.dt.float32
BF16 = mybir.dt.bfloat16
U8 = mybir.dt.uint8
U32 = mybir.dt.uint32
AF = mybir.ActivationFunctionType
ALU = mybir.AluOpType
BF = ml_dtypes.bfloat16

HID = 2048
T_ALL = 8192
CH = 64
TILE = 256
NTILE = T_ALL // TILE
NCORE = 8
TOK_PC = 1024
LNDKI = float(np.float32(math.log(128.0 ** -0.5)))
CLAMP = 80.0


class Sem:
    def __init__(self):
        self.h = None
        self.n = 0


class Emitter:
    ENGINES = ("sync", "scalar", "vector", "tensor", "gpsimd")

    def __init__(self):
        self.ops = {e: [] for e in self.ENGINES}

    def wait(self, eng, sem, val=None):
        v = sem.n if val is None else val
        if v > 0:
            self.ops[eng].append(("w", sem, v))

    def emit(self, eng, fn, inc=None):
        incs = [] if inc is None else ([inc] if isinstance(inc, tuple) else list(inc))
        for s, k in incs:
            s.n += k
        self.ops[eng].append(("e", fn, incs))

    def edge(self, eng, sem):
        for it in reversed(self.ops[eng]):
            if it[0] == "e":
                if not any(s is sem for s, _k in it[2]):
                    sem.n += 1
                    it[2].append((sem, 1))
                break
        else:
            raise RuntimeError("edge: no prior op on " + eng)
        self.ops[eng].append(("w", sem, sem.n))

    def run(self, eng, engobj):
        for it in self.ops[eng]:
            if it[0] == "w":
                engobj.wait_ge(it[1].h, it[2])
            else:
                ins = it[1](engobj)
                for s, k in it[2]:
                    ins = ins.then_inc(s.h, k)


def ap_(base_ap, off, dims):
    return bass.AP(base_ap.tensor, base_ap.offset + off, dims)


def _host_prep(inputs):
    x = np.ascontiguousarray(
        np.asarray(inputs["L0_attn_hidden_states"], np.float32).reshape(T_ALL, HID))
    x_fm = np.ascontiguousarray(x.T)
    Wq = np.asarray(inputs["L0_attn_W_qkvz"], np.float32)
    Wba = np.asarray(inputs["L0_attn_W_ba"], np.float32)
    convw = np.asarray(inputs["L0_attn_conv_w"], np.float32).reshape(8192, 4)
    dtb = np.asarray(inputs["L0_attn_dt_bias"], np.float32)
    Alog = np.asarray(inputs["L0_attn_A_log"], np.float32)
    gamma = np.asarray(inputs["L0_attn_norm_weight"], np.float32)
    WoutT = np.ascontiguousarray(np.asarray(inputs["L0_attn_W_out"], np.float32).T)
    gate_wT = np.ascontiguousarray(np.asarray(inputs["L0_ffn_gate_w"], np.int32).astype(np.uint8).T.astype(BF))
    up_wT = np.ascontiguousarray(np.asarray(inputs["L0_ffn_up_w"], np.int32).astype(np.uint8).T.astype(BF))
    down_wT = np.ascontiguousarray(np.asarray(inputs["L0_ffn_down_w"], np.int32).astype(np.uint8).T.astype(BF))

    idx = np.arange(CH)
    t_lo = np.ascontiguousarray(np.tile(-(idx[:, None] > idx[None, :]).astype(np.float32), (2, 1)))
    t_hi = np.ascontiguousarray(np.tile(-(idx[:, None] < idx[None, :]).astype(np.float32), (2, 1)))
    t_hii = np.ascontiguousarray(np.tile((idx[:, None] <= idx[None, :]).astype(np.float32), (2, 1)))
    eye2 = np.ascontiguousarray(np.tile(np.eye(CH, dtype=np.float32), (2, 1)).astype(BF))
    ident128 = np.ascontiguousarray(np.eye(128, dtype=np.float32).astype(BF))
    ident128f = np.ascontiguousarray(np.eye(128, dtype=np.float32))
    onescol = np.ascontiguousarray(np.ones((128, 1), dtype=np.float32).astype(BF))

    percore = []
    for c in range(NCORE):
        kh = [2 * c, 2 * c + 1]
        vh = [4 * c + i for i in range(4)]
        rows = []
        for h in kh:
            rows += list(range(h * 128, (h + 1) * 128))
        for h in kh:
            rows += list(range(2048 + h * 128, 2048 + (h + 1) * 128))
        for h in vh:
            rows += list(range(4096 + h * 128, 4096 + (h + 1) * 128))
        zrows = [8192 + h * 128 + i for h in vh for i in range(128)]
        barows = list(vh) + [32 + h for h in vh]
        sel = np.zeros((128, 8), np.float32)
        sel[:, c] = 1
        percore.append(dict(
            x_fm=x_fm,
            x2_fm=np.ascontiguousarray(x_fm[:, c * TOK_PC:(c + 1) * TOK_PC]),
            x_tm=np.ascontiguousarray(x[c * TOK_PC:(c + 1) * TOK_PC, :]),
            WqkvT=np.ascontiguousarray(Wq[rows, :].T),
            WzT=np.ascontiguousarray(Wq[zrows, :].T),
            WbaT=np.ascontiguousarray(Wba[barows, :].T),
            convw=np.ascontiguousarray(convw[rows, :]),
            dtb=np.ascontiguousarray(dtb[vh].reshape(4, 1)),
            alog=np.ascontiguousarray(Alog[vh].reshape(4, 1)),
            gamma=np.ascontiguousarray(gamma.reshape(128, 1)),
            WoutT=WoutT,
            gate_wT=gate_wT, up_wT=up_wT, down_wT=down_wT,
            gate_s=np.ascontiguousarray(np.asarray(inputs["L0_ffn_gate_scale"], np.float32).reshape(64, 128).T),
            up_s=np.ascontiguousarray(np.asarray(inputs["L0_ffn_up_scale"], np.float32).reshape(64, 128).T),
            down_s=np.ascontiguousarray(np.asarray(inputs["L0_ffn_down_scale"], np.float32).reshape(16, 128).T),
            t_lo=t_lo, t_hi=t_hi, t_hii=t_hii, eye2=eye2,
            ident128=ident128, ident128f=ident128f, onescol=onescol,
            sel=sel,
        ))
    return percore, x


def build_program(ntile=NTILE, phase2=True, debug=()):
    nc = bass.Bass("TRN2", target_bir_lowering=False,
                   detect_race_conditions=False)
    E = Emitter()

    din = lambda name, shape, dt: nc.dram_tensor(name, shape, dt, kind="ExternalInput").ap()
    x_fm_d = din("x_fm", [HID, T_ALL], F32)
    x2_fm_d = din("x2_fm", [HID, TOK_PC], F32)
    x_tm_d = din("x_tm", [TOK_PC, HID], F32)
    WqkvT_d = din("WqkvT", [HID, 1024], F32)
    WzT_d = din("WzT", [HID, 512], F32)
    WbaT_d = din("WbaT", [HID, 8], F32)
    convw_d = din("convw", [1024, 4], F32)
    dtb_d = din("dtb", [4, 1], F32)
    alog_d = din("alog", [4, 1], F32)
    gamma_d = din("gamma", [128, 1], F32)
    WoutT_d = din("WoutT", [4096, 2048], F32)
    gate_wT_d = din("gate_wT", [HID, 8192], BF16)
    up_wT_d = din("up_wT", [HID, 8192], BF16)
    down_wT_d = din("down_wT", [8192, HID], BF16)
    gate_s_d = din("gate_s", [128, 64], F32)
    up_s_d = din("up_s", [128, 64], F32)
    down_s_d = din("down_s", [128, 16], F32)
    tlo_d = din("t_lo", [128, 64], F32)
    thi_d = din("t_hi", [128, 64], F32)
    thii_d = din("t_hii", [128, 64], F32)
    eye2_d = din("eye2", [128, 64], BF16)
    id128_d = din("ident128", [128, 128], BF16)
    id128f_d = din("ident128f", [128, 128], F32)
    ones_d = din("onescol", [128, 1], BF16)
    sel_d = din("sel", [128, 8], F32)

    out_d = nc.dram_tensor("out", [TOK_PC, HID], F32, kind="ExternalOutput").ap()

    rows_h = nc.dram_tensor("rows_h", [20, T_ALL], F32, kind="Internal").ap()
    h_src = nc.dram_tensor("h_src", [T_ALL, 512], BF16, kind="Internal").ap()
    h_all = nc.dram_tensor("h_all", [NCORE, T_ALL, 512], BF16, kind="Internal",
                           addr_space="Shared").ap()
    h_mine = nc.dram_tensor("h_mine", [TOK_PC, 4096], BF16, kind="Internal").ap()
    rows2_h = nc.dram_tensor("rows2_h", [4, TOK_PC], F32, kind="Internal").ap()
    wo16_d = nc.dram_tensor("wo16", [4096, HID], BF16, kind="Internal").ap()
    h_fm_d = nc.dram_tensor("h_fm", [4096, TOK_PC], BF16, kind="Internal").ap()

    dbg = {}
    def DBG(name, shape, dt=F32):
        if name in debug:
            dbg[name] = nc.dram_tensor("dbg_" + name, shape, dt, kind="ExternalOutput").ap()
        return dbg.get(name)

    DBG("h_src", [T_ALL, 512], BF16)
    DBG("qkv0", [128, 8 * TILE], BF16)
    DBG("xn0", [128, 16 * TILE], BF16)
    DBG("z0", [128, 1024], F32)
    DBG("rows", [20, T_ALL], F32)
    DBG("cp0", [128, 64], F32)
    DBG("stf", [128, 512], F32)
    DBG("o20", [128, 512], F32)
    DBG("vn0", [128, 512], F32)
    DBG("chn0", [128, 256], BF16)
    DBG("kcd0", [128, 256], BF16)
    DBG("pint", [128, 512], BF16)
    DBG("patt", [128, 512], BF16)
    DBG("o2pre", [128, 512], F32)
    DBG("rmsv", [128, 20], F32)
    DBG("h_mine", [TOK_PC, 4096], BF16)
    DBG("xn2", [2 * 128, 8192], BF16)
    DBG("ii", [2 * 128, 64 * 512], BF16)
    DBG("gfin", [2 * 128, 8192], BF16)
    DBG("asm", [2 * 128, 8192], F32)

    DX = Sem(); DR = Sem(); DH = Sem(); ST = Sem(); SV = Sem(); SS = Sem()
    SG = Sem(); CC = Sem()
    SGX = [Sem(), Sem()]; SGW = Sem(); DHS = [Sem(), Sem()]
    DRM = [Sem(), Sem()]; DRL = [Sem(), Sem()]; DRR = [Sem(), Sem()]
    ALL_SEMS = dict(dx=DX, dr=DR, dh=DH, t=ST, v=SV, s=SS, g=SG, cc=CC,
                    sgx0=SGX[0], sgx1=SGX[1], sgw=SGW, dhs0=DHS[0], dhs1=DHS[1],
                    drm0=DRM[0], drm1=DRM[1], drl0=DRL[0], drl1=DRL[1],
                    drr0=DRR[0], drr1=DRR[1])

    ctx = ExitStack()
    with ctx:
        for name, s in ALL_SEMS.items():
            s.h = ctx.enter_context(nc.semaphore("sem_" + name))
        SBt = lambda n, sh, dt: ctx.enter_context(nc.sbuf_tensor(n, sh, dt))
        _psn = [0]
        def PSt(sh, dt=F32):
            _psn[0] += 1
            return ctx.enter_context(nc.psum_tensor(f"ps{_psn[0]}", sh, dt))

        dtb_sb = SBt("dtb_sb", [4, 1], F32)
        nega_sb = SBt("nega_sb", [4, 1], F32)
        gamma_sb = SBt("gamma_sb", [128, 1], F32)
        tlo_sb = SBt("tlo_sb", [128, 64], F32)
        thi_sb = SBt("thi_sb", [128, 64], F32)
        thii_sb = SBt("thii_sb", [128, 64], F32)
        eye2_sb = SBt("eye2_sb", [128, 64], BF16)
        id128_sb = SBt("id128_sb", [128, 128], BF16)
        id128f_sb = SBt("id128f_sb", [128, 128], F32)
        ones_sb = SBt("ones_sb", [128, 1], BF16)
        gs_sb = SBt("gs_sb", [128, 64], F32)
        us_sb = SBt("us_sb", [128, 64], F32)
        ds_sb = SBt("ds_sb", [128, 16], F32)
        sel_sb = SBt("sel_sb", [128, 8], F32)

        PQ = [PSt([128, 512]), PSt([128, 512])]
        PD = [PSt([128, 512]), PSt([128, 512])]
        PV = PSt([128, 512])
        PO = PSt([128, 512])
        PR = PSt([128, 512])
        PT = PSt([128, 512], BF16)

        for src, dst in [
            (dtb_d, dtb_sb), (alog_d, nega_sb), (gamma_d, gamma_sb),
            (tlo_d, tlo_sb), (thi_d, thi_sb), (thii_d, thii_sb),
            (eye2_d, eye2_sb), (id128_d, id128_sb), (id128f_d, id128f_sb),
            (ones_d, ones_sb), (gate_s_d, gs_sb), (up_s_d, us_sb),
            (down_s_d, ds_sb), (sel_d, sel_sb),
        ]:
            E.emit("sync", lambda s, a=src, b=dst: s.dma_start(out=b[:], in_=a), inc=(DX, 16))
        const_tok = DX.n

        p1 = ExitStack()
        with p1:
            SB1 = lambda n, sh, dt: p1.enter_context(nc.sbuf_tensor(n, sh, dt))
            wq_sb = SB1("wq_sb", [128, 16 * 1024], BF16)
            wz_sb = SB1("wz_sb", [128, 16 * 512], BF16)
            wba_sb = SB1("wba_sb", [128, 16 * 8], BF16)
            convw_sb = SB1("convw_sb", [128, 8 * 4], F32)
            xb_sb = [SB1(f"xb{i}", [128, 16 * TILE], BF16) for i in range(2)]
            x2b_sb = SB1("x2b", [128, TILE], BF16)
            t32_sb = SB1("t32", [128, TILE], F32)
            tail_sb = [SB1(f"tail{i}", [128, 32], F32) for i in range(2)]
            rsq32_sb = SB1("rsq32", [128, TILE], BF16)
            q2b_sb = SB1("q2b", [128, TILE], BF16)
            xn_sb = [SB1(f"xn{i}", [128, 16 * TILE], BF16) for i in range(2)]
            mr_sb = [SB1(f"mr{i}", [1, 2 * TILE], F32) for i in range(2)]
            scr_sb = SB1("scr", [4, 2 * TILE], F32)
            bb_sb = [SB1(f"bb{i}", [4, TILE], F32) for i in range(2)]
            aa_sb = [SB1(f"aa{i}", [4, TILE], F32) for i in range(2)]
            lnbp_sb = SB1("lnbp", [4, TILE], F32)
            beta_sb = [SB1(f"beta{i}", [4, TILE], F32) for i in range(2)]
            gc_sb = [SB1(f"gc{i}", [4, TILE], F32) for i in range(2)]
            gl_sb = [SB1(f"gl{i}", [4, TILE], F32) for i in range(2)]
            l2p_sb = SB1("l2p", [1, 4 * TILE], F32)
            l2hv_sb = [SB1(f"l2hv{i}", [4, 2 * TILE], F32) for i in range(2)]
            rws_sb = [SB1(f"rws{i}", [4, 4 * TILE], F32) for i in range(2)]
            mrep_sb = [SB1(f"mrep{i}", [128, TILE], F32) for i in range(2)]
            rrep_sb = [SB1(f"rrep{i}", [128, TILE], F32) for i in range(2)]
            arep_sb = [SB1(f"arep{i}", [128, 4 * TILE], F32) for i in range(2)]
            carep_sb = [SB1(f"carep{i}", [128, 4 * TILE], F32) for i in range(2)]
            srep_sb = [SB1(f"srep{i}", [128, 4 * TILE], F32) for i in range(2)]
            egrep_sb = [SB1(f"egrep{i}", [128, 16], F32) for i in range(2)]
            mixed_sb = SB1("mixed", [128, 8 * (TILE + 4)], F32)
            conv_sb = SB1("conv", [128, 8 * TILE], F32)
            qkv_sb = [SB1(f"qkv{i}", [128, 8 * TILE], BF16) for i in range(2)]
            z_sb = [SB1(f"z{i}", [128, 2 * 512], BF16) for i in range(2)]
            batm_sb = [SB1(f"batm{i}", [128, 2 * 8], F32) for i in range(2)]
            ktm_sb = [SB1(f"ktm{i}", [128, 2 * 2 * 128], BF16) for i in range(2)]
            vtm_sb = [SB1(f"vtm{i}", [128, 2 * 4 * 128], BF16) for i in range(2)]
            cp_sb = [SB1(f"cp{i}", [128, 2 * 8 * 4], F32) for i in range(2)]
            st_sb = [SB1(f"states{i}", [128, 4 * 128], F32) for i in range(2)]
            stbf_sb = [SB1(f"statesbf{i}", [128, 4 * 128], BF16) for i in range(2)]
            hbuf_sb = [SB1(f"hbuf{i}", [128, 2 * 512], BF16) for i in range(2)]
            mka_sb = SB1("mka", [128, 4 * 64], F32)
            mkc_sb = SB1("mkc", [128, 4 * 64], F32)
            mks_sb = SB1("mks", [128, 4 * 64], F32)
            ra_sb = SB1("ra", [128, 4 * 64], BF16)
            ca_sb = SB1("ca", [128, 4 * 64], BF16)
            sc_sb = SB1("sc", [128, 4 * 64], BF16)
            pw_sb = SB1("pw", [128, 8 * 4 * 64], BF16)
            r32_sb = SB1("r32", [128, 4 * 64], BF16)
            chn_sb = SB1("chn", [128, 4 * 64], BF16)
            vb_sb = SB1("vb", [128, 4 * 128], BF16)
            ke_sb = SB1("ke", [128, 4 * 128], BF16)
            k2t_sb = SB1("k2t", [128, 4 * 128], BF16)
            kcd_sb = SB1("kcd", [128, 4 * 64], BF16)
            vn_sb = SB1("vn", [128, 4 * 128], BF16)
            o2_sb = SB1("o2", [128, 4 * 128], F32)
            rms_sb = SB1("rms", [128, 20], F32)
            if "pint" in dbg:
                pint_sb = SB1("pint_sb", [128, 512], BF16)
            if "patt" in dbg:
                patt_sb = SB1("patt_sb", [128, 512], BF16)

            E.emit("sync", lambda s: s.dma_start(
                out=convw_sb[:].rearrange("p (a k) -> p a k", k=4),
                in_=convw_d.rearrange("(a p) k -> p a k", p=128)),
                inc=(DX, 16))

            def load_x(j, p):
                E.emit("gpsimd", lambda g, j=j, p=p: g.dma_start(
                    out=xb_sb[p][:].rearrange("p (a t) -> p a t", t=TILE),
                    in_=x_fm_d.rearrange("(a p) t -> p a t", p=128)[:, :, j * TILE:(j + 1) * TILE]),
                    inc=(SGX[p], 16))

            load_x(0, 0)
            x_tok = [SGX[0].n, 0]
            load_x(NTILE // 2, 1)
            x_tok[1] = SGX[1].n

            E.emit("gpsimd", lambda g: g.dma_start(
                out=wq_sb[:].rearrange("p (a o) -> p a o", o=1024),
                in_=WqkvT_d.rearrange("(a p) o -> p a o", p=128)), inc=(SGW, 16))
            E.emit("gpsimd", lambda g: g.dma_start(
                out=wz_sb[:].rearrange("p (a o) -> p a o", o=512),
                in_=WzT_d.rearrange("(a p) o -> p a o", p=128)), inc=(SGW, 16))
            E.emit("gpsimd", lambda g: g.dma_start(
                out=wba_sb[:].rearrange("p (a o) -> p a o", o=8),
                in_=WbaT_d.rearrange("(a p) o -> p a o", p=128)), inc=(SGW, 16))
            w_tok = SGW.n
            const_tok = DX.n
            E.wait("scalar", DX, const_tok)
            E.emit("scalar", lambda s: s.activation(
                out=nega_sb[:], in_=nega_sb[:], func=AF.Exp), inc=(SS, 1))
            E.wait("vector", SS, SS.n)
            E.emit("vector", lambda v: v.tensor_scalar(
                out=nega_sb[:], in0=nega_sb[:], scalar1=-1.0, scalar2=None, op0=ALU.mult),
                inc=(SV, 1))
            E.wait("vector", DX, const_tok)
            E.emit("vector", lambda v: v.memset(st_sb[0][:], 0.0))
            E.emit("vector", lambda v: v.memset(stbf_sb[0][:], 0.0))
            E.emit("vector", lambda v: v.memset(st_sb[1][:], 0.0))
            E.emit("vector", lambda v: v.memset(stbf_sb[1][:], 0.0), inc=(SV, 1))
            st_ini_tok = SV.n

            tk = dict(xb=[0, 0], xn=[0, 0], mrw=[0, 0], qkv=[0, 0], z=[0, 0],
                      ba=[0, 0], rows=[0, 0], rep=[0, 0], cp=[0, 0], tm=[0, 0],
                      eg=[0, 0], h=[0, 0], hst=[0, 0], zmm=[0, 0], deltaq=[0, 0],
                      opb=[0, 0], maskv=[0, 0], cpT=[0, 0], baT=[0, 0],
                      mrr=[0, 0])
            bank = dict(PQ=[0, 0], PD=[None, None], PV=0, PO=None, PR=0, PT=None)

            def bwait(eng, tok):
                if tok is None:
                    return
                if isinstance(tok, list):
                    for t in tok:
                        bwait(eng, t)
                    return
                if tok[1] > 0:
                    E.wait(eng, tok[0], tok[1])
            state_tok = [st_ini_tok, st_ini_tok]
            statebf_tok = [0, 0]
            FE16 = 16 * TILE

            def v3(buf, dims, off=0):
                return ap_(buf[:], off, dims)

            def ln_stage(j, p):
                E.wait("vector", SGX[p], x_tok[p])
                E.wait("tensor", SGX[p], x_tok[p])
                E.wait("tensor", SV, bank["PR"])
                bwait("tensor", bank["PO"])
                for ft in range(16):
                    if ft > 0:
                        E.wait("vector", ST, ST.n)
                    E.emit("vector", lambda v, ft=ft: v.tensor_mul(
                        out=x2b_sb[:], in0=xb_sb[p][:, ft * TILE:(ft + 1) * TILE],
                        in1=xb_sb[p][:, ft * TILE:(ft + 1) * TILE]), inc=(SV, 1))
                    E.wait("tensor", SV, SV.n)
                    E.emit("tensor", lambda t, ft=ft: t.matmul(
                        PR[0:1, 0:TILE], ones_sb[:], xb_sb[p][:, ft * TILE:(ft + 1) * TILE],
                        start=(ft == 0), stop=(ft == 15)))
                    E.emit("tensor", lambda t, ft=ft: t.matmul(
                        PO[0:1, 0:TILE], ones_sb[:], x2b_sb[:],
                        start=(ft == 0), stop=(ft == 15)), inc=(ST, 1))
                mr = mr_sb[p]
                E.wait("vector", ST, ST.n)
                E.emit("vector", lambda v: v.tensor_scalar(
                    out=mr[:, 0:TILE], in0=PR[0:1, 0:TILE], scalar1=1.0 / HID,
                    scalar2=None, op0=ALU.mult))
                E.emit("vector", lambda v: v.tensor_scalar(
                    out=scr_sb[0:1, 0:TILE], in0=PO[0:1, 0:TILE],
                    scalar1=1.0 / HID, scalar2=None, op0=ALU.mult), inc=(SV, 1))
                bank["PR"] = SV.n
                bank["PO"] = (SV, SV.n)
                E.edge("vector", SV)
                E.emit("vector", lambda v: v.tensor_mul(
                    out=scr_sb[0:1, TILE:2 * TILE], in0=mr[:, 0:TILE], in1=mr[:, 0:TILE]))
                E.edge("vector", SV)
                E.emit("vector", lambda v: v.scalar_tensor_tensor(
                    out=mr[:, TILE:2 * TILE], in0=scr_sb[0:1, 0:TILE], scalar=1e-5,
                    in1=scr_sb[0:1, TILE:2 * TILE], op0=ALU.add, op1=ALU.subtract), inc=(SV, 1))
                E.wait("scalar", SV, SV.n)
                E.emit("scalar", lambda s: s.activation(
                    out=mr[:, TILE:2 * TILE], in_=mr[:, TILE:2 * TILE], func=AF.Sqrt),
                    inc=(SS, 1))
                E.wait("vector", SS, SS.n)
                E.emit("vector", lambda v: v.reciprocal(
                    out=mr[:, TILE:2 * TILE], in_=mr[:, TILE:2 * TILE]))
                E.edge("vector", SV)
                E.emit("vector", lambda v: v.tensor_mul(
                    out=scr_sb[0:1, 0:TILE], in0=mr[:, 0:TILE],
                    in1=mr[:, TILE:2 * TILE]), inc=(SV, 1))
                mr_tok = SV.n
                E.wait("sync", SV, mr_tok)
                E.emit("sync", lambda s: s.dma_start(
                    out=ap_(rows_h, j * TILE, [[T_ALL, 2], [1, TILE]]),
                    in_=ap_(mr[:], 0, [[2 * TILE, 1], [TILE, 2], [1, TILE]])), inc=(DR, 16))
                E.emit("sync", lambda s: s.dma_start(
                    out=ap_(rows_h, 19 * T_ALL + j * TILE, [[T_ALL, 1], [1, TILE]]),
                    in_=scr_sb[0:1, 0:TILE]), inc=(DR, 16))
                tk["mrr"][p] = DR.n
                E.wait("sync", DR, DR.n)
                E.wait("sync", SV, tk["xn"][p])
                E.emit("sync", lambda s: s.dma_start(
                    out=mrep_sb[p][:],
                    in_=ap_(rows_h, 19 * T_ALL + j * TILE, [[0, 128], [1, TILE]])), inc=(DRM[p], 16))
                E.emit("sync", lambda s: s.dma_start(
                    out=rrep_sb[p][:],
                    in_=ap_(rows_h, T_ALL + j * TILE, [[0, 128], [1, TILE]])), inc=(DRM[p], 16))
                tk["mrw"][p] = DRM[p].n

            def xn_stage(j, p):
                E.wait("vector", DRM[p], tk["mrw"][p])
                E.wait("vector", ST, tk["zmm"][p])
                for ft in range(16):
                    E.emit("vector", lambda v, ft=ft: v.tensor_mul(
                        out=t32_sb[:], in0=xb_sb[p][:, ft * TILE:(ft + 1) * TILE],
                        in1=rrep_sb[p][:]))
                    E.emit("vector", lambda v, ft=ft: v.tensor_sub(
                        out=xn_sb[p][:, ft * TILE:(ft + 1) * TILE],
                        in0=t32_sb[:], in1=mrep_sb[p][:]),
                        inc=(SV, 1) if ft == 15 else None)
                tk["xn"][p] = SV.n

            def fe_stage(j, p):
                E.wait("tensor", SV, tk["xn"][p])
                E.wait("tensor", SS, tk["z"][1 - p])
                E.wait("vector", SS, tk["qkv"][1 - p])
                if j == 0:
                    E.wait("tensor", SGW, w_tok)
                for mb in range(8):
                    pq = mb % 2
                    E.wait("tensor", SV, bank["PQ"][pq])
                    for ft in range(16):
                        wsl = wq_sb[:, ft * 1024 + mb * 128: ft * 1024 + (mb + 1) * 128]
                        E.emit("tensor", lambda t, w=wsl: t.ldweights(w))
                        E.emit("tensor", lambda t, w=wsl, ft=ft, mb=mb: t.matmul(
                            PQ[mb % 2][0:128, 0:TILE], w,
                            xn_sb[p][:, ft * TILE:(ft + 1) * TILE],
                            start=(ft == 0), stop=(ft == 15)),
                            inc=(ST, 1) if ft == 15 else None)
                    E.wait("vector", ST, ST.n)
                    if j == 0 or j == NTILE // 2:
                        E.emit("vector", lambda v, mb=mb: v.memset(
                            mixed_sb[:, mb * (TILE + 4): mb * (TILE + 4) + 4], 0.0))
                    else:
                        E.emit("vector", lambda v, mb=mb, p=p: v.tensor_copy(
                            out=mixed_sb[:, mb * (TILE + 4): mb * (TILE + 4) + 4],
                            in_=tail_sb[p][:, mb * 4:(mb + 1) * 4]))
                    E.emit("vector", lambda v, mb=mb: v.tensor_copy(
                        out=mixed_sb[:, mb * (TILE + 4) + 4: mb * (TILE + 4) + 4 + TILE],
                        in_=PQ[mb % 2][0:128, 0:TILE]))
                    E.emit("vector", lambda v, mb=mb, p=p: v.tensor_copy(
                        out=tail_sb[p][:, mb * 4:(mb + 1) * 4],
                        in_=PQ[mb % 2][0:128, TILE - 4:TILE]), inc=(SV, 1))
                    bank["PQ"][pq] = SV.n
                    for tap in range(4):
                        off = mb * (TILE + 4) + 1 + tap
                        if tap == 0:
                            E.emit("vector", lambda v, mb=mb, off=off: v.tensor_scalar(
                                out=conv_sb[:, mb * TILE:(mb + 1) * TILE],
                                in0=mixed_sb[:, off:off + TILE],
                                scalar1=convw_sb[:, mb * 4: mb * 4 + 1],
                                scalar2=None, op0=ALU.mult))
                        else:
                            E.emit("vector", lambda v, mb=mb, off=off, tap=tap: v.scalar_tensor_tensor(
                                out=conv_sb[:, mb * TILE:(mb + 1) * TILE],
                                in0=mixed_sb[:, off:off + TILE],
                                scalar=convw_sb[:, mb * 4 + tap: mb * 4 + tap + 1],
                                in1=conv_sb[:, mb * TILE:(mb + 1) * TILE],
                                op0=ALU.mult, op1=ALU.add),
                                inc=(SV, 1) if tap == 3 else None)
                conv_tok = SV.n
                E.wait("scalar", SV, conv_tok)
                E.emit("scalar", lambda s: s.activation(
                    out=qkv_sb[p][:], in_=conv_sb[:], func=AF.Silu), inc=(SS, 1))
                tk["qkv"][p] = SS.n
                E.wait("scalar", SV, tk["h"][p])
                E.wait("vector", ST, tk["baT"][p])
                for ts in range(2):
                    pq = ts
                    E.wait("tensor", SS, tk["z"][p])
                    E.wait("tensor", SV, bank["PQ"][pq])
                    bwait("tensor", bank["PD"][ts])
                    for ft in range(16):
                        xsl = xn_sb[p][:, ft * TILE + ts * 128: ft * TILE + (ts + 1) * 128]
                        E.emit("tensor", lambda t, x=xsl: t.ldweights(x))
                        E.emit("tensor", lambda t, x=xsl, ft=ft, ts=ts: t.matmul(
                            PQ[ts][0:128, 0:512], x, wz_sb[:, ft * 512:(ft + 1) * 512],
                            start=(ft == 0), stop=(ft == 15)))
                        E.emit("tensor", lambda t, x=xsl, ft=ft, ts=ts: t.matmul(
                            PD[ts][0:128, 0:8], x, wba_sb[:, ft * 8:(ft + 1) * 8],
                            start=(ft == 0), stop=(ft == 15)),
                            inc=(ST, 1) if ft == 15 else None)
                    E.wait("scalar", ST, ST.n)
                    E.emit("scalar", lambda s, ts=ts: s.activation(
                        out=z_sb[p][:, ts * 512:(ts + 1) * 512],
                        in_=PQ[ts][0:128, 0:512], func=AF.Silu), inc=(SS, 1))
                    bank["PQ"][pq] = 0
                    E.wait("vector", ST, ST.n)
                    E.emit("vector", lambda v, ts=ts: v.tensor_copy(
                        out=batm_sb[p][:, ts * 8:(ts + 1) * 8],
                        in_=PD[ts][0:128, 0:8]), inc=(SV, 1))
                    bank["PD"][ts] = (SV, SV.n)
                tk["z"][p] = SS.n
                tk["ba"][p] = SV.n
                tk["zmm"][p] = ST.n

            def rows_stage(j, p):
                E.wait("tensor", SV, tk["ba"][p])
                E.wait("tensor", SV, bank["PR"])
                for ts in range(2):
                    E.emit("tensor", lambda t, ts=ts: t.transpose(
                        PR[0:4, ts * 128:(ts + 1) * 128],
                        batm_sb[p][:, ts * 8: ts * 8 + 4], id128f_sb[:]))
                    E.emit("tensor", lambda t, ts=ts: t.transpose(
                        PR[0:4, 256 + ts * 128: 256 + (ts + 1) * 128],
                        batm_sb[p][:, ts * 8 + 4: ts * 8 + 8], id128f_sb[:]),
                        inc=(ST, 1) if ts == 1 else None)
                tk["baT"][p] = ST.n
                E.wait("vector", ST, ST.n)
                E.emit("vector", lambda v: v.tensor_copy(out=bb_sb[p][:], in_=PR[0:4, 0:256]))
                E.emit("vector", lambda v: v.tensor_copy(out=aa_sb[p][:], in_=PR[0:4, 256:512]),
                       inc=(SV, 1))
                bank["PR"] = SV.n
                E.wait("vector", ST, tk["cpT"][p])
                E.wait("scalar", SV, SV.n)
                E.wait("scalar", DR, tk["mrr"][p])
                E.emit("scalar", lambda s: s.activation(
                    out=scr_sb[0:4, 0:TILE], in_=bb_sb[p][:], func=AF.Tanh, scale=0.5),
                    inc=(SS, 1))
                E.wait("vector", SS, SS.n)
                E.emit("vector", lambda v: v.tensor_scalar(
                    out=beta_sb[p][:], in0=scr_sb[0:4, 0:TILE], scalar1=0.5, scalar2=0.5,
                    op0=ALU.mult, op1=ALU.add), inc=(SV, 1))
                E.emit("vector", lambda v: v.tensor_scalar(
                    out=scr_sb[0:4, 0:TILE], in0=bb_sb[p][:], scalar1=-1.0, scalar2=CLAMP,
                    op0=ALU.mult, op1=ALU.min), inc=(SV, 1))
                E.wait("scalar", SV, SV.n)
                E.emit("scalar", lambda s: s.activation(
                    out=scr_sb[0:4, 0:TILE], in_=scr_sb[0:4, 0:TILE], func=AF.Exp))
                E.emit("scalar", lambda s: s.activation(
                    out=lnbp_sb[:], in_=scr_sb[0:4, 0:TILE], func=AF.Ln, bias=1.0),
                    inc=(SS, 1))
                E.emit("vector", lambda v: v.tensor_scalar(
                    out=scr_sb[0:4, TILE:2 * TILE], in0=aa_sb[p][:], scalar1=dtb_sb[:, :],
                    scalar2=CLAMP, op0=ALU.add, op1=ALU.min), inc=(SV, 1))
                E.wait("scalar", SV, SV.n)
                E.emit("scalar", lambda s: s.activation(
                    out=scr_sb[0:4, TILE:2 * TILE], in_=scr_sb[0:4, TILE:2 * TILE],
                    func=AF.Exp))
                E.emit("scalar", lambda s: s.activation(
                    out=scr_sb[0:4, TILE:2 * TILE], in_=scr_sb[0:4, TILE:2 * TILE],
                    func=AF.Ln, bias=1.0), inc=(SS, 1))
                E.wait("vector", SS, SS.n)
                E.emit("vector", lambda v: v.tensor_scalar(
                    out=scr_sb[0:4, TILE:2 * TILE], in0=scr_sb[0:4, TILE:2 * TILE],
                    scalar1=nega_sb[:, :], scalar2=None, op0=ALU.mult), inc=(SV, 1))
                E.edge("vector", SV)
                for cc in range(4):
                    E.emit("vector", lambda v, cc=cc: v.tensor_tensor_scan(
                        out=gc_sb[p][:, cc * 64:(cc + 1) * 64],
                        data0=scr_sb[0:4, TILE + cc * 64: TILE + (cc + 1) * 64],
                        data1=scr_sb[0:4, TILE + cc * 64: TILE + (cc + 1) * 64],
                        initial=0.0, op0=ALU.add, op1=ALU.bypass))
                E.edge("vector", SV)
                E.emit("vector", lambda v: v.tensor_copy(
                    out=gl_sb[p][:],
                    in_=ap_(gc_sb[p][:], 63, [[TILE, 4], [64, 4], [0, 64]])), inc=(SV, 1))
                E.wait("vector", SS, tk["qkv"][p])
                E.wait("tensor", SS, tk["qkv"][p])
                for rnd in range(2):
                    E.wait("tensor", SV, bank["PR"])
                    for hh in range(2):
                        hqk = rnd * 2 + hh
                        E.wait("vector", ST, ST.n)
                        E.emit("vector", lambda v, hqk=hqk: v.tensor_mul(
                            out=q2b_sb[:], in0=qkv_sb[p][:, hqk * TILE:(hqk + 1) * TILE],
                            in1=qkv_sb[p][:, hqk * TILE:(hqk + 1) * TILE]), inc=(SV, 1))
                        E.wait("tensor", SV, SV.n)
                        E.emit("tensor", lambda t, hh=hh: t.matmul(
                            PR[0:1, hh * TILE:(hh + 1) * TILE],
                            ones_sb[:], q2b_sb[:],
                            start=True, stop=True), inc=(ST, 1))
                    E.wait("vector", ST, ST.n)
                    E.emit("vector", lambda v, rnd=rnd: v.tensor_copy(
                        out=l2p_sb[:, rnd * 2 * TILE:(rnd + 1) * 2 * TILE],
                        in_=PR[0:1, 0:2 * TILE]), inc=(SV, 1))
                    bank["PR"] = SV.n
                E.wait("scalar", SV, SV.n)
                E.emit("scalar", lambda s: s.activation(
                    out=l2p_sb[:], in_=l2p_sb[:], func=AF.Sqrt), inc=(SS, 1))
                E.wait("vector", SS, SS.n)
                E.emit("vector", lambda v: v.tensor_scalar(
                    out=l2p_sb[:], in0=l2p_sb[:], scalar1=1e-6, scalar2=None,
                    op0=ALU.add), inc=(SV, 1))
                E.wait("scalar", SV, SV.n)
                E.emit("scalar", lambda s: s.activation(
                    out=l2p_sb[:], in_=l2p_sb[:], func=AF.Ln), inc=(SS, 1))
                lnn_tok = SS.n
                E.wait("sync", SS, lnn_tok)
                E.wait("sync", SV, tk["rows"][p])
                for g in range(2):
                    for khx in range(2):
                        E.emit("sync", lambda s, g=g, khx=khx: s.dma_start(
                            out=l2hv_sb[p][2 * khx: 2 * khx + 2, g * TILE:(g + 1) * TILE],
                            in_=ap_(l2p_sb[:], g * 2 * TILE + khx * TILE,
                                    [[4 * TILE, 1], [0, 2], [1, TILE]])), inc=(DRL[p], 16))
                l2hv_tok = DRL[p].n
                E.wait("vector", DRL[p], l2hv_tok)
                rw = rws_sb[p]
                lnnq = l2hv_sb[p][:, 0:TILE]
                lnnk = l2hv_sb[p][:, TILE:2 * TILE]
                E.emit("vector", lambda v: v.scalar_tensor_tensor(
                    out=rw[:, 0:TILE], in0=gc_sb[p][:], scalar=-1.0, in1=lnnk,
                    op0=ALU.mult, op1=ALU.subtract))
                E.emit("vector", lambda v: v.tensor_sub(
                    out=scr_sb[0:4, 0:TILE], in0=gc_sb[p][:], in1=lnnk))
                E.emit("vector", lambda v: v.tensor_sub(
                    out=rw[:, TILE:2 * TILE], in0=scr_sb[0:4, 0:TILE], in1=lnbp_sb[:]))
                E.emit("vector", lambda v: v.tensor_sub(
                    out=scr_sb[0:4, 0:TILE], in0=gc_sb[p][:], in1=lnnq))
                E.emit("vector", lambda v: v.tensor_scalar(
                    out=rw[:, 2 * TILE:3 * TILE], in0=scr_sb[0:4, 0:TILE], scalar1=LNDKI,
                    scalar2=None, op0=ALU.add))
                E.emit("vector", lambda v: v.tensor_sub(
                    out=scr_sb[0:4, 0:TILE], in0=gl_sb[p][:], in1=gc_sb[p][:]))
                E.emit("vector", lambda v: v.tensor_sub(
                    out=rw[:, 3 * TILE:4 * TILE], in0=scr_sb[0:4, 0:TILE], in1=lnnk),
                    inc=(SV, 1))
                tk["rows"][p] = SV.n
                E.wait("sync", SV, tk["rows"][p])
                E.emit("sync", lambda s: s.dma_start(
                    out=ap_(rows_h, 2 * T_ALL + j * TILE, [[3 * T_ALL, 4], [T_ALL, 3], [1, TILE]]),
                    in_=ap_(rw[:], 0, [[4 * TILE, 4], [TILE, 3], [1, TILE]])), inc=(DR, 16))
                E.emit("sync", lambda s: s.dma_start(
                    out=ap_(rows_h, 14 * T_ALL + j * TILE, [[T_ALL, 4], [1, TILE]]),
                    in_=gl_sb[p][:]), inc=(DR, 16))
                E.wait("sync", DR, DR.n)
                E.wait("sync", SV, tk["maskv"][p])
                E.wait("sync", SV, tk["opb"][p])
                E.emit("sync", lambda s: s.dma_start(
                    out=arep_sb[p][:].rearrange("q (h t) -> q h t", t=TILE),
                    in_=ap_(rows_h, 2 * T_ALL + j * TILE, [[0, 128], [3 * T_ALL, 4], [1, TILE]])),
                    inc=(DRR[p], 16))
                E.emit("sync", lambda s: s.dma_start(
                    out=carep_sb[p][:].rearrange("q (h t) -> q h t", t=TILE),
                    in_=ap_(rows_h, 3 * T_ALL + j * TILE, [[0, 128], [3 * T_ALL, 4], [1, TILE]])),
                    inc=(DRR[p], 16))
                E.emit("sync", lambda s: s.dma_start(
                    out=srep_sb[p][:].rearrange("q (h t) -> q h t", t=TILE),
                    in_=ap_(rows_h, 4 * T_ALL + j * TILE, [[0, 128], [3 * T_ALL, 4], [1, TILE]])),
                    inc=(DRR[p], 16))
                E.emit("sync", lambda s: s.dma_start(
                    out=ap_(rows_h, 18 * T_ALL + j * 16, [[4, 4], [1, 4]]),
                    in_=ap_(gl_sb[p][:], 0, [[TILE, 4], [64, 4], [1, 1]]),
                    allow_slow_non_contiguous=True), inc=(DR, 16))
                E.wait("sync", DR, DR.n)
                E.emit("sync", lambda s: s.dma_start(
                    out=egrep_sb[p][:],
                    in_=ap_(rows_h, 18 * T_ALL + j * 16, [[0, 128], [1, 16]])),
                    inc=(DRR[p], 16))
                tk["rep"][p] = DRR[p].n
                E.wait("scalar", DRR[p], tk["rep"][p])
                E.emit("scalar", lambda s: s.activation(
                    out=egrep_sb[p][:], in_=egrep_sb[p][:], func=AF.Exp), inc=(SS, 1))
                tk["eg"][p] = SS.n

            def cp_stage(j, p):
                E.wait("tensor", SV, tk["rows"][p])
                E.wait("tensor", SV, bank["PR"])
                rw = rws_sb[p]
                srcs = [rw[:, TILE:2 * TILE], rw[:, 2 * TILE:3 * TILE],
                        rw[:, 0:TILE], beta_sb[p][:], rw[:, 3 * TILE:4 * TILE]]
                slots = [0, 1, 2, 3, 6]
                nt = 0
                for ts in range(2):
                    for b_i, src in enumerate(srcs):
                        slot = slots[b_i]
                        nt += 1
                        E.emit("tensor", lambda t, ts=ts, slot=slot, src=src: t.transpose(
                            PR[0:128, ts * 32 + slot * 4: ts * 32 + slot * 4 + 4],
                            src[:, ts * 128:(ts + 1) * 128], id128f_sb[0:4, 0:4]),
                            inc=(ST, 1) if nt == 10 else None)
                tk["cpT"][p] = ST.n
                E.wait("vector", ST, ST.n)
                E.wait("vector", SV, tk["opb"][p])
                E.emit("vector", lambda v: v.tensor_copy(
                    out=ap_(cp_sb[p][:], 0, [[64, 128], [32, 2], [1, 16]]),
                    in_=ap_(PR[:], 0, [[512, 128], [32, 2], [1, 16]])))
                E.emit("vector", lambda v: v.tensor_copy(
                    out=ap_(cp_sb[p][:], 24, [[64, 128], [32, 2], [1, 4]]),
                    in_=ap_(PR[:], 24, [[512, 128], [32, 2], [1, 4]])), inc=(SV, 1))
                bank["PR"] = SV.n
                E.wait("scalar", SV, SV.n)
                for ts in range(2):
                    for dst_b, src_b in [(4, 0), (5, 1), (6, 6)]:
                        E.emit("scalar", lambda s, ts=ts, dst_b=dst_b, src_b=src_b: s.activation(
                            out=cp_sb[p][:, ts * 32 + dst_b * 4: ts * 32 + dst_b * 4 + 4],
                            in_=cp_sb[p][:, ts * 32 + src_b * 4: ts * 32 + src_b * 4 + 4],
                            func=AF.Exp), inc=(SS, 1))
                tk["cp"][p] = SS.n

            def tm_stage(j, p):
                E.wait("tensor", SS, tk["qkv"][p])
                E.wait("scalar", SV, tk["opb"][p])
                for ts in range(2):
                    bwait("tensor", bank["PT"])
                    for kh in range(2):
                        E.emit("tensor", lambda t, ts=ts, kh=kh: t.transpose(
                            PT[0:128, kh * 128:(kh + 1) * 128],
                            qkv_sb[p][:, (2 + kh) * TILE + ts * 128: (2 + kh) * TILE + (ts + 1) * 128],
                            id128_sb[:]), inc=(ST, 1) if kh == 1 else None)
                    E.wait("scalar", ST, ST.n)
                    E.emit("scalar", lambda s, ts=ts: s.activation(
                        out=ktm_sb[p][:, ts * 256:(ts + 1) * 256],
                        in_=PT[0:128, 0:256], func=AF.Copy), inc=(SS, 1))
                    bank["PT"] = (SS, SS.n)
                    bwait("tensor", bank["PT"])
                    for hv in range(4):
                        E.emit("tensor", lambda t, ts=ts, hv=hv: t.transpose(
                            PT[0:128, hv * 128:(hv + 1) * 128],
                            qkv_sb[p][:, (4 + hv) * TILE + ts * 128: (4 + hv) * TILE + (ts + 1) * 128],
                            id128_sb[:]), inc=(ST, 1) if hv == 3 else None)
                    E.wait("scalar", ST, ST.n)
                    E.emit("scalar", lambda s, ts=ts: s.activation(
                        out=vtm_sb[p][:, ts * 512:(ts + 1) * 512],
                        in_=PT[0:128, 0:512], func=AF.Copy), inc=(SS, 1))
                    bank["PT"] = (SS, SS.n)
                tk["tm"][p] = SS.n

            def delta_parts(j, p, cc):
                dt = {}
                ts = cc // 2
                q0 = 64 * (cc % 2)
                pdp = cc % 2
                tok0 = cc * 64

                def kblk(kh):
                    return qkv_sb[p][:, (2 + kh) * TILE + tok0: (2 + kh) * TILE + tok0 + 64]

                def qblk(kh):
                    return qkv_sb[p][:, kh * TILE + tok0: kh * TILE + tok0 + 64]

                def hv4(buf, w, lo=None):
                    lo = q0 if lo is None else lo
                    return ap_(buf[:], lo * (4 * w), [[4 * w, 64], [w, 4], [1, w]])

                def cpc(b_i, w):
                    return ap_(cp_sb[p][:], q0 * 64 + ts * 32 + b_i * 4,
                               [[64, 64], [1, 4], [0, w]])

                def pre():
                 E.wait("tensor", SS, tk["qkv"][p])
                 bwait("tensor", bank["PD"][pdp])
                 for kh in range(2):
                     E.emit("tensor", lambda t, kh=kh: t.ldweights(kblk(kh), tile_position=(0, q0)))
                     E.emit("tensor", lambda t, kh=kh: t.matmul(
                         PD[pdp][q0:q0 + 64, kh * 64:(kh + 1) * 64], kblk(kh), kblk(kh),
                         start=True, stop=True, tile_position=(0, q0)))
                     E.emit("tensor", lambda t, kh=kh: t.matmul(
                         PD[pdp][q0:q0 + 64, 128 + kh * 64: 128 + (kh + 1) * 64],
                         kblk(kh), qblk(kh),
                         start=True, stop=True, tile_position=(0, q0)),
                         inc=(ST, 1) if kh == 1 else None)
                 gs_tok = ST.n

                 E.wait("vector", DRR[p], tk["rep"][p])
                 E.wait("vector", SS, tk["cp"][p])
                 specs = [(mka_sb, arep_sb[p], 0, tlo_sb),
                          (mkc_sb, carep_sb[p], 2, thi_sb),
                          (mks_sb, srep_sb[p], 2, thii_sb)]
                 for mi, (mk, rep, colb, tmpl) in enumerate(specs):
                     repv = ap_(rep[:], q0 * (4 * TILE) + tok0,
                                [[4 * TILE, 64], [TILE, 4], [1, 64]])
                     E.emit("vector", lambda v, mk=mk, repv=repv, colb=colb: v.tensor_tensor(
                         out=hv4(mk, 64), in0=repv, in1=cpc(colb, 64), op=ALU.add))
                     E.emit("vector", lambda v, mk=mk: v.tensor_scalar(
                         out=mk[q0:q0 + 64, :], in0=mk[q0:q0 + 64, :], scalar1=CLAMP,
                         scalar2=None, op0=ALU.min),
                         inc=(SV, 1) if mi == 2 else None)
                 E.wait("scalar", SV, SV.n)
                 for mi, (mk, rep, colb, tmpl) in enumerate(specs):
                     E.emit("scalar", lambda s, mk=mk: s.activation(
                         out=mk[q0:q0 + 64, :], in_=mk[q0:q0 + 64, :], func=AF.Exp),
                         inc=(SS, 1) if mi == 2 else None)
                 E.wait("vector", SS, SS.n)
                 for mk, rep, colb, tmpl in specs:
                     tmv = ap_(tmpl[:], q0 * 64, [[64, 64], [0, 4], [1, 64]])
                     E.emit("vector", lambda v, mk=mk, tmv=tmv: v.tensor_tensor(
                         out=hv4(mk, 64), in0=hv4(mk, 64), in1=tmv, op=ALU.mult),
                         inc=(SV, 1))
                 tk["maskv"][p] = SV.n

                 E.wait("vector", ST, gs_tok)

                 def khpair(buf, kh, w=64):
                     return ap_(buf[:], q0 * (4 * w) + kh * 2 * w,
                                [[4 * w, 64], [w, 2], [1, w]])

                 for kh in range(2):
                     gd = ap_(PD[pdp][:], q0 * 512 + kh * 64, [[512, 64], [0, 2], [1, 64]])
                     sd = ap_(PD[pdp][:], q0 * 512 + 128 + kh * 64, [[512, 64], [0, 2], [1, 64]])
                     E.emit("vector", lambda v, kh=kh, gd=gd: v.tensor_tensor(
                         out=khpair(ra_sb, kh), in0=gd, in1=khpair(mka_sb, kh), op=ALU.mult))
                     E.emit("vector", lambda v, kh=kh, gd=gd: v.tensor_tensor(
                         out=khpair(ca_sb, kh), in0=gd, in1=khpair(mkc_sb, kh), op=ALU.mult))
                     E.emit("vector", lambda v, kh=kh, sd=sd: v.tensor_tensor(
                         out=khpair(sc_sb, kh), in0=sd, in1=khpair(mks_sb, kh), op=ALU.mult),
                         inc=(SV, 1) if kh == 1 else None)
                 bank["PD"][pdp] = (SV, SV.n)
                 racasc_tok = SV.n

                 def lvlR(lvl, hv):
                     if lvl < 0:
                         return ap_(ra_sb[:], q0 * 256 + hv * 64, [[256, 64], [1, 64]])
                     return ap_(pw_sb[:], q0 * 2048 + lvl * 512 + hv * 128, [[2048, 64], [1, 64]])

                 def lvlC(lvl, hv):
                     if lvl < 0:
                         return ap_(ca_sb[:], q0 * 256 + hv * 64, [[256, 64], [1, 64]])
                     return ap_(pw_sb[:], q0 * 2048 + lvl * 512 + hv * 128 + 64, [[2048, 64], [1, 64]])

                 E.wait("tensor", SV, racasc_tok)
                 for lvl in range(4):
                     lvl_bank = 1 - pdp if lvl % 2 == 0 else pdp
                     bwait("tensor", bank["PD"][lvl_bank])
                     if lvl > 0:
                         E.wait("tensor", SS, SS.n)
                     for hv in range(4):
                         lC = lvlC(lvl - 1, hv)
                         lR = lvlR(lvl - 1, hv)
                         E.emit("tensor", lambda t, lC=lC: t.ldweights(lC, tile_position=(q0, q0)))
                         E.emit("tensor", lambda t, lC=lC, lR=lR, hv=hv, lvl_bank=lvl_bank: t.matmul(
                             PD[lvl_bank][q0:q0 + 64, hv * 128: hv * 128 + 64], lC, lR,
                             start=True, stop=True, tile_position=(q0, q0)))
                         E.emit("tensor", lambda t, lR=lR: t.ldweights(lR, tile_position=(q0, q0)))
                         E.emit("tensor", lambda t, lC=lC, lR=lR, hv=hv, lvl_bank=lvl_bank: t.matmul(
                             PD[lvl_bank][q0:q0 + 64, hv * 128 + 64: hv * 128 + 128], lR, lC,
                             start=True, stop=True, tile_position=(q0, q0)),
                             inc=(ST, 1) if hv == 3 else None)
                     E.wait("scalar", ST, ST.n)
                     E.emit("scalar", lambda s, lvl=lvl, lvl_bank=lvl_bank: s.activation(
                         out=pw_sb[q0:q0 + 64, lvl * 512:(lvl + 1) * 512],
                         in_=PD[lvl_bank][q0:q0 + 64, 0:512], func=AF.Copy), inc=(SS, 1))
                     bank["PD"][lvl_bank] = (SS, SS.n)

                 def pwR(lvl, hv):
                     return lvlR(lvl, hv)

                 def pwC(lvl, hv):
                     return lvlC(lvl, hv)

                 bwait("tensor", bank["PD"][pdp])
                 E.wait("tensor", SS, SS.n)
                 for hv in range(4):
                     E.emit("tensor", lambda t, hv=hv: t.ldweights(pwC(3, hv), tile_position=(q0, q0)))
                     E.emit("tensor", lambda t, hv=hv: t.matmul(
                         PD[pdp][q0:q0 + 64, hv * 64:(hv + 1) * 64], pwC(3, hv), pwR(3, hv),
                         start=True, stop=True, tile_position=(q0, q0)),
                         inc=(ST, 1) if hv == 3 else None)
                 E.wait("scalar", ST, ST.n)
                 E.emit("scalar", lambda s: s.activation(
                     out=r32_sb[q0:q0 + 64, :], in_=PD[pdp][q0:q0 + 64, 0:256],
                     func=AF.Copy), inc=(SS, 1))
                 bank["PD"][pdp] = (SS, SS.n)

                 E.wait("vector", SS, SS.n)
                 eyev = ap_(eye2_sb[:], q0 * 64, [[64, 64], [0, 4], [1, 64]])
                 E.emit("vector", lambda v: v.tensor_tensor(
                     out=hv4(chn_sb, 64), in0=hv4(ca_sb, 64), in1=eyev, op=ALU.add),
                     inc=(SV, 1))
                 for step in range(5):
                     cb = step % 2
                     E.wait("tensor", SV, SV.n)
                     bwait("tensor", bank["PD"][cb])
                     for hv in range(4):
                         if step < 4:
                             lhs = ap_(pw_sb[:], q0 * 2048 + step * 512 + hv * 128,
                                       [[2048, 64], [1, 64]])
                         else:
                             lhs = ap_(r32_sb[:], q0 * 256 + hv * 64, [[256, 64], [1, 64]])
                         rhs = ap_(chn_sb[:], q0 * 256 + hv * 64, [[256, 64], [1, 64]])
                         E.emit("tensor", lambda t, lhs=lhs: t.ldweights(lhs, tile_position=(q0, q0)))
                         E.emit("tensor", lambda t, lhs=lhs, rhs=rhs, hv=hv, cb=cb: t.matmul(
                             PD[cb][q0:q0 + 64, hv * 64:(hv + 1) * 64], lhs, rhs,
                             start=True, stop=True, tile_position=(q0, q0)),
                             inc=(ST, 1) if hv == 3 else None)
                     E.wait("vector", ST, ST.n)
                     E.emit("vector", lambda v, cb=cb: v.scalar_tensor_tensor(
                         out=chn_sb[q0:q0 + 64, :], in0=PD[cb][q0:q0 + 64, 0:256],
                         scalar=0.0, in1=chn_sb[q0:q0 + 64, :],
                         op0=ALU.bypass, op1=ALU.add), inc=(SV, 1))
                     bank["PD"][cb] = (SV, SV.n)
                 chn_tok = SV.n

                 E.wait("vector", SS, tk["tm"][p])
                 vtmv = ap_(vtm_sb[p][:], q0 * 1024 + ts * 512, [[1024, 64], [128, 4], [1, 128]])
                 E.emit("vector", lambda v: v.tensor_tensor(
                     out=hv4(vb_sb, 128), in0=vtmv, in1=cpc(3, 128), op=ALU.mult))

                 def cpc2(b_i, kh, w):
                     return ap_(cp_sb[p][:], q0 * 64 + ts * 32 + b_i * 4 + kh * 2,
                                [[64, 64], [1, 2], [0, w]])

                 for kh in range(2):
                     ktd = ap_(ktm_sb[p][:], q0 * 512 + ts * 256 + kh * 128,
                               [[512, 64], [0, 2], [1, 128]])
                     kep = ap_(ke_sb[:], q0 * 512 + kh * 2 * 128, [[512, 64], [128, 2], [1, 128]])
                     k2p = ap_(k2t_sb[:], q0 * 512 + kh * 2 * 128, [[512, 64], [128, 2], [1, 128]])
                     E.emit("vector", lambda v, kh=kh, ktd=ktd, kep=kep: v.scalar_tensor_tensor(
                         out=kep, in0=ktd, scalar=-1.0, in1=cpc2(4, kh, 128),
                         op0=ALU.mult, op1=ALU.mult))
                     E.emit("vector", lambda v, kh=kh, ktd=ktd, k2p=k2p: v.tensor_tensor(
                         out=k2p, in0=ktd, in1=cpc2(6, kh, 128), op=ALU.mult),
                         inc=(SV, 1) if kh == 1 else None)
                 opb_tok = SV.n
                 tk["opb"][p] = SV.n

                 E.wait("tensor", SV, opb_tok)
                 bwait("tensor", bank["PD"][pdp])
                 for hv in range(4):
                     ksl = ap_(ke_sb[:], q0 * 512 + hv * 128, [[512, 64], [1, 128]])
                     tsl = ap_(chn_sb[:], q0 * 256 + hv * 64, [[256, 64], [1, 64]])
                     E.emit("tensor", lambda t, ksl=ksl: t.ldweights(ksl, tile_position=(q0, 0)))
                     E.emit("tensor", lambda t, ksl=ksl, tsl=tsl, hv=hv: t.matmul(
                         PD[pdp][0:128, hv * 64:(hv + 1) * 64], ksl, tsl,
                         start=True, stop=True, tile_position=(q0, 0)),
                         inc=(ST, 1) if hv == 3 else None)
                 E.wait("scalar", ST, ST.n)
                 E.emit("scalar", lambda s: s.activation(
                     out=kcd_sb[:, 0:256], in_=PD[pdp][0:128, 0:256], func=AF.Copy),
                     inc=(SS, 1))
                 bank["PD"][pdp] = (SS, SS.n)
                 dt["kcd_tok"] = SS.n

                def spine():
                 kcd_tok = dt["kcd_tok"]
                 E.wait("tensor", SS, kcd_tok)
                 E.wait("tensor", SV, bank["PV"])
                 E.wait("tensor", SV, state_tok[p])
                 E.wait("tensor", SS, statebf_tok[p])
                 for hv in range(4):
                     tsl = ap_(chn_sb[:], q0 * 256 + hv * 64, [[256, 64], [1, 64]])
                     vsl = ap_(vb_sb[:], q0 * 512 + hv * 128, [[512, 64], [1, 128]])
                     E.emit("tensor", lambda t, tsl=tsl: t.ldweights(tsl, tile_position=(q0, q0)))
                     E.emit("tensor", lambda t, tsl=tsl, vsl=vsl, hv=hv: t.matmul(
                         PV[q0:q0 + 64, hv * 128:(hv + 1) * 128], tsl, vsl,
                         start=True, stop=False, tile_position=(q0, q0)))
                     ksl = kcd_sb[:, hv * 64:(hv + 1) * 64]
                     E.emit("tensor", lambda t, ksl=ksl: t.ldweights(ksl, tile_position=(0, q0)))
                     E.emit("tensor", lambda t, ksl=ksl, hv=hv: t.matmul(
                         PV[q0:q0 + 64, hv * 128:(hv + 1) * 128], ksl,
                         stbf_sb[p][:, hv * 128:(hv + 1) * 128],
                         start=False, stop=True, tile_position=(0, q0)),
                         inc=(ST, 1) if hv == 3 else None)
                 E.wait("vector", ST, ST.n)
                 E.emit("vector", lambda v: v.tensor_copy(
                     out=vn_sb[q0:q0 + 64, :], in_=PV[q0:q0 + 64, 0:512]), inc=(SV, 1))
                 bank["PV"] = SV.n
                 vn_tok = SV.n
                 bwait("tensor", bank["PD"][pdp])
                 for hv in range(4):
                     qb = qblk(hv // 2)
                     E.emit("tensor", lambda t, qb=qb: t.ldweights(qb, tile_position=(0, q0)))
                     E.emit("tensor", lambda t, qb=qb, hv=hv: t.matmul(
                         PD[pdp][q0:q0 + 64, hv * 128:(hv + 1) * 128], qb,
                         stbf_sb[p][:, hv * 128:(hv + 1) * 128],
                         start=True, stop=True, tile_position=(0, q0)),
                         inc=(ST, 1) if hv == 3 else None)
                 inter_tok = ST.n
                 tk["deltaq"][p] = ST.n
                 pint_tok = 0
                 if "pint" in dbg and j == ntile - 1:
                     E.wait("scalar", ST, inter_tok)
                     E.emit("scalar", lambda s: s.activation(
                         out=pint_sb[q0:q0 + 64, :], in_=PD[pdp][q0:q0 + 64, 0:512],
                         func=AF.Copy), inc=(SS, 1))
                     pint_tok = SS.n
                 E.wait("tensor", SV, vn_tok)
                 bwait("tensor", bank["PO"])
                 for hv in range(4):
                     ssl = ap_(sc_sb[:], q0 * 256 + hv * 64, [[256, 64], [1, 64]])
                     vsl = ap_(vn_sb[:], q0 * 512 + hv * 128, [[512, 64], [1, 128]])
                     E.emit("tensor", lambda t, ssl=ssl: t.ldweights(ssl, tile_position=(q0, q0)))
                     E.emit("tensor", lambda t, ssl=ssl, vsl=vsl, hv=hv: t.matmul(
                         PO[q0:q0 + 64, hv * 128:(hv + 1) * 128], ssl, vsl,
                         start=True, stop=True, tile_position=(q0, q0)),
                         inc=(ST, 1) if hv == 3 else None)
                 attnv_tok = ST.n
                 patt_tok = 0
                 if "patt" in dbg and j == ntile - 1:
                     E.wait("scalar", ST, attnv_tok)
                     E.emit("scalar", lambda s: s.activation(
                         out=patt_sb[q0:q0 + 64, :], in_=PO[q0:q0 + 64, 0:512],
                         func=AF.Copy), inc=(SS, 1))
                     patt_tok = SS.n
                 E.wait("vector", ST, inter_tok)
                 pdv = ap_(PD[pdp][:], q0 * 512, [[512, 64], [128, 4], [1, 128]])
                 E.emit("vector", lambda v: v.tensor_tensor(
                     out=hv4(o2_sb, 128), in0=pdv, in1=cpc(5, 128), op=ALU.mult),
                     inc=(SV, 1))
                 bank["PD"][pdp] = ([(SV, SV.n), (SS, pint_tok)]
                                    if pint_tok else (SV, SV.n))
                 E.wait("vector", ST, attnv_tok)
                 E.emit("vector", lambda v: v.tensor_tensor(
                     out=o2_sb[q0:q0 + 64, :], in0=o2_sb[q0:q0 + 64, :],
                     in1=PO[q0:q0 + 64, 0:512], op=ALU.add), inc=(SV, 1))
                 bank["PO"] = ([(SV, SV.n), (SS, patt_tok)]
                               if patt_tok else (SV, SV.n))
                 oi_tok = SV.n
                 dt["oi_tok"] = oi_tok
                 if "o2pre" in dbg and j == ntile - 1:
                     E.wait("sync", SV, oi_tok)
                     E.emit("sync", lambda s: s.dma_start(
                         out=ap_(dbg["o2pre"], q0 * 512, [[512, 64], [1, 512]]),
                         in_=o2_sb[q0:q0 + 64, :]), inc=(DR, 16))
                     dt["o2pre_tok"] = DR.n
                 E.wait("tensor", SV, vn_tok)
                 E.wait("tensor", SV, bank["PV"])
                 for hv in range(4):
                     k2sl = ap_(k2t_sb[:], q0 * 512 + hv * 128, [[512, 64], [1, 128]])
                     vsl = ap_(vn_sb[:], q0 * 512 + hv * 128, [[512, 64], [1, 128]])
                     E.emit("tensor", lambda t, k2sl=k2sl: t.ldweights(k2sl, tile_position=(q0, 0)))
                     E.emit("tensor", lambda t, k2sl=k2sl, vsl=vsl, hv=hv: t.matmul(
                         PV[0:128, hv * 128:(hv + 1) * 128], k2sl, vsl,
                         start=True, stop=True, tile_position=(q0, 0)),
                         inc=(ST, 1) if hv == 3 else None)
                 ds_tok = ST.n
                 E.wait("vector", SS, tk["eg"][p])
                 E.wait("vector", ST, ds_tok)
                 for hv in range(4):
                     E.emit("vector", lambda v, hv=hv: v.tensor_scalar(
                         out=st_sb[p][:, hv * 128:(hv + 1) * 128],
                         in0=st_sb[p][:, hv * 128:(hv + 1) * 128],
                         scalar1=egrep_sb[p][:, cc + 4 * hv: cc + 4 * hv + 1],
                         scalar2=None, op0=ALU.mult))
                 E.edge("vector", SV)
                 E.emit("vector", lambda v: v.tensor_add(
                     out=st_sb[p][:], in0=st_sb[p][:], in1=PV[0:128, 0:512]),
                     inc=(SV, 1))
                 bank["PV"] = SV.n
                 E.wait("scalar", SV, SV.n)
                 E.emit("scalar", lambda s: s.activation(
                     out=stbf_sb[p][:], in_=st_sb[p][:], func=AF.Copy), inc=(SS, 1))
                 state_tok[p] = SV.n
                 statebf_tok[p] = SS.n

                def tail():
                 oi_tok = dt["oi_tok"]
                 E.wait("scalar", SV, oi_tok)
                 for rnd in range(2):
                     if rnd == 1:
                         E.wait("scalar", SV, SV.n)
                     E.emit("scalar", lambda s, rnd=rnd: s.activation(
                         out=rsq32_sb[q0:q0 + 64, :],
                         in_=o2_sb[q0:q0 + 64, rnd * 256:(rnd + 1) * 256],
                         func=AF.Square), inc=(SS, 1))
                     E.wait("vector", SS, SS.n)
                     for hh in range(2):
                         E.emit("vector", lambda v, rnd=rnd, hh=hh: v.reduce_sum(
                             out=rms_sb[q0:q0 + 64, rnd * 2 + hh: rnd * 2 + hh + 1],
                             in_=rsq32_sb[q0:q0 + 64, hh * 128:(hh + 1) * 128],
                             axis=mybir.AxisListType.X),
                             inc=(SV, 1) if hh == 1 else None)
                 E.edge("vector", SV)
                 E.emit("vector", lambda v: v.tensor_scalar(
                     out=rms_sb[q0:q0 + 64, 4:8], in0=rms_sb[q0:q0 + 64, 0:4],
                     scalar1=1.0 / 128.0, scalar2=1e-6, op0=ALU.mult, op1=ALU.add),
                     inc=(SV, 1))
                 if "rmsv" in dbg:
                     E.emit("vector", lambda v: v.tensor_copy(
                         out=rms_sb[q0:q0 + 64, 8:12], in_=rms_sb[q0:q0 + 64, 4:8]))
                 E.wait("scalar", SV, SV.n)
                 E.emit("scalar", lambda s: s.activation(
                     out=rms_sb[q0:q0 + 64, 4:8], in_=rms_sb[q0:q0 + 64, 4:8],
                     func=AF.Sqrt), inc=(SS, 1))
                 E.wait("vector", SS, SS.n)
                 if "rmsv" in dbg:
                     E.emit("vector", lambda v: v.tensor_copy(
                         out=rms_sb[q0:q0 + 64, 12:16], in_=rms_sb[q0:q0 + 64, 4:8]))
                 E.emit("vector", lambda v: v.reciprocal(
                     out=rms_sb[q0:q0 + 64, 4:8], in_=rms_sb[q0:q0 + 64, 4:8]))
                 E.edge("vector", SV)
                 if "o2pre" in dbg and j == ntile - 1:
                     E.wait("vector", DR, dt["o2pre_tok"])
                 rsqv = ap_(rms_sb[:], q0 * 20 + 4, [[20, 64], [1, 4], [0, 128]])
                 E.emit("vector", lambda v: v.tensor_tensor(
                     out=hv4(o2_sb, 128), in0=hv4(o2_sb, 128), in1=rsqv, op=ALU.mult))
                 E.wait("vector", SS, tk["z"][p])
                 E.wait("vector", DHS[p], tk["hst"][p])
                 E.emit("vector", lambda v: v.tensor_tensor(
                     out=hbuf_sb[p][q0:q0 + 64, ts * 512:(ts + 1) * 512],
                     in0=o2_sb[q0:q0 + 64, :],
                     in1=z_sb[p][q0:q0 + 64, ts * 512:(ts + 1) * 512], op=ALU.mult),
                     inc=(SV, 1))
                 tk["h"][p] = SV.n
                return pre, spine, tail

            half = ntile // 2 if ntile >= 2 else 1
            if ntile >= 2:
                order = []
                for k in range(half):
                    order.append(k)
                    order.append(half + k)
                order += list(range(2 * half, ntile))
            else:
                order = list(range(ntile))
            def h_store(jj, p, ts):
                E.wait("gpsimd", SV, tk["h"][p])
                E.emit("gpsimd", lambda g, jj=jj, ts=ts, p=p: g.dma_start(
                    out=h_src[jj * TILE + ts * 128: jj * TILE + (ts + 1) * 128, :],
                    in_=hbuf_sb[p][:, ts * 512:(ts + 1) * 512]), inc=(DHS[p], 16))
                tk["hst"][p] = DHS[p].n

            prev = None
            for s, jj in enumerate(order):
                p = s % 2
                ln_stage(jj, p)
                xn_stage(jj, p)
                if s + 2 < len(order):
                    E.wait("gpsimd", SV, tk["xn"][p])
                    load_x(order[s + 2], p)
                    x_tok[p] = SGX[p].n
                fe_stage(jj, p)
                rows_stage(jj, p)
                cp_stage(jj, p)
                tm_stage(jj, p)
                parts = [delta_parts(jj, p, cc) for cc in range(4)]
                parts[0][0]()
                parts[0][1]()
                parts[1][0]()
                parts[0][2]()
                parts[1][1]()
                parts[2][0]()
                parts[1][2]()
                h_store(jj, p, 0)
                parts[2][1]()
                parts[3][0]()
                parts[2][2]()
                parts[3][1]()
                parts[3][2]()
                h_store(jj, p, 1)
                prev = (jj, p)

            E.wait("gpsimd", DHS[0], DHS[0].n)
            E.wait("gpsimd", DHS[1], DHS[1].n)
            if "h_src" in dbg:
                E.wait("gpsimd", DH, DH.n)
                E.emit("gpsimd", lambda g: g.dma_start(
                    out=ap_(dbg["h_src"], 0, [[512, ntile * TILE], [1, 512]]),
                    in_=h_src[0:ntile * TILE, :]), inc=(DH, 16))
            if "qkv0" in dbg:
                E.emit("gpsimd", lambda g: g.dma_start(
                    out=dbg["qkv0"], in_=qkv_sb[(ntile - 1) % 2][:]), inc=(DH, 16))
            lpar = (ntile - 1) % 2
            for nm, src in [("xn0", lambda: xn_sb[lpar][:]),
                            ("z0", lambda: z_sb[lpar][:]),
                            ("cp0", lambda: cp_sb[lpar][:]),
                            ("stf", lambda: st_sb[(ntile - 1) % 2][:]),
                            ("o20", lambda: o2_sb[:]),
                            ("vn0", lambda: vn_sb[:]),
                            ("chn0", lambda: chn_sb[:]),
                            ("kcd0", lambda: kcd_sb[:]),
                            ("pint", lambda: pint_sb[:]),
                            ("patt", lambda: patt_sb[:]),
                            ("rmsv", lambda: rms_sb[:])]:
                if nm in dbg:
                    E.wait("gpsimd", SV, SV.n)
                    E.wait("gpsimd", SS, SS.n)
                    E.wait("gpsimd", ST, ST.n)
                    E.emit("gpsimd", lambda g, nm=nm, src=src: g.dma_start(
                        out=dbg[nm], in_=src()), inc=(DH, 16))
            if "rows" in dbg:
                E.wait("gpsimd", DR, DR.n)
                E.wait("gpsimd", SV, SV.n)
                E.emit("gpsimd", lambda g: g.dma_start(
                    out=ap_(dbg["rows"], 0, [[T_ALL, 20], [1, ntile * TILE]]),
                    in_=ap_(rows_h, 0, [[T_ALL, 20], [1, ntile * TILE]])), inc=(DH, 16))

            E.wait("gpsimd", DH, DH.n)
            if phase2:
                E.emit("gpsimd", lambda g: g.collective_compute(
                    "AllGather", ALU.bypass,
                    ins=[h_src[:, :]], outs=[h_all[:, :, :]],
                    replica_groups=[list(range(NCORE))]), inc=(CC, 1))

            p1_join = dict(t=ST.n, v=SV.n, s=SS.n, dx=DX.n, dr=DR.n, dh=DH.n, g=SG.n,
                           sgx0=SGX[0].n, sgx1=SGX[1].n, sgw=SGW.n,
                           dhs0=DHS[0].n, dhs1=DHS[1].n,
                           drm0=DRM[0].n, drm1=DRM[1].n, drl0=DRL[0].n,
                           drl1=DRL[1].n, drr0=DRR[0].n, drr1=DRR[1].n)

        if not phase2:
            E.wait("gpsimd", DH, DH.n)
            E.emit("gpsimd", lambda g: g.memset(sel_sb[:], 0.0))
        else:
            p2 = ExitStack()
            with p2:
                SB2 = lambda n, sh, dt: p2.enter_context(nc.sbuf_tensor(n, sh, dt))
                xb2_sb = SB2("xb2", [128, 16 * 256], BF16)
                x2b2_sb = SB2("x2b2", [128, 256], BF16)
                xn2_sb = SB2("xn2", [128, 16 * 512], BF16)
                mr2_sb = SB2("mr2", [1, 2 * 256], F32)
                scr2_sb = SB2("scr2", [4, 2 * 256], F32)
                mrep2_sb = SB2("mrep2", [128, 256], F32)
                rrep2_sb = SB2("rrep2", [128, 256], F32)
                csrep_sb = SB2("csrep", [128, 512], F32)
                cirep_sb = SB2("cirep", [128, 512], F32)
                wg_sb = [SB2(f"wg{i}", [128, 16 * 128], BF16) for i in range(2)]
                wu_sb = [SB2(f"wu{i}", [128, 16 * 128], BF16) for i in range(2)]
                wd_sb = [SB2("wd0", [128, 64 * 128], BF16)]
                wo_sb = [SB2(f"wo{i}", [128, 4 * 128], BF16) for i in range(2)]
                hfm_sb = [SB2(f"hfm{i}", [128, 1024], BF16) for i in range(2)]
                ii_sb = SB2("ii", [128, 64 * 512], BF16)
                gfin_sb = SB2("gfin", [128, 16 * 512], BF16)
                sgbuf_sb = SB2("sgbuf", [128, 512], F32)
                asm_sb = SB2("asm", [128, 4 * 2048], F32)
                xtm_sb = SB2("xtm", [128, 2048], F32)
                big1_sb = SB2("big1", [128, 512], F32)
                big2_sb = SB2("big2", [128, 512], F32)
                sil_sb = SB2("sil", [128, 512], F32)

                for eng in ("sync", "scalar", "vector", "tensor", "gpsimd"):
                    for snm, sobj in [("t", ST), ("v", SV), ("s", SS), ("dr", DR),
                                      ("dh", DH), ("dx", DX), ("g", SG),
                                      ("sgx0", SGX[0]), ("sgx1", SGX[1]), ("sgw", SGW),
                                      ("dhs0", DHS[0]), ("dhs1", DHS[1]),
                                      ("drm0", DRM[0]), ("drm1", DRM[1]),
                                      ("drl0", DRL[0]), ("drl1", DRL[1]),
                                      ("drr0", DRR[0]), ("drr1", DRR[1])]:
                        E.wait(eng, sobj, p1_join[snm])

                E.emit("gpsimd", lambda g: g.dma_start(
                    out=xb2_sb[:].rearrange("p (a t) -> p a t", t=256),
                    in_=x2_fm_d.rearrange("(a p) t -> p a t", p=128)[:, :, 0:256]),
                    inc=(SG, 16))
                xb2_ld_tok = SG.n
                xb2_use_tok = 0

                for wk in range(8):
                    E.emit("gpsimd", lambda g, wk=wk: g.dma_start(
                        out=ap_(wo16_d, wk * 512 * HID, [[HID, 512], [1, HID]]),
                        in_=ap_(WoutT_d, wk * 512 * HID, [[HID, 512], [1, HID]])),
                        inc=(SG, 16))
                wo16_tok = SG.n

                E.wait("gpsimd", CC, CC.n)
                E.wait("sync", CC, CC.n)
                IIW = 64 * 512
                cbv = lambda pb, off, dims: ap_(ii_sb[:], pb * 4096 + off, dims)
                accv = lambda pb: ap_(ii_sb[:], 8192 + pb * 512, [[IIW, 128], [1, 512]])
                ac2v = lambda pb: ap_(ii_sb[:], 9216 + pb * 512, [[IIW, 128], [1, 512]])
                cb_tok = [0, 0]
                acc_tok = [0, 0]
                for i in range(NCORE * 8):
                    rr, tb = i // 8, i % 8
                    pb = i % 2
                    if cb_tok[pb]:
                        E.wait("gpsimd", SV, cb_tok[pb])
                    E.emit("gpsimd", lambda g, rr=rr, tb=tb, pb=pb: g.dma_start(
                        out=cbv(pb, 0, [[IIW, 128], [512, 8], [1, 512]]),
                        in_=ap_(h_all, rr * T_ALL * 512 + tb * 128 * 512,
                                [[512, 128], [TOK_PC * 512, 8], [1, 512]])),
                        inc=(SG, 16))
                    E.wait("vector", SG, SG.n)
                    if acc_tok[pb]:
                        E.wait("vector", DH, acc_tok[pb])
                    for cand in range(NCORE):
                        dst = accv(pb) if cand % 2 == 0 else ac2v(pb)
                        srcv = cbv(pb, cand * 512, [[IIW, 128], [1, 512]])
                        if cand < 2:
                            E.emit("vector", lambda v, cand=cand, dst=dst, srcv=srcv: v.tensor_scalar(
                                out=dst, in0=srcv,
                                scalar1=sel_sb[:, cand:cand + 1], scalar2=None,
                                op0=ALU.mult))
                        else:
                            E.emit("vector", lambda v, cand=cand, dst=dst, srcv=srcv: v.scalar_tensor_tensor(
                                out=dst, in0=srcv,
                                scalar=sel_sb[:, cand:cand + 1], in1=dst,
                                op0=ALU.mult, op1=ALU.add))
                    E.edge("vector", SV)
                    E.emit("vector", lambda v, pb=pb: v.tensor_add(
                        out=accv(pb), in0=accv(pb), in1=ac2v(pb)), inc=(SV, 1))
                    cb_tok[pb] = SV.n
                    E.wait("gpsimd", SV, SV.n)
                    E.emit("gpsimd", lambda g, rr=rr, tb=tb, pb=pb: g.dma_start(
                        out=ap_(h_mine, tb * 128 * 4096 + rr * 512, [[4096, 128], [1, 512]]),
                        in_=accv(pb)), inc=(DH, 16))
                    acc_tok[pb] = DH.n
                hsel_tok = DH.n
                ht_tok = {}
                for vt in range(32):
                    if vt >= 2:
                        E.wait("sync", DH, ht_tok[vt - 2])
                    E.wait("sync", DH, hsel_tok)
                    E.emit("sync", lambda s, vt=vt: s.dma_start(
                        out=hfm_sb[vt % 2][:],
                        in_=h_mine[:, vt * 128:(vt + 1) * 128], transpose=True),
                        inc=(DR, 16))
                    E.wait("vector", DR, DR.n)
                    E.emit("vector", lambda v, vt=vt: v.tensor_scalar(
                        out=hfm_sb[vt % 2][:], in0=hfm_sb[vt % 2][:],
                        scalar1=gamma_sb[:, :], scalar2=None, op0=ALU.mult),
                        inc=(SV, 1))
                    E.wait("gpsimd", SV, SV.n)
                    E.emit("gpsimd", lambda g, vt=vt: g.dma_start(
                        out=ap_(h_fm_d, vt * 128 * TOK_PC, [[TOK_PC, 128], [1, TOK_PC]]),
                        in_=hfm_sb[vt % 2][:]), inc=(DH, 16))
                    ht_tok[vt] = DH.n
                hfm_done = DH.n
                if "h_mine" in dbg:
                    E.wait("gpsimd", DH, hsel_tok)
                    E.emit("gpsimd", lambda g: g.dma_start(
                        out=dbg["h_mine"], in_=h_mine[:, :]), inc=(DH, 16))
                    hsel_tok = DH.n


                vt_tok = {}
                gu_hist = {}
                dn_hist = {}
                for qq in range(2):
                    tq = qq * 512
                    for tt2 in range(2):
                        tt = qq * 2 + tt2
                        if not (qq == 0 and tt2 == 0):
                            E.wait("gpsimd", SV, xb2_use_tok)
                            E.emit("gpsimd", lambda g, tt=tt: g.dma_start(
                                out=xb2_sb[:].rearrange("p (a t) -> p a t", t=256),
                                in_=x2_fm_d.rearrange("(a p) t -> p a t", p=128)[:, :, tt * 256:(tt + 1) * 256]),
                                inc=(SG, 16))
                            xb2_ld_tok = SG.n
                        E.wait("vector", SG, xb2_ld_tok)
                        E.wait("tensor", SG, xb2_ld_tok)
                        E.wait("tensor", SV, SV.n)
                        for ft in range(16):
                            if ft > 0:
                                E.wait("vector", ST, ST.n)
                            E.emit("vector", lambda v, ft=ft: v.tensor_mul(
                                out=x2b2_sb[:], in0=xb2_sb[:, ft * 256:(ft + 1) * 256],
                                in1=xb2_sb[:, ft * 256:(ft + 1) * 256]), inc=(SV, 1))
                            E.wait("tensor", SV, SV.n)
                            E.emit("tensor", lambda t, ft=ft: t.matmul(
                                PR[0:1, 0:256], ones_sb[:], xb2_sb[:, ft * 256:(ft + 1) * 256],
                                start=(ft == 0), stop=(ft == 15)))
                            E.emit("tensor", lambda t, ft=ft: t.matmul(
                                PO[0:1, 0:256], ones_sb[:], x2b2_sb[:],
                                start=(ft == 0), stop=(ft == 15)), inc=(ST, 1))
                        E.wait("vector", ST, ST.n)
                        E.emit("vector", lambda v: v.tensor_scalar(
                            out=mr2_sb[:, 0:256], in0=PR[0:1, 0:256], scalar1=1.0 / HID,
                            scalar2=None, op0=ALU.mult))
                        E.emit("vector", lambda v: v.tensor_scalar(
                            out=scr2_sb[0:1, 0:256], in0=PO[0:1, 0:256], scalar1=1.0 / HID,
                            scalar2=None, op0=ALU.mult), inc=(SV, 1))
                        E.edge("vector", SV)
                        E.emit("vector", lambda v: v.tensor_mul(
                            out=scr2_sb[0:1, 256:512], in0=mr2_sb[:, 0:256], in1=mr2_sb[:, 0:256]))
                        E.edge("vector", SV)
                        E.emit("vector", lambda v: v.scalar_tensor_tensor(
                            out=mr2_sb[:, 256:512], in0=scr2_sb[0:1, 0:256], scalar=1e-5,
                            in1=scr2_sb[0:1, 256:512], op0=ALU.add, op1=ALU.subtract), inc=(SV, 1))
                        E.wait("scalar", SV, SV.n)
                        E.emit("scalar", lambda s: s.activation(
                            out=mr2_sb[:, 256:512], in_=mr2_sb[:, 256:512], func=AF.Sqrt),
                            inc=(SS, 1))
                        E.wait("vector", SS, SS.n)
                        E.emit("vector", lambda v: v.reciprocal(
                            out=mr2_sb[:, 256:512], in_=mr2_sb[:, 256:512]), inc=(SV, 1))
                        E.wait("sync", SV, SV.n)
                        E.emit("sync", lambda s, tt=tt: s.dma_start(
                            out=ap_(rows2_h, tt * 256, [[TOK_PC, 2], [1, 256]]),
                            in_=ap_(mr2_sb[:], 0, [[512, 1], [256, 2], [1, 256]])), inc=(DR, 16))
                        E.wait("sync", DR, DR.n)
                        E.emit("sync", lambda s, tt=tt: s.dma_start(
                            out=mrep2_sb[:], in_=ap_(rows2_h, tt * 256, [[0, 128], [1, 256]])),
                            inc=(DR, 16))
                        E.emit("sync", lambda s, tt=tt: s.dma_start(
                            out=rrep2_sb[:], in_=ap_(rows2_h, TOK_PC + tt * 256, [[0, 128], [1, 256]])),
                            inc=(DR, 16))
                        E.wait("vector", DR, DR.n)
                        xbv = ap_(xb2_sb[:], 0, [[16 * 256, 128], [256, 16], [1, 256]])
                        xnv = ap_(xn2_sb[:], tt2 * 256, [[16 * 512, 128], [512, 16], [1, 256]])
                        mrv = ap_(mrep2_sb[:], 0, [[256, 128], [0, 16], [1, 256]])
                        rrv = ap_(rrep2_sb[:], 0, [[256, 128], [0, 16], [1, 256]])
                        E.emit("vector", lambda v, xbv=xbv, xnv=xnv, mrv=mrv: v.tensor_tensor(
                            out=xnv, in0=xbv, in1=mrv, op=ALU.subtract))
                        E.emit("vector", lambda v, xnv=xnv, rrv=rrv: v.tensor_tensor(
                            out=xnv, in0=xnv, in1=rrv, op=ALU.mult), inc=(SV, 1))
                        xb2_use_tok = SV.n
                    xn2_tok = SV.n
                    if "xn2" in dbg:
                        E.wait("gpsimd", SV, xn2_tok)
                        E.emit("gpsimd", lambda g, qq=qq: g.dma_start(
                            out=ap_(dbg["xn2"], qq * 128 * 8192, [[8192, 128], [1, 8192]]),
                            in_=xn2_sb[:]), inc=(DH, 16))
                        E.wait("vector", DH, DH.n)
                    E.wait("tensor", SV, xn2_tok)
                    for ft in range(16):
                        E.emit("tensor", lambda t, ft=ft: t.matmul(
                            PR[0:1, 0:512], ones_sb[:], xn2_sb[:, ft * 512:(ft + 1) * 512],
                            start=(ft == 0), stop=(ft == 15)),
                            inc=(ST, 1) if ft == 15 else None)
                    E.wait("vector", ST, ST.n)
                    E.emit("vector", lambda v: v.tensor_copy(
                        out=scr2_sb[0:1, 0:512], in_=PR[0:1, 0:512]), inc=(SV, 1))
                    E.wait("sync", SV, SV.n)
                    E.emit("sync", lambda s: s.dma_start(
                        out=ap_(rows2_h, 2 * TOK_PC, [[TOK_PC, 1], [1, 512]]),
                        in_=scr2_sb[0:1, 0:512]), inc=(DR, 16))
                    E.wait("sync", DR, DR.n)
                    E.emit("sync", lambda s: s.dma_start(
                        out=csrep_sb[:], in_=ap_(rows2_h, 2 * TOK_PC, [[0, 128], [1, 512]])),
                        inc=(DR, 16))
                    cs_tok = DR.n
                    for ot in range(64):
                        got = qq * 64 + ot
                        if got >= 2:
                            E.wait("sync", ST, gu_hist[got - 2])
                        E.emit("sync", lambda s, ot=ot: s.dma_start(
                            out=wg_sb[ot % 2][:].rearrange("p (a o) -> p a o", o=128),
                            in_=ap_(gate_wT_d, ot * 128,
                                    [[8192, 128], [128 * 8192, 16], [1, 128]])), inc=(DX, 16))
                        if got >= 2:
                            E.wait("scalar", ST, gu_hist[got - 2])
                        E.emit("scalar", lambda s, ot=ot: s.dma_start(
                            out=wu_sb[ot % 2][:].rearrange("p (a o) -> p a o", o=128),
                            in_=ap_(up_wT_d, ot * 128,
                                    [[8192, 128], [128 * 8192, 16], [1, 128]])), inc=(DR, 16))
                        E.wait("tensor", DX, DX.n)
                        E.wait("tensor", DR, DR.n)
                        E.wait("tensor", SV, SV.n)
                        for ft in range(16):
                            E.emit("tensor", lambda t, ot=ot, ft=ft: t.ldweights(
                                wg_sb[ot % 2][:, ft * 128:(ft + 1) * 128]))
                            E.emit("tensor", lambda t, ot=ot, ft=ft: t.matmul(
                                PQ[0][0:128, 0:512], wg_sb[ot % 2][:, ft * 128:(ft + 1) * 128],
                                xn2_sb[:, ft * 512:(ft + 1) * 512],
                                start=(ft == 0), stop=(ft == 15)))
                            E.emit("tensor", lambda t, ot=ot, ft=ft: t.ldweights(
                                wu_sb[ot % 2][:, ft * 128:(ft + 1) * 128]))
                            E.emit("tensor", lambda t, ot=ot, ft=ft: t.matmul(
                                PQ[1][0:128, 0:512], wu_sb[ot % 2][:, ft * 128:(ft + 1) * 128],
                                xn2_sb[:, ft * 512:(ft + 1) * 512],
                                start=(ft == 0), stop=(ft == 15)),
                                inc=(ST, 1) if ft == 15 else None)
                        gu_tok = ST.n
                        gu_hist[qq * 64 + ot] = ST.n
                        E.wait("vector", ST, gu_tok)
                        E.wait("vector", DR, cs_tok)
                        E.emit("vector", lambda v: v.scalar_tensor_tensor(
                            out=big1_sb[:], in0=csrep_sb[:], scalar=-128.0,
                            in1=PQ[0][0:128, 0:512], op0=ALU.mult, op1=ALU.add), inc=(SV, 1))
                        E.wait("scalar", SV, SV.n)
                        E.emit("scalar", lambda s, ot=ot: s.activation(
                            out=sil_sb[:], in_=big1_sb[:], func=AF.Silu,
                            scale=gs_sb[:, ot:ot + 1]), inc=(SS, 1))
                        E.emit("vector", lambda v: v.scalar_tensor_tensor(
                            out=big2_sb[:], in0=csrep_sb[:], scalar=-128.0,
                            in1=PQ[1][0:128, 0:512], op0=ALU.mult, op1=ALU.add), inc=(SV, 1))
                        E.wait("vector", SS, SS.n)
                        E.emit("vector", lambda v, ot=ot: v.scalar_tensor_tensor(
                            out=ii_sb[:, ot * 512:(ot + 1) * 512], in0=big2_sb[:],
                            scalar=us_sb[:, ot:ot + 1], in1=sil_sb[:],
                            op0=ALU.mult, op1=ALU.mult), inc=(SV, 1))
                        ii_tok = SV.n
                        E.wait("tensor", SV, ii_tok)
                        E.emit("tensor", lambda t, ot=ot: t.matmul(
                            PR[0:1, 0:512], ones_sb[:], ii_sb[:, ot * 512:(ot + 1) * 512],
                            start=(ot == 0), stop=(ot == 63)),
                            inc=(ST, 1) if ot == 63 else None)
                    if "ii" in dbg:
                        E.wait("gpsimd", SV, ii_tok)
                        E.emit("gpsimd", lambda g, qq=qq: g.dma_start(
                            out=ap_(dbg["ii"], qq * 128 * 32768, [[32768, 128], [1, 32768]]),
                            in_=ii_sb[:]), inc=(DH, 16))
                        E.wait("vector", DH, DH.n)
                    E.wait("vector", ST, ST.n)
                    E.emit("vector", lambda v: v.tensor_copy(
                        out=scr2_sb[0:1, 0:512], in_=PR[0:1, 0:512]), inc=(SV, 1))
                    E.wait("sync", SV, SV.n)
                    E.emit("sync", lambda s: s.dma_start(
                        out=ap_(rows2_h, 3 * TOK_PC, [[TOK_PC, 1], [1, 512]]),
                        in_=scr2_sb[0:1, 0:512]), inc=(DR, 16))
                    E.wait("sync", DR, DR.n)
                    E.emit("sync", lambda s: s.dma_start(
                        out=cirep_sb[:], in_=ap_(rows2_h, 3 * TOK_PC, [[0, 128], [1, 512]])),
                        inc=(DR, 16))
                    ci_tok = DR.n

                    for ht in range(16):
                        ght = qq * 16 + ht
                        if ght >= 1:
                            E.wait("sync", ST, dn_hist[(ght - 1, 0)])
                        E.emit("sync", lambda s, ht=ht: s.dma_start(
                            out=ap_(wd_sb[0][:], 0, [[64 * 128, 128], [128, 32], [1, 128]]),
                            in_=ap_(down_wT_d, ht * 128,
                                    [[2048, 128], [128 * 2048, 32], [1, 128]])), inc=(DX, 16))
                        if ght >= 1:
                            E.wait("scalar", ST, dn_hist[(ght - 1, 1)])
                        E.emit("scalar", lambda s, ht=ht: s.dma_start(
                            out=ap_(wd_sb[0][:], 32 * 128, [[64 * 128, 128], [128, 32], [1, 128]]),
                            in_=ap_(down_wT_d, ht * 128 + 32 * 128 * 2048,
                                    [[2048, 128], [128 * 2048, 32], [1, 128]])), inc=(DR, 16))
                        E.wait("tensor", DX, DX.n)
                        E.wait("tensor", SV, SV.n)
                        E.wait("tensor", SS, SS.n)
                        for it in range(32):
                            E.emit("tensor", lambda t, ht=ht, it=it: t.ldweights(
                                wd_sb[0][:, it * 128:(it + 1) * 128]))
                            E.emit("tensor", lambda t, ht=ht, it=it: t.matmul(
                                PD[ht % 2][0:128, 0:512],
                                wd_sb[0][:, it * 128:(it + 1) * 128],
                                ii_sb[:, it * 512:(it + 1) * 512],
                                start=(it == 0), stop=False),
                                inc=(ST, 1) if it == 31 else None)
                        dn_hist[(ght, 0)] = ST.n
                        E.wait("tensor", DR, DR.n)
                        for it in range(32, 64):
                            E.emit("tensor", lambda t, ht=ht, it=it: t.ldweights(
                                wd_sb[0][:, it * 128:(it + 1) * 128]))
                            E.emit("tensor", lambda t, ht=ht, it=it: t.matmul(
                                PD[ht % 2][0:128, 0:512],
                                wd_sb[0][:, it * 128:(it + 1) * 128],
                                ii_sb[:, it * 512:(it + 1) * 512],
                                start=False, stop=(it == 63)),
                                inc=(ST, 1) if it == 63 else None)
                        dn_hist[(ght, 1)] = ST.n
                        dmm_tok = ST.n
                        E.wait("vector", ST, dmm_tok)
                        E.wait("vector", DR, ci_tok)
                        E.emit("vector", lambda v, ht=ht: v.scalar_tensor_tensor(
                            out=big1_sb[:], in0=cirep_sb[:], scalar=-128.0,
                            in1=PD[ht % 2][0:128, 0:512], op0=ALU.mult, op1=ALU.add),
                            inc=(SV, 1))
                        E.wait("scalar", SV, SV.n)
                        E.emit("scalar", lambda s, ht=ht: s.activation(
                            out=gfin_sb[:, ht * 512:(ht + 1) * 512], in_=big1_sb[:],
                            func=AF.Silu, scale=ds_sb[:, ht:ht + 1]), inc=(SS, 1))
                    gfin_tok = SS.n
                    if "gfin" in dbg:
                        E.wait("gpsimd", SS, gfin_tok)
                        E.emit("gpsimd", lambda g, qq=qq: g.dma_start(
                            out=ap_(dbg["gfin"], qq * 128 * 8192, [[8192, 128], [1, 8192]]),
                            in_=gfin_sb[:]), inc=(DH, 16))
                        E.wait("scalar", DH, DH.n)

                    for og in range(4):
                        for vt in range(32):
                            gvt = (qq * 4 + og) * 32 + vt
                            if gvt >= 2:
                                E.wait("sync", ST, vt_tok[gvt - 2])
                            if gvt == 0:
                                E.wait("sync", SG, wo16_tok)
                                E.wait("sync", DH, hfm_done)
                            E.emit("sync", lambda s, og=og, vt=vt: s.dma_start(
                                out=wo_sb[vt % 2][:],
                                in_=ap_(wo16_d, vt * 128 * HID + og * 512,
                                        [[HID, 128], [1, 512]])), inc=(DX, 16))
                            if gvt >= 2:
                                E.wait("scalar", ST, vt_tok[gvt - 2])
                            E.emit("scalar", lambda s, vt=vt, tq=tq: s.dma_start(
                                out=ap_(hfm_sb[vt % 2][:], 0, [[1024, 128], [1, 512]]),
                                in_=ap_(h_fm_d, vt * 128 * TOK_PC + tq,
                                        [[TOK_PC, 128], [1, 512]])), inc=(DR, 16))
                            E.wait("tensor", DX, DX.n)
                            E.wait("tensor", DR, DR.n)
                            for o4 in range(4):
                                pbk = [PQ[0], PQ[1], PD[0], PD[1]][o4]
                                E.emit("tensor", lambda t, vt=vt, o4=o4: t.ldweights(
                                    wo_sb[vt % 2][:, o4 * 128:(o4 + 1) * 128]))
                                E.emit("tensor", lambda t, vt=vt, o4=o4, pbk=pbk: t.matmul(
                                    pbk[0:128, 0:512],
                                    wo_sb[vt % 2][:, o4 * 128:(o4 + 1) * 128],
                                    ap_(hfm_sb[vt % 2][:], 0, [[1024, 128], [1, 512]]),
                                    start=(vt == 0), stop=(vt == 31)),
                                    inc=(ST, 1) if o4 == 3 else None)
                            vt_tok[gvt] = ST.n
                        wout_tok = ST.n
                        E.wait("vector", ST, wout_tok)
                        E.wait("vector", SS, gfin_tok)
                        E.wait("vector", DH, DH.n)
                        for o4 in range(4):
                            ot2 = og * 4 + o4
                            pbk = [PQ[0], PQ[1], PD[0], PD[1]][o4]
                            E.emit("vector", lambda v, pbk=pbk, ot2=ot2: v.tensor_tensor(
                                out=sgbuf_sb[:], in0=pbk[0:128, 0:512],
                                in1=gfin_sb[:, ot2 * 512:(ot2 + 1) * 512], op=ALU.mult),
                                inc=(SV, 1))
                            E.wait("tensor", SV, SV.n)
                            for tb in range(4):
                                E.emit("tensor", lambda t, tb=tb: t.transpose(
                                    PR[0:128, tb * 128:(tb + 1) * 128],
                                    sgbuf_sb[:, tb * 128:(tb + 1) * 128], id128f_sb[:]),
                                    inc=(ST, 1) if tb == 3 else None)
                            E.wait("vector", ST, ST.n)
                            E.emit("vector", lambda v, ot2=ot2: v.tensor_copy(
                                out=ap_(asm_sb[:], ot2 * 128,
                                        [[4 * 2048, 128], [2048, 4], [1, 128]]),
                                in_=PR[0:128, 0:512]), inc=(SV, 1))
                    asm_tok = SV.n

                    for tb in range(4):
                        E.wait("sync", SV, SV.n)
                        E.emit("sync", lambda s, qq=qq, tb=tb, tq=tq: s.dma_start(
                            out=xtm_sb[:],
                            in_=x_tm_d[tq + tb * 128: tq + (tb + 1) * 128, :]), inc=(DX, 16))
                        E.wait("vector", DX, DX.n)
                        E.wait("vector", SV, asm_tok)
                        E.emit("vector", lambda v, tb=tb: v.tensor_add(
                            out=asm_sb[:, tb * 2048:(tb + 1) * 2048], in0=xtm_sb[:],
                            in1=asm_sb[:, tb * 2048:(tb + 1) * 2048]), inc=(SV, 1))
                        E.wait("gpsimd", SV, SV.n)
                        E.emit("gpsimd", lambda g, qq=qq, tb=tb, tq=tq: g.dma_start(
                            out=out_d[tq + tb * 128: tq + (tb + 1) * 128, :],
                            in_=asm_sb[:, tb * 2048:(tb + 1) * 2048]), inc=(DH, 16))
                        E.wait("vector", DH, DH.n)
                    if "asm" in dbg:
                        E.wait("gpsimd", SV, SV.n)
                        E.emit("gpsimd", lambda g, qq=qq: g.dma_start(
                            out=ap_(dbg["asm"], qq * 128 * 8192, [[8192, 128], [1, 8192]]),
                            in_=asm_sb[:]), inc=(DH, 16))
                        E.wait("vector", DH, DH.n)

                E.wait("gpsimd", DH, DH.n)

        with nc.Block() as block:
            @block.sync
            def _(sync):
                E.run("sync", sync)

            @block.scalar
            def _(scalar):
                E.run("scalar", scalar)

            @block.vector
            def _(vector):
                E.run("vector", vector)

            @block.tensor
            def _(tensor):
                E.run("tensor", tensor)

            @block.gpsimd
            def _(gpsimd):
                E.run("gpsimd", gpsimd)

    return nc, dbg


_BUILD_CACHE = {}


def kernel(**inputs):
    percore, x = _host_prep(inputs)
    key = "full"
    if key not in _BUILD_CACHE:
        _BUILD_CACHE[key] = build_program()
    nc, _ = _BUILD_CACHE[key]
    names = ["x_fm", "x2_fm", "x_tm", "WqkvT", "WzT", "WbaT", "convw", "dtb",
             "alog", "gamma", "WoutT", "gate_wT", "up_wT", "down_wT", "gate_s",
             "up_s", "down_s", "t_lo", "t_hi", "t_hii", "eye2", "ident128",
             "ident128f", "onescol", "sel"]
    in_maps = [{n: percore[c][n] for n in names} for c in range(NCORE)]
    res = run_bass_kernel_spmd(nc, in_maps, core_ids=list(range(NCORE)))
    out = np.concatenate([np.asarray(res.results[c]["out"]) for c in range(NCORE)], axis=0)
    return out.reshape(2, 4096, 2048)
